# version 05
# baseline (speedup 1.0000x reference)
import numpy as np
import concourse.bass as bass
import concourse.mybir as mybir
from concourse.bass_utils import run_bass_kernel_spmd
from contextlib import ExitStack

F32 = mybir.dt.float32
BF16 = mybir.dt.bfloat16
F32R = mybir.dt.float32r
AF = mybir.ActivationFunctionType
ALU = mybir.AluOpType

D = 4096
TTOK = 16384
NCORE = 8
TSL = TTOK // NCORE
CSL = D // NCORE
SEQ = 8192
EPS = 1e-5

ENGS = ("q", "v", "s", "p", "g")


class _LastInstRecorder:

    def __init__(self, engine):
        self._e = engine
        self.last = None

    def __getattr__(self, name):
        f = getattr(self._e, name)
        if not callable(f):
            return f

        def wrap(*a, **k):
            r = f(*a, **k)
            if hasattr(r, "then_inc"):
                self.last = r
            return r
        return wrap


class Prog:

    def __init__(self):
        self.ctr = {e: 0 for e in ENGS}
        self.stream = {e: [] for e in ENGS}

    def stage(self, eng, emit, deps=(), ndma=0, raw=False):
        deps = [d for d in deps if d is not None]
        if ndma and self.ctr[eng]:
            deps.append((eng, self.ctr[eng]))
        self.stream[eng].append((list(deps), emit, "raw" if raw else ndma))
        if raw:
            return None
        if ndma:
            self.ctr[eng] += 16 * ndma
        else:
            self.ctr[eng] += 1
        return (eng, self.ctr[eng])

    def replay(self, eng_key, engine, sems):
        rec = _LastInstRecorder(engine)
        for deps, emit, ndma in self.stream[eng_key]:
            for (sk, val) in deps:
                engine.wait_ge(sems[sk], val)
            if ndma == "raw":
                emit(engine)
            elif ndma:
                insts = emit(engine)
                assert len(insts) == ndma, (eng_key, len(insts), ndma)
                for inst in insts:
                    inst.then_inc(sems[eng_key], 16)
            else:
                rec.last = None
                emit(rec)
                if rec.last is not None:
                    rec.last.then_inc(sems[eng_key], 1)
                else:
                    engine.drain().then_inc(sems[eng_key], 1)


def _run(nc, prog, ctx, final_deps, extra_sems=None):
    sems = {e: ctx.enter_context(nc.semaphore("sem_" + e)) for e in ENGS}
    if extra_sems:
        sems.update(extra_sems)
    block = ctx.enter_context(nc.Block())

    @block.sync
    def _(sync):
        prog.replay("q", sync, sems)

    @block.vector
    def _(vector):
        prog.replay("v", vector, sems)

    @block.scalar
    def _(scalar):
        prog.replay("s", scalar, sems)

    @block.tensor
    def _(tensor):
        prog.replay("p", tensor, sems)

    @block.gpsimd
    def _(gpsimd):
        prog.replay("g", gpsimd, sems)
        for (sk, val) in final_deps:
            gpsimd.wait_ge(sems[sk], val)



def _r3(dram_ap, inner):
    return dram_ap.rearrange("(k p) c -> p k c", p=128)


def _s3(sb_ap, inner):
    return sb_ap.rearrange("p (k c) -> p k c", c=inner)


def ln_stats(prog, sq_tile, x_tile, ones_col, st, ps_a, ps_b, nk, TC, deps, h_ones):
    h_sq = prog.stage("s", lambda s: s.activation(out=sq_tile[:], in_=x_tile[:], func=AF.Square),
                      deps=deps)

    def sums(p):
        for k in range(nk):
            c = slice(k * TC, (k + 1) * TC)
            p.matmul(ps_a[0:1, :], ones_col[:], x_tile[:, c], start=(k == 0), stop=(k == nk - 1))
        for k in range(nk):
            c = slice(k * TC, (k + 1) * TC)
            p.matmul(ps_b[0:1, :], ones_col[:], sq_tile[:, c], start=(k == 0), stop=(k == nk - 1))
    h_sum = prog.stage("p", sums, deps=list(deps) + [h_sq, h_ones])

    def stats1(v):
        v.tensor_scalar_mul(st[:, 0 * TC:1 * TC], ps_a[0:1, :], 1.0 / D)
        v.tensor_scalar(st[:, 1 * TC:2 * TC], ps_b[0:1, :], 1.0 / D, EPS,
                        op0=ALU.mult, op1=ALU.add)
        v.drain()
        v.tensor_tensor(out=st[:, 2 * TC:3 * TC], in0=st[:, 0 * TC:1 * TC],
                        in1=st[:, 0 * TC:1 * TC], op=ALU.mult)
        v.drain()
        v.tensor_tensor(out=st[:, 2 * TC:3 * TC], in0=st[:, 1 * TC:2 * TC],
                        in1=st[:, 2 * TC:3 * TC], op=ALU.subtract)
        v.drain()
        v.reciprocal(out=st[:, 3 * TC:4 * TC], in_=st[:, 2 * TC:3 * TC])
    h_st1 = prog.stage("v", stats1, deps=[h_sum])

    h_rt = prog.stage("s", lambda s: s.activation(out=st[:, 4 * TC:5 * TC],
                                                  in_=st[:, 3 * TC:4 * TC], func=AF.Sqrt),
                      deps=[h_st1])
    return h_rt


def bcast_mr(prog, ones_row, st, ps_m, ps_r, TC, deps, h_ones):
    def bc(p):
        p.matmul(ps_m[:, 0:TC], ones_row[:], st[0:1, 0 * TC:1 * TC], start=True, stop=True)
        p.matmul(ps_r[:, 0:TC], ones_row[:], st[0:1, 4 * TC:5 * TC], start=True, stop=True)
    return prog.stage("p", bc, deps=list(deps) + [h_ones])



def build_l1():
    nc = bass.Bass("TRN2", target_bir_lowering=False)
    xs = nc.dram_tensor("xs", [D, TSL], F32, kind="ExternalInput").ap()
    xpwT = nc.dram_tensor("xpwT", [D, 160], F32, kind="ExternalInput").ap()
    dtwT = nc.dram_tensor("dtwT", [128, D], F32, kind="ExternalInput").ap()
    dtb = nc.dram_tensor("dtb", [128, 32], F32, kind="ExternalInput").ap()
    dts = nc.dram_tensor("dts", [D, TSL], F32, kind="ExternalOutput").ap()
    xlns = nc.dram_tensor("xlns", [D, TSL], F32, kind="ExternalOutput").ap()
    bcs = nc.dram_tensor("bcs", [32, TSL], F32, kind="ExternalOutput").ap()

    TC = 256
    NT = TSL // TC
    prog = Prog()
    with ExitStack() as ctx:
        sb = lambda n_, sh, dt=F32: ctx.enter_context(nc.sbuf_tensor(n_, sh, dt))
        x_sb = sb("x_sb", [128, 32 * TC])
        sq_sb = sb("sq_sb", [128, 32 * TC])
        xln_sb = sb("xln_sb", [128, 32 * TC])
        dt_sb = sb("dt_sb", [128, 32 * TC])
        dtin_sb = sb("dtin_sb", [128, TC])
        bc_sb = sb("bc_sb", [32, TC])
        xpw_sb = sb("xpw_sb", [128, 32 * 160])
        dtw_sb = sb("dtw_sb", [128, 32 * 128])
        dtb_sb = sb("dtb_sb", [128, 32])
        ones_col = sb("ones_col", [128, 1])
        ones_row = sb("ones_row", [1, 128])
        st = sb("st", [1, 8 * TC])
        pb = [ctx.enter_context(nc.psum_tensor(f"pb{i}", [128, TC], F32)) for i in range(8)]
        ps_sx, ps_sq, ps_mb, ps_rb, ps_dt, ps_bc, ps_d0, ps_d1 = pb

        h_ones = prog.stage("v", lambda v: (v.memset(ones_col[:], 1.0), v.memset(ones_row[:], 1.0)))
        h_w = prog.stage("q", lambda q: [
            q.dma_start(out=_s3(xpw_sb[:], 160), in_=_r3(xpwT, 160)),
            q.dma_start(out=dtw_sb[:], in_=dtwT),
            q.dma_start(out=dtb_sb[:], in_=dtb),
        ], ndma=3)

        h_out = None
        h_xln_prev = None
        for t in range(NT):
            tsl = slice(t * TC, (t + 1) * TC)
            h_xld = prog.stage("q", (lambda tsl_: lambda q: [
                q.dma_start(out=_s3(x_sb[:], TC), in_=_r3(xs[:, tsl_], TC))
            ])(tsl), deps=[h_xln_prev], ndma=1)

            h_rt = ln_stats(prog, sq_sb, x_sb, ones_col, st, ps_sx, ps_sq, 32, TC,
                            deps=[h_xld], h_ones=h_ones)
            h_bc = bcast_mr(prog, ones_row, st, ps_mb, ps_rb, TC, deps=[h_rt], h_ones=h_ones)

            def xln_emit(v):
                for k in range(32):
                    c = slice(k * TC, (k + 1) * TC)
                    v.tensor_tensor(out=sq_sb[:, c], in0=x_sb[:, c], in1=ps_mb[:], op=ALU.subtract)
                v.drain()
                for k in range(32):
                    c = slice(k * TC, (k + 1) * TC)
                    v.tensor_tensor(out=xln_sb[:, c], in0=sq_sb[:, c], in1=ps_rb[:], op=ALU.mult)
            h_xln = prog.stage("v", xln_emit, deps=[h_bc, h_out])
            h_xln_prev = h_xln

            def xp_emit(p):
                for k in range(32):
                    c = slice(k * TC, (k + 1) * TC)
                    p.matmul(ps_dt[:], xpw_sb[:, k * 160:k * 160 + 128], xln_sb[:, c],
                             start=(k == 0), stop=(k == 31))
                for k in range(32):
                    c = slice(k * TC, (k + 1) * TC)
                    p.matmul(ps_bc[0:32, :], xpw_sb[:, k * 160 + 128:(k + 1) * 160], xln_sb[:, c],
                             start=(k == 0), stop=(k == 31))
            h_xp = prog.stage("p", xp_emit, deps=[h_xln, h_w])

            def evac_emit(v):
                v.tensor_copy(out=dtin_sb[:], in_=ps_dt[:])
                v.tensor_copy(out=bc_sb[:], in_=ps_bc[0:32, :])
            h_ev = prog.stage("v", evac_emit, deps=[h_xp, h_out])

            h_pe = None
            h_sp = None
            for gidx in range(16):
                def pe_emit(p, gi=gidx):
                    for j in range(2):
                        dm = gi * 2 + j
                        psd = ps_d0 if j == 0 else ps_d1
                        p.matmul(psd[:], dtw_sb[:, dm * 128:(dm + 1) * 128], dtin_sb[:],
                                 start=True, stop=True)
                h_pe = prog.stage("p", pe_emit, deps=[h_ev, h_sp])

                def sp_emit(s, gi=gidx):
                    for j in range(2):
                        dm = gi * 2 + j
                        c = slice(dm * TC, (dm + 1) * TC)
                        psd = ps_d0 if j == 0 else ps_d1
                        s.activation(out=dt_sb[:, c], in_=psd[:], func=AF.Exp,
                                     bias=dtb_sb[:, dm:dm + 1])
                        s.drain()
                        s.activation(out=dt_sb[:, c], in_=dt_sb[:, c], func=AF.Ln, bias=1.0)
                h_sp = prog.stage("s", sp_emit, deps=[h_pe, h_out])

            h_out = prog.stage("g", (lambda tsl_: lambda g: [
                g.dma_start(out=_r3(dts[:, tsl_], TC), in_=_s3(dt_sb[:], TC)),
                g.dma_start(out=_r3(xlns[:, tsl_], TC), in_=_s3(xln_sb[:], TC)),
                g.dma_start(out=bcs[:, tsl_], in_=bc_sb[:]),
            ])(tsl), deps=[h_sp, h_xln, h_ev], ndma=3)

        _run(nc, prog, ctx, [h_out])
    return nc



def build_l2():
    nc = bass.Bass("TRN2", target_bir_lowering=False)
    dtc = nc.dram_tensor("dtc", [CSL, TTOK], F32, kind="ExternalInput").ap()
    xlnc = nc.dram_tensor("xlnc", [CSL, TTOK], F32, kind="ExternalInput").ap()
    bc = nc.dram_tensor("bc", [32, TTOK], F32, kind="ExternalInput").ap()
    sel16 = nc.dram_tensor("sel16", [16, 16 * 128], F32, kind="ExternalInput").ap()
    alog = nc.dram_tensor("alog", [128, (CSL // 128) * 16], F32, kind="ExternalInput").ap()
    dvec = nc.dram_tensor("dvec", [128, CSL // 128], F32, kind="ExternalInput").ap()
    yc = nc.dram_tensor("yc", [CSL, TTOK], F32, kind="ExternalOutput").ap()

    TC = 1024
    NT = TTOK // TC
    NCH = CSL // 128
    NSUB = TC // 512
    prog = Prog()
    with ExitStack() as ctx:
        sb = lambda n_, sh, dt=F32: ctx.enter_context(nc.sbuf_tensor(n_, sh, dt))
        dt_sb = [sb(f"dt{i}", [128, TC]) for i in range(NCH)]
        xln_sb = [sb(f"xln{i}", [128, TC]) for i in range(NCH)]
        dtx_sb = [sb(f"dtx{i}", [128, TC]) for i in range(NCH)]
        y_sb = [sb(f"y{i}", [128, TC]) for i in range(NCH)]
        dA_sb = [sb(f"dA{i}", [128, TC]) for i in range(NCH)]
        dBx_sb = [sb(f"dBx{i}", [128, TC]) for i in range(NCH)]
        h_sb = [sb(f"h{i}", [128, TC]) for i in range(NCH)]
        b_sb = sb("b_sb", [16, TC])
        c_sb = sb("c_sb", [16, TC])
        sel_sb = sb("sel_sb", [16, 16 * 128])
        a_sb = sb("a_sb", [128, NCH * 16])
        d_sb = sb("d_sb", [128, NCH])
        carry = sb("carry", [128, NCH * 16])
        ones_row = sb("ones_row", [1, 128])
        ps_b = [ctx.enter_context(nc.psum_tensor(f"psb{i}", [128, 512], F32)) for i in range(NSUB)]
        ps_c = [ctx.enter_context(nc.psum_tensor(f"psc{i}", [128, 512], F32)) for i in range(NSUB)]

        prog.stage("v", lambda v: (v.memset(ones_row[:], 1.0), v.memset(carry[:], 0.0)))
        h_cst = prog.stage("q", lambda q: [
            q.dma_start(out=a_sb[:], in_=alog),
            q.dma_start(out=d_sb[:], in_=dvec),
            q.dma_start(out=sel_sb[:], in_=sel16),
        ], ndma=3)
        h_a = prog.stage("s", lambda s: (s.activation(out=a_sb[:], in_=a_sb[:], func=AF.Exp),
                                         s.drain(),
                                         s.mul(out=a_sb[:], in_=a_sb[:], mul=-1.0)),
                         deps=[h_cst])

        h_y = None
        h_vq = None
        h_sn = None
        for t in range(NT):
            tsl = slice(t * TC, (t + 1) * TC)
            h_ld = prog.stage("q", (lambda tsl_: lambda q: [
                q.dma_start(out=dt_sb[i][:], in_=dtc[128 * i:128 * (i + 1), tsl_])
                for i in range(NCH)] + [
                q.dma_start(out=xln_sb[i][:], in_=xlnc[128 * i:128 * (i + 1), tsl_])
                for i in range(NCH)] + [
                q.dma_start(out=b_sb[:], in_=bc[0:16, tsl_]),
                q.dma_start(out=c_sb[:], in_=bc[16:32, tsl_]),
            ])(tsl), deps=[h_vq, h_sn, h_y], ndma=2 * NCH + 2)

            def prep_emit(v):
                for i in range(NCH):
                    v.tensor_tensor(out=dtx_sb[i][:], in0=dt_sb[i][:], in1=xln_sb[i][:], op=ALU.mult)
                for i in range(NCH):
                    v.tensor_scalar_mul(y_sb[i][:], xln_sb[i][:], d_sb[:, i:i + 1])
            h_prep = prog.stage("v", prep_emit, deps=[h_ld, h_cst, h_y])

            h_pb = None
            h_vq = h_prep
            for n in range(16):
                def bcst_emit(p, n_=n):
                    sel = sel_sb[:, n_ * 128:(n_ + 1) * 128]
                    for u in range(NSUB):
                        cs = slice(u * 512, (u + 1) * 512)
                        p.matmul(ps_b[u][:], sel, b_sb[:, cs], start=True, stop=True)
                        p.matmul(ps_c[u][:], sel, c_sb[:, cs], start=True, stop=True)
                h_pb = prog.stage("p", bcst_emit, deps=[h_ld, h_vq, h_cst])

                def dA_emit(s, n_=n):
                    for i in range(NCH):
                        s.activation(out=dA_sb[i][:], in_=dt_sb[i][:], func=AF.Exp,
                                     scale=a_sb[:, i * 16 + n_:i * 16 + n_ + 1])
                h_sn = prog.stage("s", dA_emit, deps=[h_ld, h_a, h_vq])

                def scan_emit(v, n_=n, t_=t):
                    for i in range(NCH):
                        for u in range(NSUB):
                            cs = slice(u * 512, (u + 1) * 512)
                            v.tensor_tensor(out=dBx_sb[i][:, cs], in0=dtx_sb[i][:, cs],
                                            in1=ps_b[u][:], op=ALU.mult)
                    if t_ == NT // 2:
                        for i in range(NCH):
                            v.memset(carry[:, i * 16 + n_:i * 16 + n_ + 1], 0.0)
                    v.drain()
                    for i in range(NCH):
                        col = slice(i * 16 + n_, i * 16 + n_ + 1)
                        v.tensor_tensor_scan(out=h_sb[i][:], data0=dA_sb[i][:], data1=dBx_sb[i][:],
                                             initial=carry[:, col], op0=ALU.mult, op1=ALU.add)
                    v.drain()
                    for i in range(NCH):
                        col = slice(i * 16 + n_, i * 16 + n_ + 1)
                        v.tensor_copy(out=carry[:, col], in_=h_sb[i][:, TC - 1:TC])
                    for i in range(NCH):
                        for u in range(NSUB):
                            cs = slice(u * 512, (u + 1) * 512)
                            v.tensor_tensor(out=dBx_sb[i][:, cs], in0=h_sb[i][:, cs],
                                            in1=ps_c[u][:], op=ALU.mult)
                    v.drain()
                    for i in range(NCH):
                        v.tensor_tensor(out=y_sb[i][:], in0=y_sb[i][:], in1=dBx_sb[i][:], op=ALU.add)
                h_vq = prog.stage("v", scan_emit, deps=[h_pb, h_sn])

            h_y = prog.stage("g", (lambda tsl_: lambda g: [
                g.dma_start(out=yc[128 * i:128 * (i + 1), tsl_], in_=y_sb[i][:])
                for i in range(NCH)])(tsl), deps=[h_vq], ndma=NCH)

        _run(nc, prog, ctx, [h_y])
    return nc



def build_l3():
    nc = bass.Bass("TRN2", target_bir_lowering=False)
    xs = nc.dram_tensor("xs", [D, TSL], F32, kind="ExternalInput").ap()
    ys = nc.dram_tensor("ys", [D, TSL], F32, kind="ExternalInput").ap()
    wbT = nc.dram_tensor("wbT", [D, D], F32, kind="ExternalInput").ap()
    bb = nc.dram_tensor("bb", [128, 32], F32, kind="ExternalInput").ap()
    wrT = nc.dram_tensor("wrT", [D, 8], F32, kind="ExternalInput").ap()
    aaT = nc.dram_tensor("aaT", [D, 128], F32, kind="ExternalInput").ap()
    baT = nc.dram_tensor("baT", [128, D], F32, kind="ExternalInput").ap()
    ident = nc.dram_tensor("ident", [128, 128], F32, kind="ExternalInput").ap()
    e8m = nc.dram_tensor("e8m", [8, 128], F32, kind="ExternalInput").ap()
    outs = nc.dram_tensor("outs", [D, TSL], F32, kind="ExternalOutput").ap()

    TC = 256
    NT = TSL // TC
    NTI = TC // 128
    prog = Prog()
    with ExitStack() as ctx:
        sb = lambda n_, sh, dt=F32: ctx.enter_context(nc.sbuf_tensor(n_, sh, dt))
        x_sb = sb("x_sb", [128, 32 * TC])
        sq_sb = sb("sq_sb", [128, 32 * TC])
        y_sb = sb("y_sb", [128, 32 * TC])
        xln_sb = sb("xln_sb", [128, 32 * TC])
        wb_sb = sb("wb_sb", [128, 32 * 256])
        aa_sb = sb("aa_sb", [128, 32 * 128])
        wr_sb = sb("wr_sb", [128, 32 * 8])
        ba_sb = sb("ba_sb", [128, D])
        bb_sb = sb("bb_sb", [128, 32])
        id_sb = sb("id_sb", [128, 128])
        e8_sb = sb("e8_sb", [8, 128])
        lg_sb = sb("lg_sb", [8, TC])
        lgT_sb = sb("lgT_sb", [128, 8 * NTI])
        w2_sb = sb("w2_sb", [128, 8 * NTI])
        t2s = sb("t2s", [128, 16 * NTI])
        sc_sb = sb("sc_sb", [128, 4 * NTI])
        w8_sb = sb("w8_sb", [8, TC])
        g_sb = sb("g_sb", [128, TC])
        wbc_sb = sb("wbc_sb", [128, TC])
        ones_col = sb("ones_col", [128, 1])
        ones_row = sb("ones_row", [1, 128])
        st = sb("st", [1, 8 * TC])
        pb = [ctx.enter_context(nc.psum_tensor(f"pb{i}", [128, TC], F32)) for i in range(8)]
        ps_sx, ps_sq, ps_mb, ps_rb, ps_aux, ps_h, ps_o0, ps_o1 = pb

        h_ones = prog.stage("v", lambda v: (v.memset(ones_col[:], 1.0), v.memset(ones_row[:], 1.0)))
        h_w = prog.stage("q", lambda q: [
            q.dma_start(out=_s3(aa_sb[:], 128), in_=_r3(aaT, 128)),
            q.dma_start(out=_s3(wr_sb[:], 8), in_=_r3(wrT, 8)),
            q.dma_start(out=ba_sb[:], in_=baT),
            q.dma_start(out=bb_sb[:], in_=bb),
            q.dma_start(out=id_sb[:], in_=ident),
            q.dma_start(out=e8_sb[:], in_=e8m),
        ], ndma=6)

        h_out = None
        h_ffn_done = None
        h_wb_used = None
        for t in range(NT):
            tsl = slice(t * TC, (t + 1) * TC)
            h_ld = prog.stage("q", (lambda tsl_: lambda q: [
                q.dma_start(out=_s3(x_sb[:], TC), in_=_r3(xs[:, tsl_], TC)),
                q.dma_start(out=_s3(y_sb[:], TC), in_=_r3(ys[:, tsl_], TC)),
            ])(tsl), deps=[h_ffn_done, h_out], ndma=2)

            def res_emit(v):
                for k in range(32):
                    c = slice(k * TC, (k + 1) * TC)
                    v.tensor_tensor(out=x_sb[:, c], in0=x_sb[:, c], in1=y_sb[:, c], op=ALU.add)
            h_res = prog.stage("v", res_emit, deps=[h_ld])

            h_rt = ln_stats(prog, sq_sb, x_sb, ones_col, st, ps_sx, ps_sq, 32, TC,
                            deps=[h_res, h_wb_used], h_ones=h_ones)
            h_bc = bcast_mr(prog, ones_row, st, ps_mb, ps_rb, TC, deps=[h_rt], h_ones=h_ones)

            def xln_emit(v):
                for k in range(32):
                    c = slice(k * TC, (k + 1) * TC)
                    v.tensor_tensor(out=sq_sb[:, c], in0=x_sb[:, c], in1=ps_mb[:], op=ALU.subtract)
                v.drain()
                for k in range(32):
                    c = slice(k * TC, (k + 1) * TC)
                    v.tensor_tensor(out=xln_sb[:, c], in0=sq_sb[:, c], in1=ps_rb[:], op=ALU.mult)
            h_xln = prog.stage("v", xln_emit, deps=[h_bc])

            def lg_emit(p):
                for k in range(32):
                    c = slice(k * TC, (k + 1) * TC)
                    p.matmul(ps_aux[0:8, :], wr_sb[:, k * 8:(k + 1) * 8], xln_sb[:, c],
                             start=(k == 0), stop=(k == 31))
            h_lg = prog.stage("p", lg_emit, deps=[h_xln, h_w])
            h_lge = prog.stage("v", lambda v: v.tensor_copy(out=lg_sb[:], in_=ps_aux[0:8, :]),
                               deps=[h_lg])

            def tr_emit(p):
                for i in range(NTI):
                    dst = (ps_mb, ps_rb)[i]
                    p.transpose(dst[:, 0:8], lg_sb[:, i * 128:(i + 1) * 128], id_sb[0:8, 0:8])
            h_tr = prog.stage("p", tr_emit, deps=[h_lge, h_xln])

            def tre_emit(v):
                for i in range(NTI):
                    src = (ps_mb, ps_rb)[i]
                    v.tensor_copy(out=lgT_sb[:, i * 8:(i + 1) * 8], in_=src[:, 0:8])
            h_tre = prog.stage("v", tre_emit, deps=[h_tr])

            lgT3 = lgT_sb[:].rearrange("p (i e) -> p i e", e=8)
            w23 = w2_sb[:].rearrange("p (i e) -> p i e", e=8)
            eq23 = t2s[:, 0:8 * NTI].rearrange("p (i e) -> p i e", e=8)
            eq2b = t2s[:, 8 * NTI:16 * NTI].rearrange("p (i e) -> p i e", e=8)
            m1 = sc_sb[:, 0 * NTI:1 * NTI]
            m2 = sc_sb[:, 1 * NTI:2 * NTI]
            dd = sc_sb[:, 2 * NTI:3 * NTI]
            e1 = sc_sb[:, 3 * NTI:4 * NTI]

            def top2a_emit(v):
                v.reduce_max(m1, lgT3, axis=mybir.AxisListType.X)
                v.drain()
                a_, b_ = bass.broadcast_tensor_aps(lgT3, m1[:, :, None])
                v.tensor_tensor(out=w23, in0=a_, in1=b_, op=ALU.is_equal)
                v.drain()
                v.scalar_tensor_tensor(out=lgT3, in0=w23, scalar=-1e30, in1=lgT3,
                                       op0=ALU.mult, op1=ALU.add)
                v.drain()
                v.reduce_max(m2, lgT3, axis=mybir.AxisListType.X)
                v.drain()
                v.tensor_tensor(out=dd, in0=m1, in1=m2, op=ALU.subtract)
            h_t2a = prog.stage("v", top2a_emit, deps=[h_tre])

            h_sg = prog.stage("s", lambda s: s.activation(out=e1, in_=dd, func=AF.Sigmoid),
                              deps=[h_t2a])

            def top2b_emit(v):
                a_, b_ = bass.broadcast_tensor_aps(lgT3, m2[:, :, None])
                v.tensor_tensor(out=eq23, in0=a_, in1=b_, op=ALU.is_equal)
                a_, b_ = bass.broadcast_tensor_aps(w23, e1[:, :, None])
                v.tensor_tensor(out=w23, in0=a_, in1=b_, op=ALU.mult)
                v.drain()
                a_, b_ = bass.broadcast_tensor_aps(eq23, e1[:, :, None])
                v.tensor_tensor(out=eq2b, in0=a_, in1=b_, op=ALU.mult)
                v.drain()
                v.tensor_tensor(out=eq23, in0=eq23, in1=eq2b, op=ALU.subtract)
                v.drain()
                v.tensor_tensor(out=w2_sb[:], in0=w2_sb[:], in1=t2s[:, 0:8 * NTI], op=ALU.add)
            h_t2b = prog.stage("v", top2b_emit, deps=[h_sg])

            def trw_emit(p):
                for i in range(NTI):
                    dst = (ps_mb, ps_rb)[i]
                    p.transpose(dst[0:8, 0:128], w2_sb[:, i * 8:(i + 1) * 8], id_sb[:, :])
            h_trw = prog.stage("p", trw_emit, deps=[h_t2b, h_tre])

            def w8_emit(v):
                for i in range(NTI):
                    src = (ps_mb, ps_rb)[i]
                    v.tensor_copy(out=w8_sb[:, i * 128:(i + 1) * 128], in_=src[0:8, 0:128])
            h_w8 = prog.stage("v", w8_emit, deps=[h_trw])

            def hall_emit(p):
                for k in range(32):
                    c = slice(k * TC, (k + 1) * TC)
                    p.matmul(ps_h[:], aa_sb[:, k * 128:(k + 1) * 128], xln_sb[:, c],
                             start=(k == 0), stop=(k == 31))
                p.matmul(ps_aux[:], e8_sb[:], w8_sb[:], start=True, stop=True)
            h_hall = prog.stage("p", hall_emit, deps=[h_xln, h_w8, h_w, h_lge])

            def g_emit(v):
                v.tensor_copy(out=wbc_sb[:], in_=ps_aux[:])
                v.drain()
                v.scalar_tensor_tensor(out=g_sb[:], in0=ps_h[:], scalar=2.0, in1=wbc_sb[:],
                                       op0=ALU.mult, op1=ALU.mult)
            h_g = prog.stage("v", g_emit, deps=[h_hall])

            h_ev = None
            h_mm = None
            h_wb = None
            for ofb in range(16):
                prev_reader = h_mm
                h_wb = prog.stage("q", (lambda ofb_: lambda q: [
                    q.dma_start(out=_s3(wb_sb[:], 256), in_=_r3(wbT[:, ofb_ * 256:(ofb_ + 1) * 256], 256)),
                ])(ofb), deps=[h_xln, prev_reader], ndma=1)

                def mm_emit(p, ofb_=ofb):
                    for j in range(2):
                        pso = ps_o0 if j == 0 else ps_o1
                        ofc = ofb_ * 2 + j
                        for k in range(32):
                            c = slice(k * TC, (k + 1) * TC)
                            p.matmul(pso[:], wb_sb[:, (k * 2 + j) * 128:(k * 2 + j + 1) * 128],
                                     xln_sb[:, c], start=(k == 0), stop=False)
                        p.matmul(pso[:], ba_sb[:, ofc * 128:(ofc + 1) * 128], g_sb[:],
                                 start=False, stop=True)
                h_mm = prog.stage("p", mm_emit, deps=[h_wb, h_g, h_ev])

                def ev_emit(v, ofb_=ofb):
                    for j in range(2):
                        pso = ps_o0 if j == 0 else ps_o1
                        ofc = ofb_ * 2 + j
                        c = slice(ofc * TC, (ofc + 1) * TC)
                        v.scalar_tensor_tensor(out=y_sb[:, c], in0=pso[:],
                                               scalar=bb_sb[:, ofc:ofc + 1],
                                               in1=x_sb[:, c], op0=ALU.add, op1=ALU.add)
                h_ev = prog.stage("v", ev_emit, deps=[h_mm, h_out])
            h_ffn_done = h_ev
            h_wb_used = h_mm

            h_out = prog.stage("g", (lambda tsl_: lambda g: [
                g.dma_start(out=_r3(outs[:, tsl_], TC), in_=_s3(y_sb[:], TC)),
            ])(tsl), deps=[h_ffn_done], ndma=1)

        _run(nc, prog, ctx, [h_out])
    return nc



def build_fused():
    nc = bass.Bass("TRN2", target_bir_lowering=False)
    xs = nc.dram_tensor("xs", [D, TSL], F32, kind="ExternalInput").ap()
    xc = nc.dram_tensor("xc", [CSL, TTOK], F32, kind="ExternalInput").ap()
    xpw_my = nc.dram_tensor("xpw_my", [128, 4 * 160], F32, kind="ExternalInput").ap()
    dtw_my = nc.dram_tensor("dtw_my", [128, CSL], F32, kind="ExternalInput").ap()
    dtb_my = nc.dram_tensor("dtb_my", [128, 4], F32, kind="ExternalInput").ap()
    alog_my = nc.dram_tensor("alog_my", [128, 64], F32, kind="ExternalInput").ap()
    dvec_my = nc.dram_tensor("dvec_my", [128, 4], F32, kind="ExternalInput").ap()
    wbT = nc.dram_tensor("wbT", [D, D], F32, kind="ExternalInput").ap()
    bb = nc.dram_tensor("bb", [128, 32], F32, kind="ExternalInput").ap()
    wrT = nc.dram_tensor("wrT", [D, 8], F32, kind="ExternalInput").ap()
    aaT = nc.dram_tensor("aaT", [D, 128], F32, kind="ExternalInput").ap()
    baT = nc.dram_tensor("baT", [128, D], F32, kind="ExternalInput").ap()
    identb = nc.dram_tensor("identb", [128, 128], BF16, kind="ExternalInput").ap()
    ident8 = nc.dram_tensor("ident8", [8, 8], F32, kind="ExternalInput").ap()
    identf = nc.dram_tensor("identf", [128, 128], F32, kind="ExternalInput").ap()
    e8m = nc.dram_tensor("e8m", [8, 128], F32, kind="ExternalInput").ap()
    sel32 = nc.dram_tensor("sel32", [32, 32 * 128], BF16, kind="ExternalInput").ap()
    outs = nc.dram_tensor("outs", [D, TSL], F32, kind="ExternalOutput").ap()
    ar_in = nc.dram_tensor("ar_in", [163, TTOK], F32, kind="Internal").ap()
    ar_out = nc.dram_tensor("ar_out", [163, TTOK], F32, kind="Internal").ap()
    mrows = nc.dram_tensor("mrows", [3, TTOK], F32, kind="Internal").ap()
    ylocal = nc.dram_tensor("ylocal", [8 * CSL, TSL], F32, kind="Internal").ap()
    ygath = nc.dram_tensor("ygath", [8 * D, TSL], F32, kind="Internal").ap()
    wbh = nc.dram_tensor("wbh", [D, D], BF16, kind="Internal").ap()
    aah = nc.dram_tensor("aah", [D, 128], BF16, kind="Internal").ap()
    bah = nc.dram_tensor("bah", [128, D], BF16, kind="Internal").ap()
    wrh = nc.dram_tensor("wrh", [D, 8], BF16, kind="Internal").ap()

    RG = [[0, 1, 2, 3, 4, 5, 6, 7]]
    prog = Prog()
    ncc = {"c": 0}

    with ExitStack() as ctx:
        sb = lambda n_, sh, dt=F32: ctx.enter_context(nc.sbuf_tensor(n_, sh, dt))
        xpw_sb = sb("xpw_sb", [128, 4 * 160])
        dtw_sb = sb("dtw_sb", [128, CSL])
        dtwb_sb = sb("dtwb_sb", [128, CSL], BF16)
        dtb_sb = sb("dtb_sb", [128, 4])
        a_sb = sb("a_sb", [128, 64])
        d_sb = sb("d_sb", [128, 4])
        carry = sb("carry", [128, 64])
        wr_sb = sb("wr_sb", [128, 32 * 8], BF16)
        bb_sb = sb("bb_sb", [128, 32])
        idb_sb = sb("idb_sb", [128, 128], BF16)
        id8_sb = sb("id8_sb", [8, 8])
        idf_sb = sb("idf_sb", [128, 128])
        e8_sb = sb("e8_sb", [8, 128])
        sel_sb = sb("sel_sb", [32, 32 * 128], BF16)
        ones_col = sb("ones_col", [128, 1])
        ones_row = sb("ones_row", [1, 128])
        csd_sb = sb("csd_sb", [128, 1])
        csb_sb = sb("csb_sb", [32, 1])
        stt_a = sb("stt_a", [128, 128])
        stt_b = sb("stt_b", [128, 128])
        stt_c = sb("stt_c", [128, 128])
        stt_d = sb("stt_d", [128, 128])
        zrow = sb("zrow", [1, 2048])
        st = sb("st", [1, 8 * 256])
        cast_in = sb("cast_in", [128, 2048])
        cast_out = sb("cast_out", [128, 2048], BF16)
        ARN = 35840
        arena = sb("arena", [128, ARN])
        apos = {"o": 0}

        def aslc(cols, parts=128, dt=F32):
            w = cols if dt == F32 else (cols + 1) // 2
            o = apos["o"]
            apos["o"] = o + w
            assert apos["o"] <= ARN, (o, w)
            if dt == F32:
                return arena[0:parts, o:o + cols]
            return arena[0:parts, o:o + w].bitcast(dt)[0:parts, 0:cols]

        pbank = [ctx.enter_context(nc.psum_tensor(f"pb{i}", [128, 512], F32)) for i in range(8)]

        h_ones = prog.stage("v", lambda v: (v.memset(ones_col[:], 1.0),
                                            v.memset(ones_row[:], 1.0),
                                            v.memset(carry[:], 0.0),
                                            v.memset(zrow[:], 0.0)))
        cidbox = {}

        def w0_emit(q):
            cidbox["cid"] = q.partition_id()
            return [
                q.dma_start(out=xpw_sb[:], in_=xpw_my),
                q.dma_start(out=dtw_sb[:], in_=dtw_my),
                q.dma_start(out=dtb_sb[:], in_=dtb_my),
                q.dma_start(out=a_sb[:], in_=alog_my),
                q.dma_start(out=d_sb[:], in_=dvec_my),
                q.dma_start(out=bb_sb[:], in_=bb),
                q.dma_start(out=idb_sb[:], in_=identb),
                q.dma_start(out=id8_sb[:], in_=ident8),
                q.dma_start(out=idf_sb[:], in_=identf),
                q.dma_start(out=e8_sb[:], in_=e8m),
                q.dma_start(out=sel_sb[:], in_=sel32),
            ]
        h_w0 = prog.stage("q", w0_emit, ndma=11)
        h_a = prog.stage("s", lambda s: (s.activation(out=a_sb[:], in_=a_sb[:], func=AF.Exp),
                                         s.drain(),
                                         s.mul(out=a_sb[:], in_=a_sb[:], mul=-1.0)),
                         deps=[h_w0])
        h_dtwb = prog.stage("v", lambda v: v.tensor_copy(out=dtwb_sb[:], in_=dtw_sb[:]),
                            deps=[h_w0])

        def emit_cast(srcT, dstT, col0, ncols, dep, gdep):
            h1 = prog.stage("q", (lambda: lambda q: [
                q.dma_start(out=cast_in[:, 0:32 * ncols].rearrange("p (k c) -> p k c", c=ncols),
                            in_=_r3(srcT[:, col0:col0 + ncols], ncols)),
            ])(), deps=[dep], ndma=1)
            h2 = prog.stage("s", (lambda: lambda s: s.copy(
                out=cast_out[:, 0:32 * ncols], in_=cast_in[:, 0:32 * ncols]))(),
                deps=[h1, gdep])
            h3 = prog.stage("g", (lambda: lambda g: [
                g.dma_start(out=_r3(dstT[:, col0:col0 + ncols], ncols),
                            in_=cast_out[:, 0:32 * ncols].rearrange("p (k c) -> p k c", c=ncols)),
            ])(), deps=[h2], ndma=1)
            return h3

        h_cast = None
        for cb in range(64):
            h_cast = emit_cast(wbT, wbh, cb * 64, 64, h_cast, h_cast)
        for cb in range(2):
            h_cast = emit_cast(aaT, aah, cb * 64, 64, h_cast, h_cast)
        h_cast = emit_cast(wrT, wrh, 0, 8, h_cast, h_cast)
        for half in range(2):
            h1 = prog.stage("q", (lambda half_: lambda q: [
                q.dma_start(out=cast_in[:], in_=baT[:, half_ * 2048:(half_ + 1) * 2048])])(half),
                deps=[h_cast], ndma=1)
            h2 = prog.stage("s", lambda s: s.copy(out=cast_out[:], in_=cast_in[:]), deps=[h1])
            h_cast = prog.stage("g", (lambda half_: lambda g: [
                g.dma_start(out=bah[:, half_ * 2048:(half_ + 1) * 2048], in_=cast_out[:])])(half),
                deps=[h2], ndma=1)
        h_castall = h_cast

        TA = 2048
        NA = TTOK // TA
        xa = aslc(4 * TA)
        sqa = aslc(4 * TA)
        xpd_st = aslc(TA)
        xpbc_st = aslc(TA, 32)
        srow_x = aslc(TA, 1)
        srow_q = aslc(TA, 1)

        def cs_emit(p):
            for k in range(4):
                p.matmul(pbank[6][:, 0:1], xpw_sb[:, k * 160:k * 160 + 128], ones_col[:],
                         start=(k == 0), stop=(k == 3))
            for k in range(4):
                p.matmul(pbank[7][0:32, 0:1], xpw_sb[:, k * 160 + 128:(k + 1) * 160], ones_col[:],
                         start=(k == 0), stop=(k == 3))
        h_cs = prog.stage("p", cs_emit, deps=[h_w0, h_ones])
        h_cse = prog.stage("v", lambda v: (v.tensor_copy(out=csd_sb[:], in_=pbank[6][:, 0:1]),
                                           v.tensor_copy(out=csb_sb[:], in_=pbank[7][0:32, 0:1])),
                           deps=[h_cs])
        h_z = prog.stage("g", lambda g: [
            g.dma_start(out=ar_in[162:163, k * 2048:(k + 1) * 2048], in_=zrow[:])
            for k in range(8)], deps=[h_ones], ndma=8)
        h_csw = prog.stage("g", lambda g: [
            g.dma_start(out=ar_in[162:163, 0:128].rearrange("one (p c) -> (one p) c", c=1), in_=csd_sb[:]),
            g.dma_start(out=ar_in[162:163, 128:160].rearrange("one (p c) -> (one p) c", c=1), in_=csb_sb[0:32, :]),
        ], deps=[h_cse, h_z], ndma=2)

        h_xa_prev = None
        h_arw = h_csw
        for t in range(NA):
            tsl = slice(t * TA, (t + 1) * TA)
            h_xa = prog.stage("q", (lambda tsl_: lambda q: [
                q.dma_start(out=xa.rearrange("p (k c) -> p k c", c=TA),
                            in_=xc[:, tsl_].rearrange("(k p) c -> p k c", p=128)),
            ])(tsl), deps=[h_xa_prev], ndma=1)
            h_sq = prog.stage("s", (lambda: lambda s: s.activation(out=sqa, in_=xa, func=AF.Square))(),
                              deps=[h_xa])
            h_pa = None
            h_ve = None
            h_ve2 = None
            for u in range(4):
                def pa_u(p, u_=u):
                    b0, b1, b2, b3 = (0, 1, 2, 3) if u_ % 2 == 0 else (4, 5, 6, 7)
                    for k in range(4):
                        kc = slice(k * TA + u_ * 512, k * TA + (u_ + 1) * 512)
                        p.matmul(pbank[b0][0:1, :], ones_col[:], xa[:, kc],
                                 start=(k == 0), stop=(k == 3))
                    for k in range(4):
                        kc = slice(k * TA + u_ * 512, k * TA + (u_ + 1) * 512)
                        p.matmul(pbank[b1][0:1, :], ones_col[:], sqa[:, kc],
                                 start=(k == 0), stop=(k == 3))
                    for k in range(4):
                        kc = slice(k * TA + u_ * 512, k * TA + (u_ + 1) * 512)
                        p.matmul(pbank[b2][:, :], xpw_sb[:, k * 160:k * 160 + 128], xa[:, kc],
                                 start=(k == 0), stop=(k == 3))
                    for k in range(4):
                        kc = slice(k * TA + u_ * 512, k * TA + (u_ + 1) * 512)
                        p.matmul(pbank[b3][0:32, :], xpw_sb[:, k * 160 + 128:(k + 1) * 160], xa[:, kc],
                                 start=(k == 0), stop=(k == 3))
                prev2 = h_ve2
                h_pa = prog.stage("p", pa_u, deps=[h_sq, h_xa, prev2, h_cse])

                def ve_u(v, u_=u):
                    b0, b1, b2, b3 = (0, 1, 2, 3) if u_ % 2 == 0 else (4, 5, 6, 7)
                    cs = slice(u_ * 512, (u_ + 1) * 512)
                    v.tensor_copy(out=srow_x[:, cs], in_=pbank[b0][0:1, :])
                    v.tensor_copy(out=srow_q[:, cs], in_=pbank[b1][0:1, :])
                    v.tensor_copy(out=xpd_st[:, cs], in_=pbank[b2][:, :])
                    v.tensor_copy(out=xpbc_st[:, cs], in_=pbank[b3][0:32, :])
                h_ve2 = h_ve
                h_ve = prog.stage("v", (ve_u), deps=[h_pa, h_arw])
            h_arw = prog.stage("g", (lambda tsl_: lambda g: [
                g.dma_start(out=ar_in[0:128, tsl_], in_=xpd_st),
                g.dma_start(out=ar_in[128:160, tsl_], in_=xpbc_st),
                g.dma_start(out=ar_in[160:161, tsl_], in_=srow_x),
                g.dma_start(out=ar_in[161:162, tsl_], in_=srow_q),
            ])(tsl), deps=[h_ve], ndma=4)
            h_xa_prev = h_ve

        csem = ctx.enter_context(nc.semaphore("sem_c"))

        def ar_emit(g):
            g.collective_compute("AllReduce", ALU.add, replica_groups=RG,
                                 ins=[ar_in], outs=[ar_out]).then_inc(csem, 1)
        h_arq = prog.stage("g", ar_emit, deps=[h_arw, h_csw], raw=True)
        ncc["c"] += 1
        AR_DONE = ("c", ncc["c"])

        h_strd = prog.stage("q", lambda q: [
            q.dma_start(out=stt_a[:], in_=ar_out[160:161, :].rearrange("one (p c) -> (one p) c", c=128)),
            q.dma_start(out=stt_b[:], in_=ar_out[161:162, :].rearrange("one (p c) -> (one p) c", c=128)),
            q.dma_start(out=csd_sb[:], in_=ar_out[162:163, 0:128].rearrange("one (p c) -> (one p) c", c=1)),
            q.dma_start(out=csb_sb[0:32, :], in_=ar_out[162:163, 128:160].rearrange("one (p c) -> (one p) c", c=1)),
        ], deps=[AR_DONE, h_cse], ndma=4)

        def stats_emit(v):
            v.tensor_scalar_mul(stt_a[:], stt_a[:], 1.0 / D)
            v.tensor_scalar(stt_b[:], stt_b[:], 1.0 / D, EPS, op0=ALU.mult, op1=ALU.add)
            v.drain()
            v.tensor_tensor(out=stt_c[:], in0=stt_a[:], in1=stt_a[:], op=ALU.mult)
            v.drain()
            v.tensor_tensor(out=stt_c[:], in0=stt_b[:], in1=stt_c[:], op=ALU.subtract)
            v.drain()
            v.reciprocal(out=stt_d[:], in_=stt_c[:])
        h_st1 = prog.stage("v", stats_emit, deps=[h_strd])
        h_st2 = prog.stage("s", lambda s: s.activation(out=stt_b[:], in_=stt_d[:], func=AF.Sqrt),
                           deps=[h_st1])
        h_st3 = prog.stage("v", lambda v: v.tensor_tensor(out=stt_c[:], in0=stt_a[:], in1=stt_b[:],
                                                          op=ALU.mult), deps=[h_st2])
        h_mw = prog.stage("g", lambda g: [
            g.dma_start(out=mrows[0:1, :].rearrange("one (p c) -> (one p) c", c=128), in_=stt_a[:]),
            g.dma_start(out=mrows[1:2, :].rearrange("one (p c) -> (one p) c", c=128), in_=stt_b[:]),
            g.dma_start(out=mrows[2:3, :].rearrange("one (p c) -> (one p) c", c=128), in_=stt_c[:]),
        ], deps=[h_st3], ndma=3)

        apos["o"] = 0
        TB = 512
        NB = TTOK // TB
        NCH = 4
        xbp = [[aslc(TB) for _ in range(NCH)] for _ in range(2)]
        dtp = [[aslc(TB) for _ in range(NCH)] for _ in range(2)]
        dtxp = [[aslc(TB, 128, BF16) for _ in range(NCH)] for _ in range(2)]
        xlnb = [aslc(TB) for _ in range(NCH)]
        seedb = [aslc(TB) for _ in range(NCH)]
        dAn = [[aslc(TB) for _ in range(NCH)] for _ in range(2)]
        dBxb = [aslc(TB, 128, BF16) for _ in range(NCH)]
        hbf = [aslc(TB, 128, BF16) for _ in range(NCH)]
        bcb_sb = [[aslc(TB, 128, BF16) for _ in range(2)] for _ in range(2)]
        qbf = [[aslc(TB, 128, BF16) for _ in range(NCH)] for _ in range(2)]
        xpdtp = [aslc(TB) for _ in range(2)]
        xpdcp = [aslc(TB, 128, BF16) for _ in range(2)]
        xpbtp = [aslc(TB, 32) for _ in range(2)]
        xpbcp = [aslc(TB, 32, BF16) for _ in range(2)]
        t1s = aslc(TB)
        t2s_ = aslc(TB)
        mrowp = [aslc(TB, 1) for _ in range(2)]
        rrowp = [aslc(TB, 1) for _ in range(2)]
        mrrowp = [aslc(TB, 1) for _ in range(2)]

        h_scan_hist = []
        h_yacc_hist = []
        h_scan_done = {}
        h_prep_done = {}
        h_dt_done = {}
        h_gy = {-1: h_mw}
        h_yev = {-1: None}
        h_corr = None
        h_sp = None
        ag_handles = []

        def sh(idx):
            return h_scan_hist[idx] if 0 <= idx < len(h_scan_hist) else None

        def yh(idx):
            return h_yacc_hist[idx] if 0 <= idx < len(h_yacc_hist) else None

        for t in range(NB):
            par = t % 2
            tsl = slice(t * TB, (t + 1) * TB)
            h_ld = prog.stage("q", (lambda tsl_, par_: lambda q: [
                q.dma_start(out=xbp[par_][i], in_=xc[128 * i:128 * (i + 1), tsl_])
                for i in range(NCH)] + [
                q.dma_start(out=xpdtp[par_], in_=ar_out[0:128, tsl_]),
                q.dma_start(out=xpbtp[par_], in_=ar_out[128:160, tsl_]),
                q.dma_start(out=mrowp[par_][:], in_=mrows[0:1, tsl_]),
                q.dma_start(out=rrowp[par_][:], in_=mrows[1:2, tsl_]),
                q.dma_start(out=mrrowp[par_][:], in_=mrows[2:3, tsl_]),
            ])(tsl, par), deps=[h_prep_done.get(t - 2), h_scan_done.get(t - 2),
                                h_dt_done.get(t - 2), AR_DONE, h_mw], ndma=NCH + 5)

            def bcrm_emit(p, par_=par):
                p.matmul(pbank[6][:, :], ones_row[:], rrowp[par_][:], start=True, stop=True)
                p.matmul(pbank[7][:, :], ones_row[:], mrrowp[par_][:], start=True, stop=True)
            h_bcrm = prog.stage("p", bcrm_emit, deps=[h_ld, h_scan_done.get(t - 1), h_ones])

            def corr_emit(v, par_=par):
                v.tensor_tensor(out=t1s, in0=xpdtp[par_], in1=pbank[6][:, :], op=ALU.mult)
                v.tensor_scalar_mul(t2s_, pbank[7][:, :], csd_sb[:, 0:1])
                v.drain()
                v.tensor_tensor(out=xpdcp[par_], in0=t1s, in1=t2s_, op=ALU.subtract)
                v.drain()
                v.tensor_tensor(out=t1s[0:32, :], in0=xpbtp[par_], in1=pbank[6][0:32, :], op=ALU.mult)
                v.tensor_scalar_mul(t2s_[0:32, :], pbank[7][0:32, :], csb_sb[:, 0:1])
                v.drain()
                v.tensor_tensor(out=xpbcp[par_], in0=t1s[0:32, :], in1=t2s_[0:32, :], op=ALU.subtract)
            h_corr = prog.stage("v", corr_emit, deps=[h_bcrm, h_cse])

            h_pdp = None
            h_sp = None
            for i in range(NCH):
                def pd3(p, i_=i, par_=par):
                    bk = pbank[6] if i_ % 2 == 0 else pbank[7]
                    p.matmul(bk[:, :], dtwb_sb[:, i_ * 128:(i_ + 1) * 128], xpdcp[par_],
                             start=True, stop=True)
                h_pdp2 = prog.stage("p", pd3, deps=[h_corr, h_sp, h_dtwb])

                def sp3(s, i_=i, par_=par):
                    bk = pbank[6] if i_ % 2 == 0 else pbank[7]
                    s.activation(out=dtp[par_][i_], in_=bk[:, :], func=AF.Exp,
                                 bias=dtb_sb[:, i_:i_ + 1])
                    s.drain()
                    s.activation(out=dtp[par_][i_], in_=dtp[par_][i_], func=AF.Ln, bias=1.0)
                h_sp = prog.stage("s", sp3, deps=[h_pdp, h_pdp2])
                h_pdp = h_pdp2
            h_dt_done[t] = h_sp

            def bcmr_emit(p, par_=par):
                p.matmul(pbank[4][:, :], ones_row[:], mrowp[par_][:], start=True, stop=True)
                p.matmul(pbank[5][:, :], ones_row[:], rrowp[par_][:], start=True, stop=True)
            h_bcmr = prog.stage("p", bcmr_emit, deps=[h_ld, h_scan_done.get(t - 1), h_ones])

            def xln_emit(v, par_=par):
                for i in range(NCH):
                    v.tensor_tensor(out=dAn[0][i], in0=xbp[par_][i], in1=pbank[4][:, :], op=ALU.subtract)
                v.drain()
                for i in range(NCH):
                    v.tensor_tensor(out=xlnb[i], in0=dAn[0][i], in1=pbank[5][:, :], op=ALU.mult)
            h_xln = prog.stage("v", xln_emit, deps=[h_bcmr, h_scan_done.get(t - 1),
                                                    h_yev.get(t - 1)])

            def prep_emit(v, par_=par):
                for i in range(NCH):
                    v.tensor_tensor(out=dtxp[par_][i], in0=dtp[par_][i], in1=xlnb[i], op=ALU.mult)
            h_prep = prog.stage("v", prep_emit, deps=[h_xln, h_sp])
            h_prep_done[t] = h_prep

            def seed_emit(s, par_=par):
                for i in range(NCH):
                    s.activation(out=seedb[i], in_=xlnb[i], func=AF.Copy, scale=d_sb[:, i:i + 1])
            h_seed = prog.stage("s", seed_emit, deps=[h_xln, h_sp, h_gy.get(t - 1), h_w0])

            h_yacc = None
            h_sev = None
            base = len(h_scan_hist)
            for n in range(16):
                np2 = n % 2
                bB, bC = (4, 5) if np2 == 0 else (6, 7)

                def bcb_emit(p, n_=n, bB_=bB, bC_=bC, par_=par):
                    p.matmul(pbank[bB_][:, :], sel_sb[:, n_ * 128:(n_ + 1) * 128],
                             xpbcp[par_], start=True, stop=True)
                    p.matmul(pbank[bC_][:, :], sel_sb[:, (16 + n_) * 128:(17 + n_) * 128],
                             xpbcp[par_], start=True, stop=True)
                bdeps = [h_corr, sh(base + n - 2), h_w0]
                if np2 == 0:
                    bdeps.append(h_xln if n == 0 else None)
                else:
                    bdeps.append(h_dt_done[t])
                h_pbB = prog.stage("p", bcb_emit, deps=bdeps)

                def sev_emit(s, np2_=np2, bB_=bB, bC_=bC):
                    s.activation(out=bcb_sb[np2_][0], in_=pbank[bB_][:, :], func=AF.Copy)
                    s.activation(out=bcb_sb[np2_][1], in_=pbank[bC_][:, :], func=AF.Copy)
                h_sev = prog.stage("s", sev_emit, deps=[h_pbB, sh(base + n - 2)])

                def dA_emit(s, n_=n, par_=par, np2_=np2):
                    for i in range(NCH):
                        s.activation(out=dAn[np2_][i], in_=dtp[par_][i], func=AF.Exp,
                                     scale=a_sb[:, i * 16 + n_:i * 16 + n_ + 1])
                h_sn = prog.stage("s", dA_emit, deps=[h_sp, h_a, sh(base + n - 2), h_seed])

                def scan_emit(v, n_=n, t_=t, par_=par, np2_=np2, bB_=bB, bC_=bC):
                    for i in range(NCH):
                        v.tensor_tensor(out=dBxb[i], in0=dtxp[par_][i], in1=bcb_sb[np2_][0],
                                        op=ALU.mult)
                    if t_ == NB // 2:
                        for i in range(NCH):
                            v.memset(carry[:, i * 16 + n_:i * 16 + n_ + 1], 0.0)
                    v.drain()
                    for i in range(NCH):
                        col = slice(i * 16 + n_, i * 16 + n_ + 1)
                        v.tensor_tensor_scan(out=hbf[i], data0=dAn[np2_][i], data1=dBxb[i],
                                             initial=carry[:, col], op0=ALU.mult, op1=ALU.add)
                    v.drain()
                    for i in range(NCH):
                        col = slice(i * 16 + n_, i * 16 + n_ + 1)
                        v.tensor_copy(out=carry[:, col], in_=hbf[i][:, TB - 1:TB])
                    for i in range(NCH):
                        v.tensor_tensor(out=qbf[np2_][i], in0=hbf[i], in1=bcb_sb[np2_][1],
                                        op=ALU.mult)
                sdeps = [h_pbB, h_sn, h_sev, h_prep, yh(base + n - 2)]
                if n == 0:
                    sdeps.append(h_seed)
                h_vq = prog.stage("v", scan_emit, deps=sdeps)
                h_scan_hist.append(h_vq)

                def yacc_emit(p, n_=n, np2_=np2):
                    for i in range(NCH):
                        p.matmul(pbank[i][:, :], idb_sb[:], qbf[np2_][i],
                                 start=(n_ == 0), stop=(n_ == 15))
                h_yacc = prog.stage("p", yacc_emit, deps=[h_vq, h_yev.get(t - 1), h_w0])
                h_yacc_hist.append(h_yacc)
            h_scan_done[t] = h_vq

            def yev_emit(v):
                for i in range(NCH):
                    v.tensor_tensor(out=seedb[i], in0=pbank[i][:, :], in1=seedb[i], op=ALU.add)
            h_yev[t] = prog.stage("v", yev_emit, deps=[h_yacc, h_seed])

            j = t // 4
            co = (t % 4) * TB
            h_gy[t] = prog.stage("g", (lambda j_, co_: lambda g: [
                g.dma_start(out=ylocal[CSL * j_ + 128 * i:CSL * j_ + 128 * (i + 1), co_:co_ + TB],
                            in_=seedb[i]) for i in range(NCH)])(j, co),
                deps=[h_yev[t]], ndma=NCH)

            if t % 4 == 3:
                agdep = ag_handles[-1] if ag_handles else None

                def ag_emit(g, j_=j):
                    g.collective_compute("AllGather", ALU.bypass, replica_groups=RG,
                                         ins=[ylocal[CSL * j_:CSL * (j_ + 1), :]],
                                         outs=[ygath[D * j_:D * (j_ + 1), :]]).then_inc(csem, 1)
                prog.stage("g", ag_emit, deps=[h_gy[t], agdep], raw=True)
                ncc["c"] += 1
                ag_handles.append(("c", ncc["c"]))

        AG_ALL = ag_handles[-1]
        B_TAIL = [h_scan_done[NB - 1], h_prep_done[NB - 1], h_dt_done[NB - 1],
                  h_yev[NB - 1], h_gy[NB - 1]]

        apos["o"] = 0
        TC3 = 256
        NC3 = TSL // TC3
        NTI = TC3 // 128
        x3 = aslc(32 * TC3)
        y3 = aslc(32 * TC3)
        xln3 = aslc(32 * TC3, 128, BF16)
        wb3p = [aslc(32 * 256, 128, BF16) for _ in range(2)]
        aa3 = aslc(32 * 128, 128, BF16)
        ba3 = aslc(D, 128, BF16)
        lg3 = aslc(TC3, 8)
        lgT3t = aslc(8 * NTI)
        w23t = aslc(8 * NTI)
        t2s3 = aslc(16 * NTI)
        sc3 = aslc(4 * NTI)
        w83 = aslc(TC3, 8)
        g3 = aslc(TC3, 128, BF16)
        wbc3 = aslc(TC3)

        ps_sx, ps_sq, ps_mb, ps_rb, ps_aux, ps_h, ps_o0, ps_o1 = pbank

        h_w2 = prog.stage("q", lambda q: [
            q.dma_start(out=aa3.rearrange("p (k m) -> p k m", m=128), in_=_r3(aah, 128)),
            q.dma_start(out=ba3, in_=bah),
            q.dma_start(out=wr_sb[:].rearrange("p (k m) -> p k m", m=8), in_=_r3(wrh, 8)),
        ], deps=[AG_ALL, h_castall] + B_TAIL, ndma=3)

        ygv = ygath.rearrange("(b p) c -> p b c", p=128)
        h_out = {-1: None}
        h_ffn_done = None
        h_wb_used = [None, None]
        for t in range(NC3):
            tsl = slice(t * TC3, (t + 1) * TC3)

            def ld3(q, tsl_=tsl):
                cid = cidbox["cid"]
                return [
                    q.dma_start(out=_s3(x3, TC3), in_=_r3(xs[:, tsl_], TC3)),
                    q.dma_start(out=y3.rearrange("p (k c) -> p k c", c=TC3),
                                in_=ygv[:, bass.ts(cid, 32), tsl_]),
                ]
            h_ld = prog.stage("q", ld3, deps=[h_ffn_done, h_out[t - 1], AG_ALL, h_w2] + B_TAIL,
                              ndma=2)

            def res_emit(v):
                for k in range(32):
                    c = slice(k * TC3, (k + 1) * TC3)
                    v.tensor_tensor(out=x3[:, c], in0=x3[:, c], in1=y3[:, c], op=ALU.add)
            h_res = prog.stage("v", res_emit, deps=[h_ld])

            h_sq3 = prog.stage("s", lambda s: s.activation(out=y3, in_=x3, func=AF.Square),
                               deps=[h_res, h_out[t - 1]])

            def sums3(p):
                for k in range(32):
                    c = slice(k * TC3, (k + 1) * TC3)
                    p.matmul(ps_sx[0:1, 0:TC3], ones_col[:], x3[:, c], start=(k == 0), stop=(k == 31))
                for k in range(32):
                    c = slice(k * TC3, (k + 1) * TC3)
                    p.matmul(ps_sq[0:1, 0:TC3], ones_col[:], y3[:, c], start=(k == 0), stop=(k == 31))
            h_sum3 = prog.stage("p", sums3, deps=[h_sq3, h_ones] + h_wb_used)

            def st31(v):
                v.tensor_scalar_mul(st[:, 0:TC3], ps_sx[0:1, 0:TC3], 1.0 / D)
                v.tensor_scalar(st[:, TC3:2 * TC3], ps_sq[0:1, 0:TC3], 1.0 / D, EPS,
                                op0=ALU.mult, op1=ALU.add)
                v.drain()
                v.tensor_tensor(out=st[:, 2 * TC3:3 * TC3], in0=st[:, 0:TC3],
                                in1=st[:, 0:TC3], op=ALU.mult)
                v.drain()
                v.tensor_tensor(out=st[:, 2 * TC3:3 * TC3], in0=st[:, TC3:2 * TC3],
                                in1=st[:, 2 * TC3:3 * TC3], op=ALU.subtract)
                v.drain()
                v.reciprocal(out=st[:, 3 * TC3:4 * TC3], in_=st[:, 2 * TC3:3 * TC3])
            h_st31 = prog.stage("v", st31, deps=[h_sum3])
            h_st32 = prog.stage("s", lambda s: s.activation(out=st[:, 4 * TC3:5 * TC3],
                                                            in_=st[:, 3 * TC3:4 * TC3], func=AF.Sqrt),
                                deps=[h_st31])

            def bc3(p):
                p.matmul(ps_mb[:, 0:TC3], ones_row[:], st[0:1, 0:TC3], start=True, stop=True)
                p.matmul(ps_rb[:, 0:TC3], ones_row[:], st[0:1, 4 * TC3:5 * TC3], start=True, stop=True)
            h_bc3 = prog.stage("p", bc3, deps=[h_st32, h_ones])

            def xln3_emit(v):
                for k in range(32):
                    c = slice(k * TC3, (k + 1) * TC3)
                    v.tensor_tensor(out=y3[:, c], in0=x3[:, c], in1=ps_mb[:, 0:TC3], op=ALU.subtract)
                v.drain()
                for k in range(32):
                    c = slice(k * TC3, (k + 1) * TC3)
                    v.tensor_tensor(out=xln3[:, c], in0=y3[:, c], in1=ps_rb[:, 0:TC3], op=ALU.mult)
            h_xln3 = prog.stage("v", xln3_emit, deps=[h_bc3])

            def lg_emit(p):
                for k in range(32):
                    c = slice(k * TC3, (k + 1) * TC3)
                    p.matmul(ps_aux[0:8, 0:TC3], wr_sb[:, k * 8:(k + 1) * 8], xln3[:, c],
                             start=(k == 0), stop=(k == 31))
            h_lg = prog.stage("p", lg_emit, deps=[h_xln3, h_w2])
            h_lge = prog.stage("v", lambda v: v.tensor_copy(out=lg3, in_=ps_aux[0:8, 0:TC3]),
                               deps=[h_lg])

            def tr_emit(p):
                for i in range(NTI):
                    dst = (ps_mb, ps_rb)[i]
                    p.transpose(dst[:, 0:8], lg3[:, i * 128:(i + 1) * 128], id8_sb[:])
            h_tr = prog.stage("p", tr_emit, deps=[h_lge, h_xln3])

            def tre_emit(v):
                for i in range(NTI):
                    src = (ps_mb, ps_rb)[i]
                    v.tensor_copy(out=lgT3t[:, i * 8:(i + 1) * 8], in_=src[:, 0:8])
            h_tre = prog.stage("v", tre_emit, deps=[h_tr])

            lgT3v = lgT3t.rearrange("p (i e) -> p i e", e=8)
            w23v = w23t.rearrange("p (i e) -> p i e", e=8)
            eq23 = t2s3[:, 0:8 * NTI].rearrange("p (i e) -> p i e", e=8)
            eq2b = t2s3[:, 8 * NTI:16 * NTI].rearrange("p (i e) -> p i e", e=8)
            m1 = sc3[:, 0 * NTI:1 * NTI]
            m2 = sc3[:, 1 * NTI:2 * NTI]
            dd = sc3[:, 2 * NTI:3 * NTI]
            e1 = sc3[:, 3 * NTI:4 * NTI]

            def top2a_emit(v):
                v.reduce_max(m1, lgT3v, axis=mybir.AxisListType.X)
                v.drain()
                a_, b_ = bass.broadcast_tensor_aps(lgT3v, m1[:, :, None])
                v.tensor_tensor(out=w23v, in0=a_, in1=b_, op=ALU.is_equal)
                v.drain()
                v.scalar_tensor_tensor(out=lgT3v, in0=w23v, scalar=-1e30, in1=lgT3v,
                                       op0=ALU.mult, op1=ALU.add)
                v.drain()
                v.reduce_max(m2, lgT3v, axis=mybir.AxisListType.X)
                v.drain()
                v.tensor_tensor(out=dd, in0=m1, in1=m2, op=ALU.subtract)
            h_t2a = prog.stage("v", top2a_emit, deps=[h_tre])

            h_sg = prog.stage("s", lambda s: s.activation(out=e1, in_=dd, func=AF.Sigmoid),
                              deps=[h_t2a])

            def top2b_emit(v):
                a_, b_ = bass.broadcast_tensor_aps(lgT3v, m2[:, :, None])
                v.tensor_tensor(out=eq23, in0=a_, in1=b_, op=ALU.is_equal)
                a_, b_ = bass.broadcast_tensor_aps(w23v, e1[:, :, None])
                v.tensor_tensor(out=w23v, in0=a_, in1=b_, op=ALU.mult)
                v.drain()
                a_, b_ = bass.broadcast_tensor_aps(eq23, e1[:, :, None])
                v.tensor_tensor(out=eq2b, in0=a_, in1=b_, op=ALU.mult)
                v.drain()
                v.tensor_tensor(out=eq23, in0=eq23, in1=eq2b, op=ALU.subtract)
                v.drain()
                v.tensor_tensor(out=w23t, in0=w23t, in1=t2s3[:, 0:8 * NTI], op=ALU.add)
            h_t2b = prog.stage("v", top2b_emit, deps=[h_sg])

            def trw_emit(p):
                for i in range(NTI):
                    dst = (ps_mb, ps_rb)[i]
                    p.transpose(dst[0:8, 0:128], w23t[:, i * 8:(i + 1) * 8], idf_sb[:])
            h_trw = prog.stage("p", trw_emit, deps=[h_t2b, h_tre])

            def w8_emit(v):
                for i in range(NTI):
                    src = (ps_mb, ps_rb)[i]
                    v.tensor_copy(out=w83[:, i * 128:(i + 1) * 128], in_=src[0:8, 0:128])
            h_w8 = prog.stage("v", w8_emit, deps=[h_trw])

            def hall_emit(p):
                for k in range(32):
                    c = slice(k * TC3, (k + 1) * TC3)
                    p.matmul(ps_h[:, 0:TC3], aa3[:, k * 128:(k + 1) * 128], xln3[:, c],
                             start=(k == 0), stop=(k == 31))
                p.matmul(ps_aux[:, 0:TC3], e8_sb[:], w83, start=True, stop=True)
            h_hall = prog.stage("p", hall_emit, deps=[h_xln3, h_w8, h_w2, h_lge])

            def g_emit(v):
                v.tensor_copy(out=wbc3, in_=ps_aux[:, 0:TC3])
                v.drain()
                v.scalar_tensor_tensor(out=g3, in0=ps_h[:, 0:TC3], scalar=2.0, in1=wbc3,
                                       op0=ALU.mult, op1=ALU.mult)
            h_g = prog.stage("v", g_emit, deps=[h_hall])

            h_ev = None
            h_mm = None
            for ofb in range(16):
                wpar = ofb % 2
                h_wb = prog.stage("q", (lambda ofb_, wpar_: lambda q: [
                    q.dma_start(out=wb3p[wpar_].rearrange("p (k c) -> p k c", c=256),
                                in_=_r3(wbh[:, ofb_ * 256:(ofb_ + 1) * 256], 256)),
                ])(ofb, wpar), deps=[h_xln3, h_wb_used[wpar], h_castall], ndma=1)

                def mm_emit(p, ofb_=ofb, wpar_=wpar):
                    for j2 in range(2):
                        pso = ps_o0 if j2 == 0 else ps_o1
                        ofc = ofb_ * 2 + j2
                        for k in range(32):
                            c = slice(k * TC3, (k + 1) * TC3)
                            p.matmul(pso[:, 0:TC3],
                                     wb3p[wpar_][:, (k * 2 + j2) * 128:(k * 2 + j2 + 1) * 128],
                                     xln3[:, c], start=(k == 0), stop=False)
                        p.matmul(pso[:, 0:TC3], ba3[:, ofc * 128:(ofc + 1) * 128], g3,
                                 start=False, stop=True)
                h_mm = prog.stage("p", mm_emit, deps=[h_wb, h_g, h_ev])
                h_wb_used[wpar] = h_mm

                def ev_emit(v, ofb_=ofb):
                    for j2 in range(2):
                        pso = ps_o0 if j2 == 0 else ps_o1
                        ofc = ofb_ * 2 + j2
                        c = slice(ofc * TC3, (ofc + 1) * TC3)
                        v.scalar_tensor_tensor(out=y3[:, c], in0=pso[:, 0:TC3],
                                               scalar=bb_sb[:, ofc:ofc + 1],
                                               in1=x3[:, c], op0=ALU.add, op1=ALU.add)
                h_ev = prog.stage("v", ev_emit, deps=[h_mm, h_out[t - 1], h_xln3])
            h_ffn_done = h_ev

            h_out[t] = prog.stage("g", (lambda tsl_: lambda g: [
                g.dma_start(out=_r3(outs[:, tsl_], TC3), in_=_s3(y3, TC3)),
            ])(tsl), deps=[h_ffn_done], ndma=1)

        _run(nc, prog, ctx, [h_out[NC3 - 1]], extra_sems={"c": csem})
    return nc

_CACHE = {}


def _get(name, builder):
    if name not in _CACHE:
        _CACHE[name] = builder()
    return _CACHE[name]


def kernel(L0_attn_x, L0_attn_A_log, L0_attn_D, L0_attn_x_proj_w, L0_attn_dt_proj_w,
           L0_attn_dt_proj_b, L0_ffn_W_base, L0_ffn_b_base, L0_ffn_W_router,
           L0_ffn_A, L0_ffn_B):
    f32 = np.float32
    xT = np.ascontiguousarray(np.asarray(L0_attn_x, f32).reshape(TTOK, D).T)
    xpwT = np.ascontiguousarray(np.asarray(L0_attn_x_proj_w, f32).T)
    dtwT = np.ascontiguousarray(np.asarray(L0_attn_dt_proj_w, f32).T)
    dtb = np.ascontiguousarray(np.asarray(L0_attn_dt_proj_b, f32).reshape(32, 128).T)
    alog = np.ascontiguousarray(np.asarray(L0_attn_A_log, f32))
    dvec = np.asarray(L0_attn_D, f32)
    wbT = np.ascontiguousarray(np.asarray(L0_ffn_W_base, f32).T)
    bb = np.ascontiguousarray(np.asarray(L0_ffn_b_base, f32).reshape(32, 128).T)
    wrT = np.ascontiguousarray(np.asarray(L0_ffn_W_router, f32).T)
    aaT = np.ascontiguousarray(np.asarray(L0_ffn_A, f32).reshape(128, D).T)
    baT = np.ascontiguousarray(np.asarray(L0_ffn_B, f32).transpose(0, 2, 1).reshape(128, D))
    import ml_dtypes
    identb = np.eye(128, dtype=ml_dtypes.bfloat16)
    ident8 = np.eye(8, dtype=f32)
    identf = np.eye(128, dtype=f32)
    e8m = np.repeat(np.eye(8, dtype=f32), 16, axis=1)
    sel32 = np.zeros((32, 32 * 128), dtype=ml_dtypes.bfloat16)
    for n in range(32):
        sel32[n, n * 128:(n + 1) * 128] = 1.0

    cores = list(range(NCORE))
    nc = _get("fused", build_fused)
    in_maps = []
    for c in cores:
        ch = slice(c * CSL, (c + 1) * CSL)
        in_maps.append(dict(
            xs=np.ascontiguousarray(xT[:, c * TSL:(c + 1) * TSL]),
            xc=np.ascontiguousarray(xT[ch]),
            xpw_my=np.ascontiguousarray(
                xpwT[ch].reshape(4, 128, 160).transpose(1, 0, 2).reshape(128, 640)),
            dtw_my=np.ascontiguousarray(dtwT[:, ch]),
            dtb_my=np.ascontiguousarray(dtb[:, 4 * c:4 * (c + 1)]),
            alog_my=np.ascontiguousarray(
                alog[ch].reshape(4, 128, 16).transpose(1, 0, 2).reshape(128, 64)),
            dvec_my=np.ascontiguousarray(dvec[ch].reshape(4, 128).T),
            wbT=wbT, bb=bb, wrT=wrT, aaT=aaT, baT=baT,
            identb=identb, ident8=ident8, identf=identf, e8m=e8m, sel32=sel32,
        ))
    r = run_bass_kernel_spmd(nc, in_maps, core_ids=cores)
    out_full = np.concatenate([r.results[c]["outs"] for c in cores], axis=1)
    return np.ascontiguousarray(out_full.T).reshape(2, SEQ, D).astype(np.float32)


# version 06
# speedup vs baseline: 1.0326x; 1.0326x over previous
import numpy as np
import concourse.bass as bass
import concourse.mybir as mybir
from concourse.bass_utils import run_bass_kernel_spmd
from contextlib import ExitStack

F32 = mybir.dt.float32
BF16 = mybir.dt.bfloat16
F32R = mybir.dt.float32r
AF = mybir.ActivationFunctionType
ALU = mybir.AluOpType

D = 4096
TTOK = 16384
NCORE = 8
TSL = TTOK // NCORE
CSL = D // NCORE
SEQ = 8192
EPS = 1e-5

ENGS = ("q", "v", "s", "p", "g")


class _LastInstRecorder:

    def __init__(self, engine):
        self._e = engine
        self.last = None

    def __getattr__(self, name):
        f = getattr(self._e, name)
        if not callable(f):
            return f

        def wrap(*a, **k):
            r = f(*a, **k)
            if hasattr(r, "then_inc"):
                self.last = r
            return r
        return wrap


class Prog:

    def __init__(self):
        self.ctr = {e: 0 for e in ENGS}
        self.stream = {e: [] for e in ENGS}

    def stage(self, eng, emit, deps=(), ndma=0, raw=False):
        deps = [d for d in deps if d is not None]
        if ndma and self.ctr[eng]:
            deps.append((eng, self.ctr[eng]))
        self.stream[eng].append((list(deps), emit, "raw" if raw else ndma))
        if raw:
            return None
        if ndma:
            self.ctr[eng] += 16 * ndma
        else:
            self.ctr[eng] += 1
        return (eng, self.ctr[eng])

    def replay(self, eng_key, engine, sems):
        rec = _LastInstRecorder(engine)
        for deps, emit, ndma in self.stream[eng_key]:
            for (sk, val) in deps:
                engine.wait_ge(sems[sk], val)
            if ndma == "raw":
                emit(engine)
            elif ndma:
                insts = emit(engine)
                assert len(insts) == ndma, (eng_key, len(insts), ndma)
                for inst in insts:
                    inst.then_inc(sems[eng_key], 16)
            else:
                rec.last = None
                emit(rec)
                if rec.last is not None:
                    rec.last.then_inc(sems[eng_key], 1)
                else:
                    engine.drain().then_inc(sems[eng_key], 1)


def _run(nc, prog, ctx, final_deps, extra_sems=None):
    sems = {e: ctx.enter_context(nc.semaphore("sem_" + e)) for e in ENGS}
    if extra_sems:
        sems.update(extra_sems)
    block = ctx.enter_context(nc.Block())

    @block.sync
    def _(sync):
        prog.replay("q", sync, sems)

    @block.vector
    def _(vector):
        prog.replay("v", vector, sems)

    @block.scalar
    def _(scalar):
        prog.replay("s", scalar, sems)

    @block.tensor
    def _(tensor):
        prog.replay("p", tensor, sems)

    @block.gpsimd
    def _(gpsimd):
        prog.replay("g", gpsimd, sems)
        for (sk, val) in final_deps:
            gpsimd.wait_ge(sems[sk], val)



def _r3(dram_ap, inner):
    return dram_ap.rearrange("(k p) c -> p k c", p=128)


def _s3(sb_ap, inner):
    return sb_ap.rearrange("p (k c) -> p k c", c=inner)


def ln_stats(prog, sq_tile, x_tile, ones_col, st, ps_a, ps_b, nk, TC, deps, h_ones):
    h_sq = prog.stage("s", lambda s: s.activation(out=sq_tile[:], in_=x_tile[:], func=AF.Square),
                      deps=deps)

    def sums(p):
        for k in range(nk):
            c = slice(k * TC, (k + 1) * TC)
            p.matmul(ps_a[0:1, :], ones_col[:], x_tile[:, c], start=(k == 0), stop=(k == nk - 1))
        for k in range(nk):
            c = slice(k * TC, (k + 1) * TC)
            p.matmul(ps_b[0:1, :], ones_col[:], sq_tile[:, c], start=(k == 0), stop=(k == nk - 1))
    h_sum = prog.stage("p", sums, deps=list(deps) + [h_sq, h_ones])

    def stats1(v):
        v.tensor_scalar_mul(st[:, 0 * TC:1 * TC], ps_a[0:1, :], 1.0 / D)
        v.tensor_scalar(st[:, 1 * TC:2 * TC], ps_b[0:1, :], 1.0 / D, EPS,
                        op0=ALU.mult, op1=ALU.add)
        v.drain()
        v.tensor_tensor(out=st[:, 2 * TC:3 * TC], in0=st[:, 0 * TC:1 * TC],
                        in1=st[:, 0 * TC:1 * TC], op=ALU.mult)
        v.drain()
        v.tensor_tensor(out=st[:, 2 * TC:3 * TC], in0=st[:, 1 * TC:2 * TC],
                        in1=st[:, 2 * TC:3 * TC], op=ALU.subtract)
        v.drain()
        v.reciprocal(out=st[:, 3 * TC:4 * TC], in_=st[:, 2 * TC:3 * TC])
    h_st1 = prog.stage("v", stats1, deps=[h_sum])

    h_rt = prog.stage("s", lambda s: s.activation(out=st[:, 4 * TC:5 * TC],
                                                  in_=st[:, 3 * TC:4 * TC], func=AF.Sqrt),
                      deps=[h_st1])
    return h_rt


def bcast_mr(prog, ones_row, st, ps_m, ps_r, TC, deps, h_ones):
    def bc(p):
        p.matmul(ps_m[:, 0:TC], ones_row[:], st[0:1, 0 * TC:1 * TC], start=True, stop=True)
        p.matmul(ps_r[:, 0:TC], ones_row[:], st[0:1, 4 * TC:5 * TC], start=True, stop=True)
    return prog.stage("p", bc, deps=list(deps) + [h_ones])



def build_l1():
    nc = bass.Bass("TRN2", target_bir_lowering=False)
    xs = nc.dram_tensor("xs", [D, TSL], F32, kind="ExternalInput").ap()
    xpwT = nc.dram_tensor("xpwT", [D, 160], F32, kind="ExternalInput").ap()
    dtwT = nc.dram_tensor("dtwT", [128, D], F32, kind="ExternalInput").ap()
    dtb = nc.dram_tensor("dtb", [128, 32], F32, kind="ExternalInput").ap()
    dts = nc.dram_tensor("dts", [D, TSL], F32, kind="ExternalOutput").ap()
    xlns = nc.dram_tensor("xlns", [D, TSL], F32, kind="ExternalOutput").ap()
    bcs = nc.dram_tensor("bcs", [32, TSL], F32, kind="ExternalOutput").ap()

    TC = 256
    NT = TSL // TC
    prog = Prog()
    with ExitStack() as ctx:
        sb = lambda n_, sh, dt=F32: ctx.enter_context(nc.sbuf_tensor(n_, sh, dt))
        x_sb = sb("x_sb", [128, 32 * TC])
        sq_sb = sb("sq_sb", [128, 32 * TC])
        xln_sb = sb("xln_sb", [128, 32 * TC])
        dt_sb = sb("dt_sb", [128, 32 * TC])
        dtin_sb = sb("dtin_sb", [128, TC])
        bc_sb = sb("bc_sb", [32, TC])
        xpw_sb = sb("xpw_sb", [128, 32 * 160])
        dtw_sb = sb("dtw_sb", [128, 32 * 128])
        dtb_sb = sb("dtb_sb", [128, 32])
        ones_col = sb("ones_col", [128, 1])
        ones_row = sb("ones_row", [1, 128])
        st = sb("st", [1, 8 * TC])
        pb = [ctx.enter_context(nc.psum_tensor(f"pb{i}", [128, TC], F32)) for i in range(8)]
        ps_sx, ps_sq, ps_mb, ps_rb, ps_dt, ps_bc, ps_d0, ps_d1 = pb

        h_ones = prog.stage("v", lambda v: (v.memset(ones_col[:], 1.0), v.memset(ones_row[:], 1.0)))
        h_w = prog.stage("q", lambda q: [
            q.dma_start(out=_s3(xpw_sb[:], 160), in_=_r3(xpwT, 160)),
            q.dma_start(out=dtw_sb[:], in_=dtwT),
            q.dma_start(out=dtb_sb[:], in_=dtb),
        ], ndma=3)

        h_out = None
        h_xln_prev = None
        for t in range(NT):
            tsl = slice(t * TC, (t + 1) * TC)
            h_xld = prog.stage("q", (lambda tsl_: lambda q: [
                q.dma_start(out=_s3(x_sb[:], TC), in_=_r3(xs[:, tsl_], TC))
            ])(tsl), deps=[h_xln_prev], ndma=1)

            h_rt = ln_stats(prog, sq_sb, x_sb, ones_col, st, ps_sx, ps_sq, 32, TC,
                            deps=[h_xld], h_ones=h_ones)
            h_bc = bcast_mr(prog, ones_row, st, ps_mb, ps_rb, TC, deps=[h_rt], h_ones=h_ones)

            def xln_emit(v):
                for k in range(32):
                    c = slice(k * TC, (k + 1) * TC)
                    v.tensor_tensor(out=sq_sb[:, c], in0=x_sb[:, c], in1=ps_mb[:], op=ALU.subtract)
                v.drain()
                for k in range(32):
                    c = slice(k * TC, (k + 1) * TC)
                    v.tensor_tensor(out=xln_sb[:, c], in0=sq_sb[:, c], in1=ps_rb[:], op=ALU.mult)
            h_xln = prog.stage("v", xln_emit, deps=[h_bc, h_out])
            h_xln_prev = h_xln

            def xp_emit(p):
                for k in range(32):
                    c = slice(k * TC, (k + 1) * TC)
                    p.matmul(ps_dt[:], xpw_sb[:, k * 160:k * 160 + 128], xln_sb[:, c],
                             start=(k == 0), stop=(k == 31))
                for k in range(32):
                    c = slice(k * TC, (k + 1) * TC)
                    p.matmul(ps_bc[0:32, :], xpw_sb[:, k * 160 + 128:(k + 1) * 160], xln_sb[:, c],
                             start=(k == 0), stop=(k == 31))
            h_xp = prog.stage("p", xp_emit, deps=[h_xln, h_w])

            def evac_emit(v):
                v.tensor_copy(out=dtin_sb[:], in_=ps_dt[:])
                v.tensor_copy(out=bc_sb[:], in_=ps_bc[0:32, :])
            h_ev = prog.stage("v", evac_emit, deps=[h_xp, h_out])

            h_pe = None
            h_sp = None
            for gidx in range(16):
                def pe_emit(p, gi=gidx):
                    for j in range(2):
                        dm = gi * 2 + j
                        psd = ps_d0 if j == 0 else ps_d1
                        p.matmul(psd[:], dtw_sb[:, dm * 128:(dm + 1) * 128], dtin_sb[:],
                                 start=True, stop=True)
                h_pe = prog.stage("p", pe_emit, deps=[h_ev, h_sp])

                def sp_emit(s, gi=gidx):
                    for j in range(2):
                        dm = gi * 2 + j
                        c = slice(dm * TC, (dm + 1) * TC)
                        psd = ps_d0 if j == 0 else ps_d1
                        s.activation(out=dt_sb[:, c], in_=psd[:], func=AF.Exp,
                                     bias=dtb_sb[:, dm:dm + 1])
                        s.drain()
                        s.activation(out=dt_sb[:, c], in_=dt_sb[:, c], func=AF.Ln, bias=1.0)
                h_sp = prog.stage("s", sp_emit, deps=[h_pe, h_out])

            h_out = prog.stage("g", (lambda tsl_: lambda g: [
                g.dma_start(out=_r3(dts[:, tsl_], TC), in_=_s3(dt_sb[:], TC)),
                g.dma_start(out=_r3(xlns[:, tsl_], TC), in_=_s3(xln_sb[:], TC)),
                g.dma_start(out=bcs[:, tsl_], in_=bc_sb[:]),
            ])(tsl), deps=[h_sp, h_xln, h_ev], ndma=3)

        _run(nc, prog, ctx, [h_out])
    return nc



def build_l2():
    nc = bass.Bass("TRN2", target_bir_lowering=False)
    dtc = nc.dram_tensor("dtc", [CSL, TTOK], F32, kind="ExternalInput").ap()
    xlnc = nc.dram_tensor("xlnc", [CSL, TTOK], F32, kind="ExternalInput").ap()
    bc = nc.dram_tensor("bc", [32, TTOK], F32, kind="ExternalInput").ap()
    sel16 = nc.dram_tensor("sel16", [16, 16 * 128], F32, kind="ExternalInput").ap()
    alog = nc.dram_tensor("alog", [128, (CSL // 128) * 16], F32, kind="ExternalInput").ap()
    dvec = nc.dram_tensor("dvec", [128, CSL // 128], F32, kind="ExternalInput").ap()
    yc = nc.dram_tensor("yc", [CSL, TTOK], F32, kind="ExternalOutput").ap()

    TC = 1024
    NT = TTOK // TC
    NCH = CSL // 128
    NSUB = TC // 512
    prog = Prog()
    with ExitStack() as ctx:
        sb = lambda n_, sh, dt=F32: ctx.enter_context(nc.sbuf_tensor(n_, sh, dt))
        dt_sb = [sb(f"dt{i}", [128, TC]) for i in range(NCH)]
        xln_sb = [sb(f"xln{i}", [128, TC]) for i in range(NCH)]
        dtx_sb = [sb(f"dtx{i}", [128, TC]) for i in range(NCH)]
        y_sb = [sb(f"y{i}", [128, TC]) for i in range(NCH)]
        dA_sb = [sb(f"dA{i}", [128, TC]) for i in range(NCH)]
        dBx_sb = [sb(f"dBx{i}", [128, TC]) for i in range(NCH)]
        h_sb = [sb(f"h{i}", [128, TC]) for i in range(NCH)]
        b_sb = sb("b_sb", [16, TC])
        c_sb = sb("c_sb", [16, TC])
        sel_sb = sb("sel_sb", [16, 16 * 128])
        a_sb = sb("a_sb", [128, NCH * 16])
        d_sb = sb("d_sb", [128, NCH])
        carry = sb("carry", [128, NCH * 16])
        ones_row = sb("ones_row", [1, 128])
        ps_b = [ctx.enter_context(nc.psum_tensor(f"psb{i}", [128, 512], F32)) for i in range(NSUB)]
        ps_c = [ctx.enter_context(nc.psum_tensor(f"psc{i}", [128, 512], F32)) for i in range(NSUB)]

        prog.stage("v", lambda v: (v.memset(ones_row[:], 1.0), v.memset(carry[:], 0.0)))
        h_cst = prog.stage("q", lambda q: [
            q.dma_start(out=a_sb[:], in_=alog),
            q.dma_start(out=d_sb[:], in_=dvec),
            q.dma_start(out=sel_sb[:], in_=sel16),
        ], ndma=3)
        h_a = prog.stage("s", lambda s: (s.activation(out=a_sb[:], in_=a_sb[:], func=AF.Exp),
                                         s.drain(),
                                         s.mul(out=a_sb[:], in_=a_sb[:], mul=-1.0)),
                         deps=[h_cst])

        h_y = None
        h_vq = None
        h_sn = None
        for t in range(NT):
            tsl = slice(t * TC, (t + 1) * TC)
            h_ld = prog.stage("q", (lambda tsl_: lambda q: [
                q.dma_start(out=dt_sb[i][:], in_=dtc[128 * i:128 * (i + 1), tsl_])
                for i in range(NCH)] + [
                q.dma_start(out=xln_sb[i][:], in_=xlnc[128 * i:128 * (i + 1), tsl_])
                for i in range(NCH)] + [
                q.dma_start(out=b_sb[:], in_=bc[0:16, tsl_]),
                q.dma_start(out=c_sb[:], in_=bc[16:32, tsl_]),
            ])(tsl), deps=[h_vq, h_sn, h_y], ndma=2 * NCH + 2)

            def prep_emit(v):
                for i in range(NCH):
                    v.tensor_tensor(out=dtx_sb[i][:], in0=dt_sb[i][:], in1=xln_sb[i][:], op=ALU.mult)
                for i in range(NCH):
                    v.tensor_scalar_mul(y_sb[i][:], xln_sb[i][:], d_sb[:, i:i + 1])
            h_prep = prog.stage("v", prep_emit, deps=[h_ld, h_cst, h_y])

            h_pb = None
            h_vq = h_prep
            for n in range(16):
                def bcst_emit(p, n_=n):
                    sel = sel_sb[:, n_ * 128:(n_ + 1) * 128]
                    for u in range(NSUB):
                        cs = slice(u * 512, (u + 1) * 512)
                        p.matmul(ps_b[u][:], sel, b_sb[:, cs], start=True, stop=True)
                        p.matmul(ps_c[u][:], sel, c_sb[:, cs], start=True, stop=True)
                h_pb = prog.stage("p", bcst_emit, deps=[h_ld, h_vq, h_cst])

                def dA_emit(s, n_=n):
                    for i in range(NCH):
                        s.activation(out=dA_sb[i][:], in_=dt_sb[i][:], func=AF.Exp,
                                     scale=a_sb[:, i * 16 + n_:i * 16 + n_ + 1])
                h_sn = prog.stage("s", dA_emit, deps=[h_ld, h_a, h_vq])

                def scan_emit(v, n_=n, t_=t):
                    for i in range(NCH):
                        for u in range(NSUB):
                            cs = slice(u * 512, (u + 1) * 512)
                            v.tensor_tensor(out=dBx_sb[i][:, cs], in0=dtx_sb[i][:, cs],
                                            in1=ps_b[u][:], op=ALU.mult)
                    if t_ == NT // 2:
                        for i in range(NCH):
                            v.memset(carry[:, i * 16 + n_:i * 16 + n_ + 1], 0.0)
                    v.drain()
                    for i in range(NCH):
                        col = slice(i * 16 + n_, i * 16 + n_ + 1)
                        v.tensor_tensor_scan(out=h_sb[i][:], data0=dA_sb[i][:], data1=dBx_sb[i][:],
                                             initial=carry[:, col], op0=ALU.mult, op1=ALU.add)
                    v.drain()
                    for i in range(NCH):
                        col = slice(i * 16 + n_, i * 16 + n_ + 1)
                        v.tensor_copy(out=carry[:, col], in_=h_sb[i][:, TC - 1:TC])
                    for i in range(NCH):
                        for u in range(NSUB):
                            cs = slice(u * 512, (u + 1) * 512)
                            v.tensor_tensor(out=dBx_sb[i][:, cs], in0=h_sb[i][:, cs],
                                            in1=ps_c[u][:], op=ALU.mult)
                    v.drain()
                    for i in range(NCH):
                        v.tensor_tensor(out=y_sb[i][:], in0=y_sb[i][:], in1=dBx_sb[i][:], op=ALU.add)
                h_vq = prog.stage("v", scan_emit, deps=[h_pb, h_sn])

            h_y = prog.stage("g", (lambda tsl_: lambda g: [
                g.dma_start(out=yc[128 * i:128 * (i + 1), tsl_], in_=y_sb[i][:])
                for i in range(NCH)])(tsl), deps=[h_vq], ndma=NCH)

        _run(nc, prog, ctx, [h_y])
    return nc



def build_l3():
    nc = bass.Bass("TRN2", target_bir_lowering=False)
    xs = nc.dram_tensor("xs", [D, TSL], F32, kind="ExternalInput").ap()
    ys = nc.dram_tensor("ys", [D, TSL], F32, kind="ExternalInput").ap()
    wbT = nc.dram_tensor("wbT", [D, D], F32, kind="ExternalInput").ap()
    bb = nc.dram_tensor("bb", [128, 32], F32, kind="ExternalInput").ap()
    wrT = nc.dram_tensor("wrT", [D, 8], F32, kind="ExternalInput").ap()
    aaT = nc.dram_tensor("aaT", [D, 128], F32, kind="ExternalInput").ap()
    baT = nc.dram_tensor("baT", [128, D], F32, kind="ExternalInput").ap()
    ident = nc.dram_tensor("ident", [128, 128], F32, kind="ExternalInput").ap()
    e8m = nc.dram_tensor("e8m", [8, 128], F32, kind="ExternalInput").ap()
    outs = nc.dram_tensor("outs", [D, TSL], F32, kind="ExternalOutput").ap()

    TC = 256
    NT = TSL // TC
    NTI = TC // 128
    prog = Prog()
    with ExitStack() as ctx:
        sb = lambda n_, sh, dt=F32: ctx.enter_context(nc.sbuf_tensor(n_, sh, dt))
        x_sb = sb("x_sb", [128, 32 * TC])
        sq_sb = sb("sq_sb", [128, 32 * TC])
        y_sb = sb("y_sb", [128, 32 * TC])
        xln_sb = sb("xln_sb", [128, 32 * TC])
        wb_sb = sb("wb_sb", [128, 32 * 256])
        aa_sb = sb("aa_sb", [128, 32 * 128])
        wr_sb = sb("wr_sb", [128, 32 * 8])
        ba_sb = sb("ba_sb", [128, D])
        bb_sb = sb("bb_sb", [128, 32])
        id_sb = sb("id_sb", [128, 128])
        e8_sb = sb("e8_sb", [8, 128])
        lg_sb = sb("lg_sb", [8, TC])
        lgT_sb = sb("lgT_sb", [128, 8 * NTI])
        w2_sb = sb("w2_sb", [128, 8 * NTI])
        t2s = sb("t2s", [128, 16 * NTI])
        sc_sb = sb("sc_sb", [128, 4 * NTI])
        w8_sb = sb("w8_sb", [8, TC])
        g_sb = sb("g_sb", [128, TC])
        wbc_sb = sb("wbc_sb", [128, TC])
        ones_col = sb("ones_col", [128, 1])
        ones_row = sb("ones_row", [1, 128])
        st = sb("st", [1, 8 * TC])
        pb = [ctx.enter_context(nc.psum_tensor(f"pb{i}", [128, TC], F32)) for i in range(8)]
        ps_sx, ps_sq, ps_mb, ps_rb, ps_aux, ps_h, ps_o0, ps_o1 = pb

        h_ones = prog.stage("v", lambda v: (v.memset(ones_col[:], 1.0), v.memset(ones_row[:], 1.0)))
        h_w = prog.stage("q", lambda q: [
            q.dma_start(out=_s3(aa_sb[:], 128), in_=_r3(aaT, 128)),
            q.dma_start(out=_s3(wr_sb[:], 8), in_=_r3(wrT, 8)),
            q.dma_start(out=ba_sb[:], in_=baT),
            q.dma_start(out=bb_sb[:], in_=bb),
            q.dma_start(out=id_sb[:], in_=ident),
            q.dma_start(out=e8_sb[:], in_=e8m),
        ], ndma=6)

        h_out = None
        h_ffn_done = None
        h_wb_used = None
        for t in range(NT):
            tsl = slice(t * TC, (t + 1) * TC)
            h_ld = prog.stage("q", (lambda tsl_: lambda q: [
                q.dma_start(out=_s3(x_sb[:], TC), in_=_r3(xs[:, tsl_], TC)),
                q.dma_start(out=_s3(y_sb[:], TC), in_=_r3(ys[:, tsl_], TC)),
            ])(tsl), deps=[h_ffn_done, h_out], ndma=2)

            def res_emit(v):
                for k in range(32):
                    c = slice(k * TC, (k + 1) * TC)
                    v.tensor_tensor(out=x_sb[:, c], in0=x_sb[:, c], in1=y_sb[:, c], op=ALU.add)
            h_res = prog.stage("v", res_emit, deps=[h_ld])

            h_rt = ln_stats(prog, sq_sb, x_sb, ones_col, st, ps_sx, ps_sq, 32, TC,
                            deps=[h_res, h_wb_used], h_ones=h_ones)
            h_bc = bcast_mr(prog, ones_row, st, ps_mb, ps_rb, TC, deps=[h_rt], h_ones=h_ones)

            def xln_emit(v):
                for k in range(32):
                    c = slice(k * TC, (k + 1) * TC)
                    v.tensor_tensor(out=sq_sb[:, c], in0=x_sb[:, c], in1=ps_mb[:], op=ALU.subtract)
                v.drain()
                for k in range(32):
                    c = slice(k * TC, (k + 1) * TC)
                    v.tensor_tensor(out=xln_sb[:, c], in0=sq_sb[:, c], in1=ps_rb[:], op=ALU.mult)
            h_xln = prog.stage("v", xln_emit, deps=[h_bc])

            def lg_emit(p):
                for k in range(32):
                    c = slice(k * TC, (k + 1) * TC)
                    p.matmul(ps_aux[0:8, :], wr_sb[:, k * 8:(k + 1) * 8], xln_sb[:, c],
                             start=(k == 0), stop=(k == 31))
            h_lg = prog.stage("p", lg_emit, deps=[h_xln, h_w])
            h_lge = prog.stage("v", lambda v: v.tensor_copy(out=lg_sb[:], in_=ps_aux[0:8, :]),
                               deps=[h_lg])

            def tr_emit(p):
                for i in range(NTI):
                    dst = (ps_mb, ps_rb)[i]
                    p.transpose(dst[:, 0:8], lg_sb[:, i * 128:(i + 1) * 128], id_sb[0:8, 0:8])
            h_tr = prog.stage("p", tr_emit, deps=[h_lge, h_xln])

            def tre_emit(v):
                for i in range(NTI):
                    src = (ps_mb, ps_rb)[i]
                    v.tensor_copy(out=lgT_sb[:, i * 8:(i + 1) * 8], in_=src[:, 0:8])
            h_tre = prog.stage("v", tre_emit, deps=[h_tr])

            lgT3 = lgT_sb[:].rearrange("p (i e) -> p i e", e=8)
            w23 = w2_sb[:].rearrange("p (i e) -> p i e", e=8)
            eq23 = t2s[:, 0:8 * NTI].rearrange("p (i e) -> p i e", e=8)
            eq2b = t2s[:, 8 * NTI:16 * NTI].rearrange("p (i e) -> p i e", e=8)
            m1 = sc_sb[:, 0 * NTI:1 * NTI]
            m2 = sc_sb[:, 1 * NTI:2 * NTI]
            dd = sc_sb[:, 2 * NTI:3 * NTI]
            e1 = sc_sb[:, 3 * NTI:4 * NTI]

            def top2a_emit(v):
                v.reduce_max(m1, lgT3, axis=mybir.AxisListType.X)
                v.drain()
                a_, b_ = bass.broadcast_tensor_aps(lgT3, m1[:, :, None])
                v.tensor_tensor(out=w23, in0=a_, in1=b_, op=ALU.is_equal)
                v.drain()
                v.scalar_tensor_tensor(out=lgT3, in0=w23, scalar=-1e30, in1=lgT3,
                                       op0=ALU.mult, op1=ALU.add)
                v.drain()
                v.reduce_max(m2, lgT3, axis=mybir.AxisListType.X)
                v.drain()
                v.tensor_tensor(out=dd, in0=m1, in1=m2, op=ALU.subtract)
            h_t2a = prog.stage("v", top2a_emit, deps=[h_tre])

            h_sg = prog.stage("s", lambda s: s.activation(out=e1, in_=dd, func=AF.Sigmoid),
                              deps=[h_t2a])

            def top2b_emit(v):
                a_, b_ = bass.broadcast_tensor_aps(lgT3, m2[:, :, None])
                v.tensor_tensor(out=eq23, in0=a_, in1=b_, op=ALU.is_equal)
                a_, b_ = bass.broadcast_tensor_aps(w23, e1[:, :, None])
                v.tensor_tensor(out=w23, in0=a_, in1=b_, op=ALU.mult)
                v.drain()
                a_, b_ = bass.broadcast_tensor_aps(eq23, e1[:, :, None])
                v.tensor_tensor(out=eq2b, in0=a_, in1=b_, op=ALU.mult)
                v.drain()
                v.tensor_tensor(out=eq23, in0=eq23, in1=eq2b, op=ALU.subtract)
                v.drain()
                v.tensor_tensor(out=w2_sb[:], in0=w2_sb[:], in1=t2s[:, 0:8 * NTI], op=ALU.add)
            h_t2b = prog.stage("v", top2b_emit, deps=[h_sg])

            def trw_emit(p):
                for i in range(NTI):
                    dst = (ps_mb, ps_rb)[i]
                    p.transpose(dst[0:8, 0:128], w2_sb[:, i * 8:(i + 1) * 8], id_sb[:, :])
            h_trw = prog.stage("p", trw_emit, deps=[h_t2b, h_tre])

            def w8_emit(v):
                for i in range(NTI):
                    src = (ps_mb, ps_rb)[i]
                    v.tensor_copy(out=w8_sb[:, i * 128:(i + 1) * 128], in_=src[0:8, 0:128])
            h_w8 = prog.stage("v", w8_emit, deps=[h_trw])

            def hall_emit(p):
                for k in range(32):
                    c = slice(k * TC, (k + 1) * TC)
                    p.matmul(ps_h[:], aa_sb[:, k * 128:(k + 1) * 128], xln_sb[:, c],
                             start=(k == 0), stop=(k == 31))
                p.matmul(ps_aux[:], e8_sb[:], w8_sb[:], start=True, stop=True)
            h_hall = prog.stage("p", hall_emit, deps=[h_xln, h_w8, h_w, h_lge])

            def g_emit(v):
                v.tensor_copy(out=wbc_sb[:], in_=ps_aux[:])
                v.drain()
                v.scalar_tensor_tensor(out=g_sb[:], in0=ps_h[:], scalar=2.0, in1=wbc_sb[:],
                                       op0=ALU.mult, op1=ALU.mult)
            h_g = prog.stage("v", g_emit, deps=[h_hall])

            h_ev = None
            h_mm = None
            h_wb = None
            for ofb in range(16):
                prev_reader = h_mm
                h_wb = prog.stage("q", (lambda ofb_: lambda q: [
                    q.dma_start(out=_s3(wb_sb[:], 256), in_=_r3(wbT[:, ofb_ * 256:(ofb_ + 1) * 256], 256)),
                ])(ofb), deps=[h_xln, prev_reader], ndma=1)

                def mm_emit(p, ofb_=ofb):
                    for j in range(2):
                        pso = ps_o0 if j == 0 else ps_o1
                        ofc = ofb_ * 2 + j
                        for k in range(32):
                            c = slice(k * TC, (k + 1) * TC)
                            p.matmul(pso[:], wb_sb[:, (k * 2 + j) * 128:(k * 2 + j + 1) * 128],
                                     xln_sb[:, c], start=(k == 0), stop=False)
                        p.matmul(pso[:], ba_sb[:, ofc * 128:(ofc + 1) * 128], g_sb[:],
                                 start=False, stop=True)
                h_mm = prog.stage("p", mm_emit, deps=[h_wb, h_g, h_ev])

                def ev_emit(v, ofb_=ofb):
                    for j in range(2):
                        pso = ps_o0 if j == 0 else ps_o1
                        ofc = ofb_ * 2 + j
                        c = slice(ofc * TC, (ofc + 1) * TC)
                        v.scalar_tensor_tensor(out=y_sb[:, c], in0=pso[:],
                                               scalar=bb_sb[:, ofc:ofc + 1],
                                               in1=x_sb[:, c], op0=ALU.add, op1=ALU.add)
                h_ev = prog.stage("v", ev_emit, deps=[h_mm, h_out])
            h_ffn_done = h_ev
            h_wb_used = h_mm

            h_out = prog.stage("g", (lambda tsl_: lambda g: [
                g.dma_start(out=_r3(outs[:, tsl_], TC), in_=_s3(y_sb[:], TC)),
            ])(tsl), deps=[h_ffn_done], ndma=1)

        _run(nc, prog, ctx, [h_out])
    return nc



def build_fused():
    nc = bass.Bass("TRN2", target_bir_lowering=False)
    xs = nc.dram_tensor("xs", [D, TSL], F32, kind="ExternalInput").ap()
    xc = nc.dram_tensor("xc", [CSL, TTOK], F32, kind="ExternalInput").ap()
    xpw_my = nc.dram_tensor("xpw_my", [128, 4 * 160], F32, kind="ExternalInput").ap()
    dtw_my = nc.dram_tensor("dtw_my", [128, CSL], F32, kind="ExternalInput").ap()
    dtb_my = nc.dram_tensor("dtb_my", [128, 4], F32, kind="ExternalInput").ap()
    alog_my = nc.dram_tensor("alog_my", [128, 64], F32, kind="ExternalInput").ap()
    dvec_my = nc.dram_tensor("dvec_my", [128, 4], F32, kind="ExternalInput").ap()
    wbT = nc.dram_tensor("wbT", [D, D], F32, kind="ExternalInput").ap()
    bb = nc.dram_tensor("bb", [128, 32], F32, kind="ExternalInput").ap()
    wrT = nc.dram_tensor("wrT", [D, 8], F32, kind="ExternalInput").ap()
    aaT = nc.dram_tensor("aaT", [D, 128], F32, kind="ExternalInput").ap()
    baT = nc.dram_tensor("baT", [128, D], F32, kind="ExternalInput").ap()
    identb = nc.dram_tensor("identb", [128, 128], BF16, kind="ExternalInput").ap()
    ident8 = nc.dram_tensor("ident8", [8, 8], F32, kind="ExternalInput").ap()
    identf = nc.dram_tensor("identf", [128, 128], F32, kind="ExternalInput").ap()
    e8m = nc.dram_tensor("e8m", [8, 128], F32, kind="ExternalInput").ap()
    sel32 = nc.dram_tensor("sel32", [32, 32 * 128], BF16, kind="ExternalInput").ap()
    outs = nc.dram_tensor("outs", [D, TSL], F32, kind="ExternalOutput").ap()
    ar_in = nc.dram_tensor("ar_in", [163, TTOK], F32, kind="Internal").ap()
    ar_out = nc.dram_tensor("ar_out", [163, TTOK], F32, kind="Internal").ap()
    mrows = nc.dram_tensor("mrows", [3, TTOK], F32, kind="Internal").ap()
    ylocal = nc.dram_tensor("ylocal", [8 * CSL, TSL], F32, kind="Internal").ap()
    ygath = nc.dram_tensor("ygath", [8 * D, TSL], F32, kind="Internal").ap()
    wbh = nc.dram_tensor("wbh", [D, D], BF16, kind="Internal").ap()
    aah = nc.dram_tensor("aah", [D, 128], BF16, kind="Internal").ap()
    bah = nc.dram_tensor("bah", [128, D], BF16, kind="Internal").ap()
    wrh = nc.dram_tensor("wrh", [D, 8], BF16, kind="Internal").ap()

    RG = [[0, 1, 2, 3, 4, 5, 6, 7]]
    prog = Prog()
    ncc = {"c": 0}

    with ExitStack() as ctx:
        sb = lambda n_, sh, dt=F32: ctx.enter_context(nc.sbuf_tensor(n_, sh, dt))
        xpw_sb = sb("xpw_sb", [128, 4 * 160])
        dtw_sb = sb("dtw_sb", [128, CSL])
        dtwb_sb = sb("dtwb_sb", [128, CSL], BF16)
        dtb_sb = sb("dtb_sb", [128, 4])
        a_sb = sb("a_sb", [128, 64])
        d_sb = sb("d_sb", [128, 4])
        carry = sb("carry", [128, 64])
        wr_sb = sb("wr_sb", [128, 32 * 8], BF16)
        bb_sb = sb("bb_sb", [128, 32])
        idb_sb = sb("idb_sb", [128, 128], BF16)
        id8_sb = sb("id8_sb", [8, 8])
        idf_sb = sb("idf_sb", [128, 128])
        e8_sb = sb("e8_sb", [8, 128])
        sel_sb = sb("sel_sb", [32, 32 * 128], BF16)
        ones_col = sb("ones_col", [128, 1])
        ones_row = sb("ones_row", [1, 128])
        csd_sb = sb("csd_sb", [128, 1])
        csb_sb = sb("csb_sb", [32, 1])
        stt_a = sb("stt_a", [128, 128])
        stt_b = sb("stt_b", [128, 128])
        stt_c = sb("stt_c", [128, 128])
        stt_d = sb("stt_d", [128, 128])
        zrow = sb("zrow", [1, 2048])
        st = sb("st", [1, 8 * 256])
        cast_in = sb("cast_in", [128, 2048])
        cast_out = sb("cast_out", [128, 2048], BF16)
        ARN = 36864
        arena = sb("arena", [128, ARN])
        apos = {"o": 0}

        def aslc(cols, parts=128, dt=F32):
            w = cols if dt == F32 else (cols + 1) // 2
            o = apos["o"]
            apos["o"] = o + w
            assert apos["o"] <= ARN, (o, w)
            if dt == F32:
                return arena[0:parts, o:o + cols]
            return arena[0:parts, o:o + w].bitcast(dt)[0:parts, 0:cols]

        pbank = [ctx.enter_context(nc.psum_tensor(f"pb{i}", [128, 512], F32)) for i in range(8)]

        h_ones = prog.stage("v", lambda v: (v.memset(ones_col[:], 1.0),
                                            v.memset(ones_row[:], 1.0),
                                            v.memset(carry[:], 0.0),
                                            v.memset(zrow[:], 0.0)))
        cidbox = {}

        def w0_emit(q):
            cidbox["cid"] = q.partition_id()
            return [
                q.dma_start(out=xpw_sb[:], in_=xpw_my),
                q.dma_start(out=dtw_sb[:], in_=dtw_my),
                q.dma_start(out=dtb_sb[:], in_=dtb_my),
                q.dma_start(out=a_sb[:], in_=alog_my),
                q.dma_start(out=d_sb[:], in_=dvec_my),
                q.dma_start(out=bb_sb[:], in_=bb),
                q.dma_start(out=idb_sb[:], in_=identb),
                q.dma_start(out=id8_sb[:], in_=ident8),
                q.dma_start(out=idf_sb[:], in_=identf),
                q.dma_start(out=e8_sb[:], in_=e8m),
                q.dma_start(out=sel_sb[:], in_=sel32),
            ]
        h_w0 = prog.stage("q", w0_emit, ndma=11)
        h_a = prog.stage("s", lambda s: (s.activation(out=a_sb[:], in_=a_sb[:], func=AF.Exp),
                                         s.drain(),
                                         s.mul(out=a_sb[:], in_=a_sb[:], mul=-1.0)),
                         deps=[h_w0])
        h_dtwb = prog.stage("v", lambda v: v.tensor_copy(out=dtwb_sb[:], in_=dtw_sb[:]),
                            deps=[h_w0])

        def emit_cast(srcT, dstT, col0, ncols, dep, gdep):
            h1 = prog.stage("q", (lambda: lambda q: [
                q.dma_start(out=cast_in[:, 0:32 * ncols].rearrange("p (k c) -> p k c", c=ncols),
                            in_=_r3(srcT[:, col0:col0 + ncols], ncols)),
            ])(), deps=[dep], ndma=1)
            h2 = prog.stage("s", (lambda: lambda s: s.copy(
                out=cast_out[:, 0:32 * ncols], in_=cast_in[:, 0:32 * ncols]))(),
                deps=[h1, gdep])
            h3 = prog.stage("g", (lambda: lambda g: [
                g.dma_start(out=_r3(dstT[:, col0:col0 + ncols], ncols),
                            in_=cast_out[:, 0:32 * ncols].rearrange("p (k c) -> p k c", c=ncols)),
            ])(), deps=[h2], ndma=1)
            return h3

        h_cast = None
        for cb in range(64):
            h_cast = emit_cast(wbT, wbh, cb * 64, 64, h_cast, h_cast)
        for cb in range(2):
            h_cast = emit_cast(aaT, aah, cb * 64, 64, h_cast, h_cast)
        h_cast = emit_cast(wrT, wrh, 0, 8, h_cast, h_cast)
        for half in range(2):
            h1 = prog.stage("q", (lambda half_: lambda q: [
                q.dma_start(out=cast_in[:], in_=baT[:, half_ * 2048:(half_ + 1) * 2048])])(half),
                deps=[h_cast], ndma=1)
            h2 = prog.stage("s", lambda s: s.copy(out=cast_out[:], in_=cast_in[:]), deps=[h1])
            h_cast = prog.stage("g", (lambda half_: lambda g: [
                g.dma_start(out=bah[:, half_ * 2048:(half_ + 1) * 2048], in_=cast_out[:])])(half),
                deps=[h2], ndma=1)
        h_castall = h_cast

        TA = 2048
        NA = TTOK // TA
        xa = aslc(4 * TA)
        sqa = aslc(4 * TA)
        xpd_st = aslc(TA)
        xpbc_st = aslc(TA, 32)
        srow_x = aslc(TA, 1)
        srow_q = aslc(TA, 1)

        def cs_emit(p):
            for k in range(4):
                p.matmul(pbank[6][:, 0:1], xpw_sb[:, k * 160:k * 160 + 128], ones_col[:],
                         start=(k == 0), stop=(k == 3))
            for k in range(4):
                p.matmul(pbank[7][0:32, 0:1], xpw_sb[:, k * 160 + 128:(k + 1) * 160], ones_col[:],
                         start=(k == 0), stop=(k == 3))
        h_cs = prog.stage("p", cs_emit, deps=[h_w0, h_ones])
        h_cse = prog.stage("v", lambda v: (v.tensor_copy(out=csd_sb[:], in_=pbank[6][:, 0:1]),
                                           v.tensor_copy(out=csb_sb[:], in_=pbank[7][0:32, 0:1])),
                           deps=[h_cs])
        h_z = prog.stage("g", lambda g: [
            g.dma_start(out=ar_in[162:163, k * 2048:(k + 1) * 2048], in_=zrow[:])
            for k in range(8)], deps=[h_ones], ndma=8)
        h_csw = prog.stage("g", lambda g: [
            g.dma_start(out=ar_in[162:163, 0:128].rearrange("one (p c) -> (one p) c", c=1), in_=csd_sb[:]),
            g.dma_start(out=ar_in[162:163, 128:160].rearrange("one (p c) -> (one p) c", c=1), in_=csb_sb[0:32, :]),
        ], deps=[h_cse, h_z], ndma=2)

        h_xa_prev = None
        h_arw = h_csw
        for t in range(NA):
            tsl = slice(t * TA, (t + 1) * TA)
            h_xa = prog.stage("q", (lambda tsl_: lambda q: [
                q.dma_start(out=xa.rearrange("p (k c) -> p k c", c=TA),
                            in_=xc[:, tsl_].rearrange("(k p) c -> p k c", p=128)),
            ])(tsl), deps=[h_xa_prev], ndma=1)
            h_sq = prog.stage("s", (lambda: lambda s: s.activation(out=sqa, in_=xa, func=AF.Square))(),
                              deps=[h_xa])
            h_pa = None
            h_ve = None
            h_ve2 = None
            for u in range(4):
                def pa_u(p, u_=u):
                    b0, b1, b2, b3 = (0, 1, 2, 3) if u_ % 2 == 0 else (4, 5, 6, 7)
                    for k in range(4):
                        kc = slice(k * TA + u_ * 512, k * TA + (u_ + 1) * 512)
                        p.matmul(pbank[b0][0:1, :], ones_col[:], xa[:, kc],
                                 start=(k == 0), stop=(k == 3))
                    for k in range(4):
                        kc = slice(k * TA + u_ * 512, k * TA + (u_ + 1) * 512)
                        p.matmul(pbank[b1][0:1, :], ones_col[:], sqa[:, kc],
                                 start=(k == 0), stop=(k == 3))
                    for k in range(4):
                        kc = slice(k * TA + u_ * 512, k * TA + (u_ + 1) * 512)
                        p.matmul(pbank[b2][:, :], xpw_sb[:, k * 160:k * 160 + 128], xa[:, kc],
                                 start=(k == 0), stop=(k == 3))
                    for k in range(4):
                        kc = slice(k * TA + u_ * 512, k * TA + (u_ + 1) * 512)
                        p.matmul(pbank[b3][0:32, :], xpw_sb[:, k * 160 + 128:(k + 1) * 160], xa[:, kc],
                                 start=(k == 0), stop=(k == 3))
                prev2 = h_ve2
                h_pa = prog.stage("p", pa_u, deps=[h_sq, h_xa, prev2, h_cse])

                def ve_u(v, u_=u):
                    b0, b1, b2, b3 = (0, 1, 2, 3) if u_ % 2 == 0 else (4, 5, 6, 7)
                    cs = slice(u_ * 512, (u_ + 1) * 512)
                    v.tensor_copy(out=srow_x[:, cs], in_=pbank[b0][0:1, :])
                    v.tensor_copy(out=srow_q[:, cs], in_=pbank[b1][0:1, :])
                    v.tensor_copy(out=xpd_st[:, cs], in_=pbank[b2][:, :])
                    v.tensor_copy(out=xpbc_st[:, cs], in_=pbank[b3][0:32, :])
                h_ve2 = h_ve
                h_ve = prog.stage("v", (ve_u), deps=[h_pa, h_arw])
            h_arw = prog.stage("g", (lambda tsl_: lambda g: [
                g.dma_start(out=ar_in[0:128, tsl_], in_=xpd_st),
                g.dma_start(out=ar_in[128:160, tsl_], in_=xpbc_st),
                g.dma_start(out=ar_in[160:161, tsl_], in_=srow_x),
                g.dma_start(out=ar_in[161:162, tsl_], in_=srow_q),
            ])(tsl), deps=[h_ve], ndma=4)
            h_xa_prev = h_ve

        csem = ctx.enter_context(nc.semaphore("sem_c"))

        def ar_emit(g):
            g.collective_compute("AllReduce", ALU.add, replica_groups=RG,
                                 ins=[ar_in], outs=[ar_out]).then_inc(csem, 1)
        h_arq = prog.stage("g", ar_emit, deps=[h_arw, h_csw], raw=True)
        ncc["c"] += 1
        AR_DONE = ("c", ncc["c"])

        h_strd = prog.stage("q", lambda q: [
            q.dma_start(out=stt_a[:], in_=ar_out[160:161, :].rearrange("one (p c) -> (one p) c", c=128)),
            q.dma_start(out=stt_b[:], in_=ar_out[161:162, :].rearrange("one (p c) -> (one p) c", c=128)),
            q.dma_start(out=csd_sb[:], in_=ar_out[162:163, 0:128].rearrange("one (p c) -> (one p) c", c=1)),
            q.dma_start(out=csb_sb[0:32, :], in_=ar_out[162:163, 128:160].rearrange("one (p c) -> (one p) c", c=1)),
        ], deps=[AR_DONE, h_cse], ndma=4)

        def stats_emit(v):
            v.tensor_scalar_mul(stt_a[:], stt_a[:], 1.0 / D)
            v.tensor_scalar(stt_b[:], stt_b[:], 1.0 / D, EPS, op0=ALU.mult, op1=ALU.add)
            v.drain()
            v.tensor_tensor(out=stt_c[:], in0=stt_a[:], in1=stt_a[:], op=ALU.mult)
            v.drain()
            v.tensor_tensor(out=stt_c[:], in0=stt_b[:], in1=stt_c[:], op=ALU.subtract)
            v.drain()
            v.reciprocal(out=stt_d[:], in_=stt_c[:])
        h_st1 = prog.stage("v", stats_emit, deps=[h_strd])
        h_st2 = prog.stage("s", lambda s: s.activation(out=stt_b[:], in_=stt_d[:], func=AF.Sqrt),
                           deps=[h_st1])
        h_st3 = prog.stage("v", lambda v: v.tensor_tensor(out=stt_c[:], in0=stt_a[:], in1=stt_b[:],
                                                          op=ALU.mult), deps=[h_st2])
        h_mw = prog.stage("g", lambda g: [
            g.dma_start(out=mrows[0:1, :].rearrange("one (p c) -> (one p) c", c=128), in_=stt_a[:]),
            g.dma_start(out=mrows[1:2, :].rearrange("one (p c) -> (one p) c", c=128), in_=stt_b[:]),
            g.dma_start(out=mrows[2:3, :].rearrange("one (p c) -> (one p) c", c=128), in_=stt_c[:]),
        ], deps=[h_st3], ndma=3)

        apos["o"] = 0
        TB = 512
        NB = TTOK // TB
        NCH = 4
        xbp = [[aslc(TB) for _ in range(NCH)] for _ in range(2)]
        dtp = [[aslc(TB) for _ in range(NCH)] for _ in range(2)]
        dtxp = [[aslc(TB, 128, BF16) for _ in range(NCH)] for _ in range(2)]
        xlnb = [aslc(TB) for _ in range(NCH)]
        seedb = [aslc(TB) for _ in range(NCH)]
        dAn = [[aslc(TB) for _ in range(NCH)] for _ in range(2)]
        dBxb = [aslc(TB, 128, BF16) for _ in range(NCH)]
        hbf = [aslc(TB, 128, BF16) for _ in range(NCH)]
        bcb_sb = [[aslc(TB, 128, BF16) for _ in range(2)] for _ in range(2)]
        qbf = [[aslc(TB, 128, BF16) for _ in range(NCH)] for _ in range(2)]
        xpdtp = [aslc(TB) for _ in range(2)]
        xpdcp = [aslc(TB, 128, BF16) for _ in range(2)]
        xpbtp = [aslc(TB, 32) for _ in range(2)]
        xpbcp = [aslc(TB, 32, BF16) for _ in range(2)]
        t1s = aslc(TB)
        t2s_ = aslc(TB)
        mrowp = [aslc(TB, 1) for _ in range(2)]
        rrowp = [aslc(TB, 1) for _ in range(2)]
        mrrowp = [aslc(TB, 1) for _ in range(2)]

        h_scan_hist = []
        h_yacc_hist = []
        h_scan_done = {}
        h_prep_done = {}
        h_dt_done = {}
        h_gy = {-1: h_mw}
        h_yev = {-1: None}
        h_corr = None
        h_sp = None
        ag_handles = []

        def sh(idx):
            return h_scan_hist[idx] if 0 <= idx < len(h_scan_hist) else None

        def yh(idx):
            return h_yacc_hist[idx] if 0 <= idx < len(h_yacc_hist) else None

        for t in range(NB):
            par = t % 2
            tsl = slice(t * TB, (t + 1) * TB)
            h_ld = prog.stage("q", (lambda tsl_, par_: lambda q: [
                q.dma_start(out=xbp[par_][i], in_=xc[128 * i:128 * (i + 1), tsl_])
                for i in range(NCH)] + [
                q.dma_start(out=xpdtp[par_], in_=ar_out[0:128, tsl_]),
                q.dma_start(out=xpbtp[par_], in_=ar_out[128:160, tsl_]),
                q.dma_start(out=mrowp[par_][:], in_=mrows[0:1, tsl_]),
                q.dma_start(out=rrowp[par_][:], in_=mrows[1:2, tsl_]),
                q.dma_start(out=mrrowp[par_][:], in_=mrows[2:3, tsl_]),
            ])(tsl, par), deps=[h_prep_done.get(t - 2), h_scan_done.get(t - 2),
                                h_dt_done.get(t - 2), AR_DONE, h_mw], ndma=NCH + 5)

            def bcrm_emit(p, par_=par):
                p.matmul(pbank[6][:, :], ones_row[:], rrowp[par_][:], start=True, stop=True)
                p.matmul(pbank[7][:, :], ones_row[:], mrrowp[par_][:], start=True, stop=True)
            h_bcrm = prog.stage("p", bcrm_emit, deps=[h_ld, h_scan_done.get(t - 1), h_ones])

            def corr_emit(v, par_=par):
                v.tensor_tensor(out=t1s, in0=xpdtp[par_], in1=pbank[6][:, :], op=ALU.mult)
                v.tensor_scalar_mul(t2s_, pbank[7][:, :], csd_sb[:, 0:1])
                v.drain()
                v.tensor_tensor(out=xpdcp[par_], in0=t1s, in1=t2s_, op=ALU.subtract)
                v.drain()
                v.tensor_tensor(out=t1s[0:32, :], in0=xpbtp[par_], in1=pbank[6][0:32, :], op=ALU.mult)
                v.tensor_scalar_mul(t2s_[0:32, :], pbank[7][0:32, :], csb_sb[:, 0:1])
                v.drain()
                v.tensor_tensor(out=xpbcp[par_], in0=t1s[0:32, :], in1=t2s_[0:32, :], op=ALU.subtract)
            h_corr = prog.stage("v", corr_emit, deps=[h_bcrm, h_cse])

            h_pdp = None
            h_sp = None
            for i in range(NCH):
                def pd3(p, i_=i, par_=par):
                    bk = pbank[6] if i_ % 2 == 0 else pbank[7]
                    p.matmul(bk[:, :], dtwb_sb[:, i_ * 128:(i_ + 1) * 128], xpdcp[par_],
                             start=True, stop=True)
                h_pdp2 = prog.stage("p", pd3, deps=[h_corr, h_sp, h_dtwb])

                def sp3(s, i_=i, par_=par):
                    bk = pbank[6] if i_ % 2 == 0 else pbank[7]
                    s.activation(out=dtp[par_][i_], in_=bk[:, :], func=AF.Exp,
                                 bias=dtb_sb[:, i_:i_ + 1])
                    s.drain()
                    s.activation(out=dtp[par_][i_], in_=dtp[par_][i_], func=AF.Ln, bias=1.0)
                h_sp = prog.stage("s", sp3, deps=[h_pdp, h_pdp2])
                h_pdp = h_pdp2
            h_dt_done[t] = h_sp

            def bcmr_emit(p, par_=par):
                p.matmul(pbank[4][:, :], ones_row[:], mrowp[par_][:], start=True, stop=True)
                p.matmul(pbank[5][:, :], ones_row[:], rrowp[par_][:], start=True, stop=True)
            h_bcmr = prog.stage("p", bcmr_emit, deps=[h_ld, h_scan_done.get(t - 1), h_ones])

            def xln_emit(v, par_=par):
                for i in range(NCH):
                    v.tensor_tensor(out=dAn[0][i], in0=xbp[par_][i], in1=pbank[4][:, :], op=ALU.subtract)
                v.drain()
                for i in range(NCH):
                    v.tensor_tensor(out=xlnb[i], in0=dAn[0][i], in1=pbank[5][:, :], op=ALU.mult)
            h_xln = prog.stage("v", xln_emit, deps=[h_bcmr, h_scan_done.get(t - 1),
                                                    h_yev.get(t - 1)])

            def prep_emit(v, par_=par):
                for i in range(NCH):
                    v.tensor_tensor(out=dtxp[par_][i], in0=dtp[par_][i], in1=xlnb[i], op=ALU.mult)
            h_prep = prog.stage("v", prep_emit, deps=[h_xln, h_sp])
            h_prep_done[t] = h_prep

            def seed_emit(s, par_=par):
                for i in range(NCH):
                    s.activation(out=seedb[i], in_=xlnb[i], func=AF.Copy, scale=d_sb[:, i:i + 1])
            h_seed = prog.stage("s", seed_emit, deps=[h_xln, h_sp, h_gy.get(t - 1), h_w0])

            h_yacc = None
            h_sev = None
            base = len(h_scan_hist)
            for n in range(16):
                np2 = n % 2
                bB, bC = (4, 5) if np2 == 0 else (6, 7)

                def bcb_emit(p, n_=n, bB_=bB, bC_=bC, par_=par):
                    p.matmul(pbank[bB_][:, :], sel_sb[:, n_ * 128:(n_ + 1) * 128],
                             xpbcp[par_], start=True, stop=True)
                    p.matmul(pbank[bC_][:, :], sel_sb[:, (16 + n_) * 128:(17 + n_) * 128],
                             xpbcp[par_], start=True, stop=True)
                bdeps = [h_corr, sh(base + n - 2), h_w0]
                if np2 == 0:
                    bdeps.append(h_xln if n == 0 else None)
                else:
                    bdeps.append(h_dt_done[t])
                h_pbB = prog.stage("p", bcb_emit, deps=bdeps)

                def sev_emit(s, np2_=np2, bB_=bB, bC_=bC):
                    s.activation(out=bcb_sb[np2_][0], in_=pbank[bB_][:, :], func=AF.Copy)
                    s.activation(out=bcb_sb[np2_][1], in_=pbank[bC_][:, :], func=AF.Copy)
                h_sev = prog.stage("s", sev_emit, deps=[h_pbB, sh(base + n - 2)])

                def dA_emit(s, n_=n, par_=par, np2_=np2):
                    for i in range(NCH):
                        s.activation(out=dAn[np2_][i], in_=dtp[par_][i], func=AF.Exp,
                                     scale=a_sb[:, i * 16 + n_:i * 16 + n_ + 1])
                h_sn = prog.stage("s", dA_emit, deps=[h_sp, h_a, sh(base + n - 2), h_seed])

                def scan_emit(v, n_=n, t_=t, par_=par, np2_=np2, bB_=bB, bC_=bC):
                    for i in range(NCH):
                        v.tensor_tensor(out=dBxb[i], in0=dtxp[par_][i], in1=bcb_sb[np2_][0],
                                        op=ALU.mult)
                    if t_ == NB // 2:
                        for i in range(NCH):
                            v.memset(carry[:, i * 16 + n_:i * 16 + n_ + 1], 0.0)
                    v.drain()
                    for i in range(NCH):
                        col = slice(i * 16 + n_, i * 16 + n_ + 1)
                        v.tensor_tensor_scan(out=hbf[i], data0=dAn[np2_][i], data1=dBxb[i],
                                             initial=carry[:, col], op0=ALU.mult, op1=ALU.add)
                    v.drain()
                    for i in range(NCH):
                        col = slice(i * 16 + n_, i * 16 + n_ + 1)
                        v.tensor_copy(out=carry[:, col], in_=hbf[i][:, TB - 1:TB])
                    for i in range(NCH):
                        v.tensor_tensor(out=qbf[np2_][i], in0=hbf[i], in1=bcb_sb[np2_][1],
                                        op=ALU.mult)
                sdeps = [h_pbB, h_sn, h_sev, h_prep, yh(base + n - 2)]
                if n == 0:
                    sdeps.append(h_seed)
                h_vq = prog.stage("v", scan_emit, deps=sdeps)
                h_scan_hist.append(h_vq)

                def yacc_emit(p, n_=n, np2_=np2):
                    for i in range(NCH):
                        p.matmul(pbank[i][:, :], idb_sb[:], qbf[np2_][i],
                                 start=(n_ == 0), stop=(n_ == 15))
                h_yacc = prog.stage("p", yacc_emit, deps=[h_vq, h_yev.get(t - 1), h_w0])
                h_yacc_hist.append(h_yacc)
            h_scan_done[t] = h_vq

            def yev_emit(v):
                for i in range(NCH):
                    v.tensor_tensor(out=seedb[i], in0=pbank[i][:, :], in1=seedb[i], op=ALU.add)
            h_yev[t] = prog.stage("v", yev_emit, deps=[h_yacc, h_seed])

            j = t // 4
            co = (t % 4) * TB
            h_gy[t] = prog.stage("g", (lambda j_, co_: lambda g: [
                g.dma_start(out=ylocal[CSL * j_ + 128 * i:CSL * j_ + 128 * (i + 1), co_:co_ + TB],
                            in_=seedb[i]) for i in range(NCH)])(j, co),
                deps=[h_yev[t]], ndma=NCH)

            if t % 4 == 3:
                agdep = ag_handles[-1] if ag_handles else None

                def ag_emit(g, j_=j):
                    g.collective_compute("AllGather", ALU.bypass, replica_groups=RG,
                                         ins=[ylocal[CSL * j_:CSL * (j_ + 1), :]],
                                         outs=[ygath[D * j_:D * (j_ + 1), :]]).then_inc(csem, 1)
                prog.stage("g", ag_emit, deps=[h_gy[t], agdep], raw=True)
                ncc["c"] += 1
                ag_handles.append(("c", ncc["c"]))

        AG_ALL = ag_handles[-1]
        B_TAIL = [h_scan_done[NB - 1], h_prep_done[NB - 1], h_dt_done[NB - 1],
                  h_yev[NB - 1], h_gy[NB - 1]]

        apos["o"] = 0
        TC3 = 512
        NC3 = TSL // TC3
        NTI = TC3 // 128
        x3 = aslc(32 * TC3)
        xln3 = aslc(32 * TC3, 128, BF16)
        sqpp = [aslc(1024) for _ in range(2)]
        lg3 = sqpp[1][0:8, 0:512]
        wbc3 = sqpp[0][:, 0:512]
        wb3p = [aslc(32 * 128, 128, BF16) for _ in range(2)]
        aa3 = aslc(32 * 128, 128, BF16)
        ba3 = aslc(D, 128, BF16)
        evb = [aslc(TC3) for _ in range(2)]
        lgT3t = aslc(8 * NTI)
        w23t = aslc(8 * NTI)
        t2s3 = aslc(16 * NTI)
        sc3 = aslc(4 * NTI)
        w83 = aslc(TC3, 8)
        g3 = aslc(TC3, 128, BF16)

        ps_sx, ps_sq, ps_mb, ps_rb, ps_aux, ps_h, ps_o0, ps_o1 = pbank

        def w2_emit(q):
            return [
                q.dma_start(out=aa3.rearrange("p (k m) -> p k m", m=128), in_=_r3(aah, 128)),
                q.dma_start(out=ba3, in_=bah),
                q.dma_start(out=wr_sb[:].rearrange("p (k m) -> p k m", m=8), in_=_r3(wrh, 8)),
            ]
        h_w2 = prog.stage("q", w2_emit, deps=[AG_ALL, h_castall] + B_TAIL, ndma=3)

        ygv = ygath.rearrange("(b p) c -> p b c", p=128)
        h_out = {-1: None}
        h_ffn_done = None
        h_acc = None
        h_wb_used = [None, None]
        h_ev_hist = {}
        h_go_hist = {}
        for t in range(NC3):
            tsl = slice(t * TC3, (t + 1) * TC3)

            def ld3(q, tsl_=tsl):
                return [q.dma_start(out=_s3(x3, TC3), in_=_r3(xs[:, tsl_], TC3))]
            h_ld = prog.stage("q", ld3, deps=[h_ffn_done, h_out[t - 1], h_w2], ndma=1)

            def acc3(g, tsl_=tsl):
                if "cid_g" not in cidbox:
                    cidbox["cid_g"] = g.partition_id()
                cid = cidbox["cid_g"]
                return [g.dma_start(out=y3v, in_=ygv[:, bass.ts(cid, 32), tsl_],
                                    accum_op=ALU.add)
                        for y3v in [x3.rearrange("p (k c) -> p k c", c=TC3)]]
            h_acc = prog.stage("g", acc3, deps=[h_ld, AG_ALL, h_out[t - 1]], ndma=1)

            h_sqs = None
            h_sums = None
            h_sums2 = None
            for jp in range(16):
                def sq_emit(s, jp_=jp):
                    c = slice(jp_ * 2 * TC3, (jp_ * 2 + 2) * TC3)
                    s.activation(out=sqpp[jp_ % 2][:, 0:1024], in_=x3[:, c], func=AF.Square)
                h_sqs = prog.stage("s", sq_emit, deps=[h_acc, h_sums2, h_out[t - 1]])

                def sum_emit(p, jp_=jp):
                    for kk in range(2):
                        k = jp_ * 2 + kk
                        c = slice(k * TC3, (k + 1) * TC3)
                        p.matmul(ps_sx[0:1, :], ones_col[:], x3[:, c],
                                 start=(k == 0), stop=(k == 31))
                        p.matmul(ps_sq[0:1, :], ones_col[:],
                                 sqpp[jp_ % 2][:, kk * TC3:(kk + 1) * TC3],
                                 start=(k == 0), stop=(k == 31))
                h_sums2 = h_sums
                h_sums = prog.stage("p", sum_emit,
                                    deps=[h_sqs, h_ones, h_sums2] + h_wb_used)
            h_sum3 = h_sums

            def st31(v):
                v.tensor_scalar_mul(st[:, 0:512], ps_sx[0:1, :], 1.0 / D)
                v.tensor_scalar(st[:, 512:1024], ps_sq[0:1, :], 1.0 / D, EPS,
                                op0=ALU.mult, op1=ALU.add)
                v.drain()
                v.tensor_tensor(out=st[:, 1024:1536], in0=st[:, 0:512],
                                in1=st[:, 0:512], op=ALU.mult)
                v.drain()
                v.tensor_tensor(out=st[:, 1024:1536], in0=st[:, 512:1024],
                                in1=st[:, 1024:1536], op=ALU.subtract)
                v.drain()
                v.reciprocal(out=st[:, 512:1024], in_=st[:, 1024:1536])
            h_st31 = prog.stage("v", st31, deps=[h_sum3])
            h_st32 = prog.stage("s", lambda s: s.activation(out=st[:, 1536:2048],
                                                            in_=st[:, 512:1024], func=AF.Sqrt),
                                deps=[h_st31])

            def bc3(p):
                p.matmul(ps_mb[:, :], ones_row[:], st[0:1, 0:512], start=True, stop=True)
                p.matmul(ps_rb[:, :], ones_row[:], st[0:1, 1536:2048], start=True, stop=True)
            h_bc3 = prog.stage("p", bc3, deps=[h_st32, h_ones])

            def xln3_emit(v):
                for k in range(32):
                    c = slice(k * TC3, (k + 1) * TC3)
                    v.tensor_tensor(out=xln3[:, c], in0=x3[:, c], in1=ps_mb[:, :], op=ALU.subtract)
                v.drain()
                for k in range(32):
                    c = slice(k * TC3, (k + 1) * TC3)
                    v.tensor_tensor(out=xln3[:, c], in0=xln3[:, c], in1=ps_rb[:, :], op=ALU.mult)
            h_xln3 = prog.stage("v", xln3_emit, deps=[h_bc3])

            def lg_emit(p):
                for k in range(32):
                    c = slice(k * TC3, (k + 1) * TC3)
                    p.matmul(ps_aux[0:8, :], wr_sb[:, k * 8:(k + 1) * 8], xln3[:, c],
                             start=(k == 0), stop=(k == 31))
            h_lg = prog.stage("p", lg_emit, deps=[h_xln3, h_w2])
            h_lge = prog.stage("v", lambda v: v.tensor_copy(out=lg3, in_=ps_aux[0:8, :]),
                               deps=[h_lg])

            trb = (ps_mb, ps_rb, ps_o0, ps_o1)

            def tr_emit(p):
                for i in range(NTI):
                    p.transpose(trb[i][:, 0:8], lg3[:, i * 128:(i + 1) * 128], id8_sb[:])
            h_tr = prog.stage("p", tr_emit, deps=[h_lge, h_xln3] + h_wb_used)

            def tre_emit(v):
                for i in range(NTI):
                    v.tensor_copy(out=lgT3t[:, i * 8:(i + 1) * 8], in_=trb[i][:, 0:8])
            h_tre = prog.stage("v", tre_emit, deps=[h_tr])

            lgT3v = lgT3t.rearrange("p (i e) -> p i e", e=8)
            w23v = w23t.rearrange("p (i e) -> p i e", e=8)
            eq23 = t2s3[:, 0:8 * NTI].rearrange("p (i e) -> p i e", e=8)
            eq2b = t2s3[:, 8 * NTI:16 * NTI].rearrange("p (i e) -> p i e", e=8)
            m1 = sc3[:, 0 * NTI:1 * NTI]
            m2 = sc3[:, 1 * NTI:2 * NTI]
            dd = sc3[:, 2 * NTI:3 * NTI]
            e1 = sc3[:, 3 * NTI:4 * NTI]

            def top2a_emit(v):
                v.reduce_max(m1, lgT3v, axis=mybir.AxisListType.X)
                v.drain()
                a_, b_ = bass.broadcast_tensor_aps(lgT3v, m1[:, :, None])
                v.tensor_tensor(out=w23v, in0=a_, in1=b_, op=ALU.is_equal)
                v.drain()
                v.scalar_tensor_tensor(out=lgT3v, in0=w23v, scalar=-1e30, in1=lgT3v,
                                       op0=ALU.mult, op1=ALU.add)
                v.drain()
                v.reduce_max(m2, lgT3v, axis=mybir.AxisListType.X)
                v.drain()
                v.tensor_tensor(out=dd, in0=m1, in1=m2, op=ALU.subtract)
            h_t2a = prog.stage("v", top2a_emit, deps=[h_tre])

            h_sg = prog.stage("s", lambda s: s.activation(out=e1, in_=dd, func=AF.Sigmoid),
                              deps=[h_t2a])

            def top2b_emit(v):
                a_, b_ = bass.broadcast_tensor_aps(lgT3v, m2[:, :, None])
                v.tensor_tensor(out=eq23, in0=a_, in1=b_, op=ALU.is_equal)
                a_, b_ = bass.broadcast_tensor_aps(w23v, e1[:, :, None])
                v.tensor_tensor(out=w23v, in0=a_, in1=b_, op=ALU.mult)
                v.drain()
                a_, b_ = bass.broadcast_tensor_aps(eq23, e1[:, :, None])
                v.tensor_tensor(out=eq2b, in0=a_, in1=b_, op=ALU.mult)
                v.drain()
                v.tensor_tensor(out=eq23, in0=eq23, in1=eq2b, op=ALU.subtract)
                v.drain()
                v.tensor_tensor(out=w23t, in0=w23t, in1=t2s3[:, 0:8 * NTI], op=ALU.add)
            h_t2b = prog.stage("v", top2b_emit, deps=[h_sg])

            def trw_emit(p):
                for i in range(NTI):
                    p.transpose(trb[i][0:8, 0:128], w23t[:, i * 8:(i + 1) * 8], idf_sb[:])
            h_trw = prog.stage("p", trw_emit, deps=[h_t2b, h_tre])

            def w8_emit(v):
                for i in range(NTI):
                    v.tensor_copy(out=w83[:, i * 128:(i + 1) * 128], in_=trb[i][0:8, 0:128])
            h_w8 = prog.stage("v", w8_emit, deps=[h_trw])

            def hall_emit(p):
                for k in range(32):
                    c = slice(k * TC3, (k + 1) * TC3)
                    p.matmul(ps_h[:, :], aa3[:, k * 128:(k + 1) * 128], xln3[:, c],
                             start=(k == 0), stop=(k == 31))
                p.matmul(ps_aux[:, :], e8_sb[:], w83, start=True, stop=True)
            h_hall = prog.stage("p", hall_emit, deps=[h_xln3, h_w8, h_w2, h_lge])

            def g_emit(v):
                v.tensor_copy(out=wbc3, in_=ps_aux[:, :])
                v.drain()
                v.scalar_tensor_tensor(out=g3, in0=ps_h[:, :], scalar=2.0, in1=wbc3,
                                       op0=ALU.mult, op1=ALU.mult)
            h_g = prog.stage("v", g_emit, deps=[h_hall])

            h_ev = None
            h_mm = None
            for ofb in range(32):
                wpar = ofb % 2
                h_wb = prog.stage("q", (lambda ofb_, wpar_: lambda q: [
                    q.dma_start(out=wb3p[wpar_].rearrange("p (k c) -> p k c", c=128),
                                in_=_r3(wbh[:, ofb_ * 128:(ofb_ + 1) * 128], 128)),
                ])(ofb, wpar), deps=[h_xln3, h_wb_used[wpar], h_castall], ndma=1)

                def mm_emit(p, ofb_=ofb, wpar_=wpar):
                    pso = ps_o0 if ofb_ % 2 == 0 else ps_o1
                    for k in range(32):
                        c = slice(k * TC3, (k + 1) * TC3)
                        p.matmul(pso[:, :], wb3p[wpar_][:, k * 128:(k + 1) * 128],
                                 xln3[:, c], start=(k == 0), stop=False)
                    p.matmul(pso[:, :], ba3[:, ofb_ * 128:(ofb_ + 1) * 128], g3,
                             start=False, stop=True)
                h_mm = prog.stage("p", mm_emit, deps=[h_wb, h_g, h_ev_hist.get(ofb - 2),
                                                      h_trw])
                h_wb_used[wpar] = h_mm

                def ev_emit(v, ofb_=ofb):
                    pso = ps_o0 if ofb_ % 2 == 0 else ps_o1
                    c = slice(ofb_ * TC3, (ofb_ + 1) * TC3)
                    v.scalar_tensor_tensor(out=evb[ofb_ % 2], in0=pso[:, :],
                                           scalar=bb_sb[:, ofb_:ofb_ + 1],
                                           in1=x3[:, c], op0=ALU.add, op1=ALU.add)
                h_ev = prog.stage("v", ev_emit, deps=[h_mm, h_go_hist.get(ofb - 2)])
                h_ev_hist[ofb] = h_ev

                h_go = prog.stage("g", (lambda ofb_, tsl_: lambda g: [
                    g.dma_start(out=outs[ofb_ * 128:(ofb_ + 1) * 128, tsl_],
                                in_=evb[ofb_ % 2]),
                ])(ofb, tsl), deps=[h_ev], ndma=1)
                h_go_hist[ofb] = h_go
            h_ffn_done = h_ev
            h_out[t] = h_go
            h_ev_hist = {}
            h_go_hist = {}

        _run(nc, prog, ctx, [h_out[NC3 - 1]], extra_sems={"c": csem})
    return nc



_CACHE = {}


def _get(name, builder):
    if name not in _CACHE:
        _CACHE[name] = builder()
    return _CACHE[name]


def kernel(L0_attn_x, L0_attn_A_log, L0_attn_D, L0_attn_x_proj_w, L0_attn_dt_proj_w,
           L0_attn_dt_proj_b, L0_ffn_W_base, L0_ffn_b_base, L0_ffn_W_router,
           L0_ffn_A, L0_ffn_B):
    f32 = np.float32
    xT = np.ascontiguousarray(np.asarray(L0_attn_x, f32).reshape(TTOK, D).T)
    xpwT = np.ascontiguousarray(np.asarray(L0_attn_x_proj_w, f32).T)
    dtwT = np.ascontiguousarray(np.asarray(L0_attn_dt_proj_w, f32).T)
    dtb = np.ascontiguousarray(np.asarray(L0_attn_dt_proj_b, f32).reshape(32, 128).T)
    alog = np.ascontiguousarray(np.asarray(L0_attn_A_log, f32))
    dvec = np.asarray(L0_attn_D, f32)
    wbT = np.ascontiguousarray(np.asarray(L0_ffn_W_base, f32).T)
    bb = np.ascontiguousarray(np.asarray(L0_ffn_b_base, f32).reshape(32, 128).T)
    wrT = np.ascontiguousarray(np.asarray(L0_ffn_W_router, f32).T)
    aaT = np.ascontiguousarray(np.asarray(L0_ffn_A, f32).reshape(128, D).T)
    baT = np.ascontiguousarray(np.asarray(L0_ffn_B, f32).transpose(0, 2, 1).reshape(128, D))
    import ml_dtypes
    identb = np.eye(128, dtype=ml_dtypes.bfloat16)
    ident8 = np.eye(8, dtype=f32)
    identf = np.eye(128, dtype=f32)
    e8m = np.repeat(np.eye(8, dtype=f32), 16, axis=1)
    sel32 = np.zeros((32, 32 * 128), dtype=ml_dtypes.bfloat16)
    for n in range(32):
        sel32[n, n * 128:(n + 1) * 128] = 1.0

    cores = list(range(NCORE))
    nc = _get("fused", build_fused)
    in_maps = []
    for c in cores:
        ch = slice(c * CSL, (c + 1) * CSL)
        in_maps.append(dict(
            xs=np.ascontiguousarray(xT[:, c * TSL:(c + 1) * TSL]),
            xc=np.ascontiguousarray(xT[ch]),
            xpw_my=np.ascontiguousarray(
                xpwT[ch].reshape(4, 128, 160).transpose(1, 0, 2).reshape(128, 640)),
            dtw_my=np.ascontiguousarray(dtwT[:, ch]),
            dtb_my=np.ascontiguousarray(dtb[:, 4 * c:4 * (c + 1)]),
            alog_my=np.ascontiguousarray(
                alog[ch].reshape(4, 128, 16).transpose(1, 0, 2).reshape(128, 64)),
            dvec_my=np.ascontiguousarray(dvec[ch].reshape(4, 128).T),
            wbT=wbT, bb=bb, wrT=wrT, aaT=aaT, baT=baT,
            identb=identb, ident8=ident8, identf=identf, e8m=e8m, sel32=sel32,
        ))
    r = run_bass_kernel_spmd(nc, in_maps, core_ids=cores)
    out_full = np.concatenate([r.results[c]["outs"] for c in cores], axis=1)
    return np.ascontiguousarray(out_full.T).reshape(2, SEQ, D).astype(np.float32)


# version 07
# speedup vs baseline: 1.1102x; 1.0752x over previous
import numpy as np
import concourse.bass as bass
import concourse.mybir as mybir
from concourse.bass_utils import run_bass_kernel_spmd
from contextlib import ExitStack

F32 = mybir.dt.float32
BF16 = mybir.dt.bfloat16
F32R = mybir.dt.float32r
AF = mybir.ActivationFunctionType
ALU = mybir.AluOpType

D = 4096
TTOK = 16384
NCORE = 8
TSL = TTOK // NCORE
CSL = D // NCORE
SEQ = 8192
EPS = 1e-5

ENGS = ("q", "v", "s", "p", "g")


class _LastInstRecorder:

    def __init__(self, engine):
        self._e = engine
        self.last = None

    def __getattr__(self, name):
        f = getattr(self._e, name)
        if not callable(f):
            return f

        def wrap(*a, **k):
            r = f(*a, **k)
            if hasattr(r, "then_inc"):
                self.last = r
            return r
        return wrap


class Prog:

    def __init__(self):
        self.ctr = {e: 0 for e in ENGS}
        self.stream = {e: [] for e in ENGS}

    def stage(self, eng, emit, deps=(), ndma=0, raw=False):
        deps = [d for d in deps if d is not None]
        if ndma and self.ctr[eng]:
            deps.append((eng, self.ctr[eng]))
        self.stream[eng].append((list(deps), emit, "raw" if raw else ndma))
        if raw:
            return None
        if ndma:
            self.ctr[eng] += 16 * ndma
        else:
            self.ctr[eng] += 1
        return (eng, self.ctr[eng])

    def replay(self, eng_key, engine, sems):
        rec = _LastInstRecorder(engine)
        for deps, emit, ndma in self.stream[eng_key]:
            for (sk, val) in deps:
                engine.wait_ge(sems[sk], val)
            if ndma == "raw":
                emit(engine)
            elif ndma:
                insts = emit(engine)
                assert len(insts) == ndma, (eng_key, len(insts), ndma)
                for inst in insts:
                    inst.then_inc(sems[eng_key], 16)
            else:
                rec.last = None
                emit(rec)
                if rec.last is not None:
                    rec.last.then_inc(sems[eng_key], 1)
                else:
                    engine.drain().then_inc(sems[eng_key], 1)


def _run(nc, prog, ctx, final_deps, extra_sems=None):
    sems = {e: ctx.enter_context(nc.semaphore("sem_" + e)) for e in ENGS}
    if extra_sems:
        sems.update(extra_sems)
    block = ctx.enter_context(nc.Block())

    @block.sync
    def _(sync):
        prog.replay("q", sync, sems)

    @block.vector
    def _(vector):
        prog.replay("v", vector, sems)

    @block.scalar
    def _(scalar):
        prog.replay("s", scalar, sems)

    @block.tensor
    def _(tensor):
        prog.replay("p", tensor, sems)

    @block.gpsimd
    def _(gpsimd):
        prog.replay("g", gpsimd, sems)
        for (sk, val) in final_deps:
            gpsimd.wait_ge(sems[sk], val)



def _r3(dram_ap, inner):
    return dram_ap.rearrange("(k p) c -> p k c", p=128)


def _s3(sb_ap, inner):
    return sb_ap.rearrange("p (k c) -> p k c", c=inner)


def ln_stats(prog, sq_tile, x_tile, ones_col, st, ps_a, ps_b, nk, TC, deps, h_ones):
    h_sq = prog.stage("s", lambda s: s.activation(out=sq_tile[:], in_=x_tile[:], func=AF.Square),
                      deps=deps)

    def sums(p):
        for k in range(nk):
            c = slice(k * TC, (k + 1) * TC)
            p.matmul(ps_a[0:1, :], ones_col[:], x_tile[:, c], start=(k == 0), stop=(k == nk - 1))
        for k in range(nk):
            c = slice(k * TC, (k + 1) * TC)
            p.matmul(ps_b[0:1, :], ones_col[:], sq_tile[:, c], start=(k == 0), stop=(k == nk - 1))
    h_sum = prog.stage("p", sums, deps=list(deps) + [h_sq, h_ones])

    def stats1(v):
        v.tensor_scalar_mul(st[:, 0 * TC:1 * TC], ps_a[0:1, :], 1.0 / D)
        v.tensor_scalar(st[:, 1 * TC:2 * TC], ps_b[0:1, :], 1.0 / D, EPS,
                        op0=ALU.mult, op1=ALU.add)
        v.drain()
        v.tensor_tensor(out=st[:, 2 * TC:3 * TC], in0=st[:, 0 * TC:1 * TC],
                        in1=st[:, 0 * TC:1 * TC], op=ALU.mult)
        v.drain()
        v.tensor_tensor(out=st[:, 2 * TC:3 * TC], in0=st[:, 1 * TC:2 * TC],
                        in1=st[:, 2 * TC:3 * TC], op=ALU.subtract)
        v.drain()
        v.reciprocal(out=st[:, 3 * TC:4 * TC], in_=st[:, 2 * TC:3 * TC])
    h_st1 = prog.stage("v", stats1, deps=[h_sum])

    h_rt = prog.stage("s", lambda s: s.activation(out=st[:, 4 * TC:5 * TC],
                                                  in_=st[:, 3 * TC:4 * TC], func=AF.Sqrt),
                      deps=[h_st1])
    return h_rt


def bcast_mr(prog, ones_row, st, ps_m, ps_r, TC, deps, h_ones):
    def bc(p):
        p.matmul(ps_m[:, 0:TC], ones_row[:], st[0:1, 0 * TC:1 * TC], start=True, stop=True)
        p.matmul(ps_r[:, 0:TC], ones_row[:], st[0:1, 4 * TC:5 * TC], start=True, stop=True)
    return prog.stage("p", bc, deps=list(deps) + [h_ones])



def build_l1():
    nc = bass.Bass("TRN2", target_bir_lowering=False)
    xs = nc.dram_tensor("xs", [D, TSL], F32, kind="ExternalInput").ap()
    xpwT = nc.dram_tensor("xpwT", [D, 160], F32, kind="ExternalInput").ap()
    dtwT = nc.dram_tensor("dtwT", [128, D], F32, kind="ExternalInput").ap()
    dtb = nc.dram_tensor("dtb", [128, 32], F32, kind="ExternalInput").ap()
    dts = nc.dram_tensor("dts", [D, TSL], F32, kind="ExternalOutput").ap()
    xlns = nc.dram_tensor("xlns", [D, TSL], F32, kind="ExternalOutput").ap()
    bcs = nc.dram_tensor("bcs", [32, TSL], F32, kind="ExternalOutput").ap()

    TC = 256
    NT = TSL // TC
    prog = Prog()
    with ExitStack() as ctx:
        sb = lambda n_, sh, dt=F32: ctx.enter_context(nc.sbuf_tensor(n_, sh, dt))
        x_sb = sb("x_sb", [128, 32 * TC])
        sq_sb = sb("sq_sb", [128, 32 * TC])
        xln_sb = sb("xln_sb", [128, 32 * TC])
        dt_sb = sb("dt_sb", [128, 32 * TC])
        dtin_sb = sb("dtin_sb", [128, TC])
        bc_sb = sb("bc_sb", [32, TC])
        xpw_sb = sb("xpw_sb", [128, 32 * 160])
        dtw_sb = sb("dtw_sb", [128, 32 * 128])
        dtb_sb = sb("dtb_sb", [128, 32])
        ones_col = sb("ones_col", [128, 1])
        ones_row = sb("ones_row", [1, 128])
        st = sb("st", [1, 8 * TC])
        pb = [ctx.enter_context(nc.psum_tensor(f"pb{i}", [128, TC], F32)) for i in range(8)]
        ps_sx, ps_sq, ps_mb, ps_rb, ps_dt, ps_bc, ps_d0, ps_d1 = pb

        h_ones = prog.stage("v", lambda v: (v.memset(ones_col[:], 1.0), v.memset(ones_row[:], 1.0)))
        h_w = prog.stage("q", lambda q: [
            q.dma_start(out=_s3(xpw_sb[:], 160), in_=_r3(xpwT, 160)),
            q.dma_start(out=dtw_sb[:], in_=dtwT),
            q.dma_start(out=dtb_sb[:], in_=dtb),
        ], ndma=3)

        h_out = None
        h_xln_prev = None
        for t in range(NT):
            tsl = slice(t * TC, (t + 1) * TC)
            h_xld = prog.stage("q", (lambda tsl_: lambda q: [
                q.dma_start(out=_s3(x_sb[:], TC), in_=_r3(xs[:, tsl_], TC))
            ])(tsl), deps=[h_xln_prev], ndma=1)

            h_rt = ln_stats(prog, sq_sb, x_sb, ones_col, st, ps_sx, ps_sq, 32, TC,
                            deps=[h_xld], h_ones=h_ones)
            h_bc = bcast_mr(prog, ones_row, st, ps_mb, ps_rb, TC, deps=[h_rt], h_ones=h_ones)

            def xln_emit(v):
                for k in range(32):
                    c = slice(k * TC, (k + 1) * TC)
                    v.tensor_tensor(out=sq_sb[:, c], in0=x_sb[:, c], in1=ps_mb[:], op=ALU.subtract)
                v.drain()
                for k in range(32):
                    c = slice(k * TC, (k + 1) * TC)
                    v.tensor_tensor(out=xln_sb[:, c], in0=sq_sb[:, c], in1=ps_rb[:], op=ALU.mult)
            h_xln = prog.stage("v", xln_emit, deps=[h_bc, h_out])
            h_xln_prev = h_xln

            def xp_emit(p):
                for k in range(32):
                    c = slice(k * TC, (k + 1) * TC)
                    p.matmul(ps_dt[:], xpw_sb[:, k * 160:k * 160 + 128], xln_sb[:, c],
                             start=(k == 0), stop=(k == 31))
                for k in range(32):
                    c = slice(k * TC, (k + 1) * TC)
                    p.matmul(ps_bc[0:32, :], xpw_sb[:, k * 160 + 128:(k + 1) * 160], xln_sb[:, c],
                             start=(k == 0), stop=(k == 31))
            h_xp = prog.stage("p", xp_emit, deps=[h_xln, h_w])

            def evac_emit(v):
                v.tensor_copy(out=dtin_sb[:], in_=ps_dt[:])
                v.tensor_copy(out=bc_sb[:], in_=ps_bc[0:32, :])
            h_ev = prog.stage("v", evac_emit, deps=[h_xp, h_out])

            h_pe = None
            h_sp = None
            for gidx in range(16):
                def pe_emit(p, gi=gidx):
                    for j in range(2):
                        dm = gi * 2 + j
                        psd = ps_d0 if j == 0 else ps_d1
                        p.matmul(psd[:], dtw_sb[:, dm * 128:(dm + 1) * 128], dtin_sb[:],
                                 start=True, stop=True)
                h_pe = prog.stage("p", pe_emit, deps=[h_ev, h_sp])

                def sp_emit(s, gi=gidx):
                    for j in range(2):
                        dm = gi * 2 + j
                        c = slice(dm * TC, (dm + 1) * TC)
                        psd = ps_d0 if j == 0 else ps_d1
                        s.activation(out=dt_sb[:, c], in_=psd[:], func=AF.Exp,
                                     bias=dtb_sb[:, dm:dm + 1])
                        s.drain()
                        s.activation(out=dt_sb[:, c], in_=dt_sb[:, c], func=AF.Ln, bias=1.0)
                h_sp = prog.stage("s", sp_emit, deps=[h_pe, h_out])

            h_out = prog.stage("g", (lambda tsl_: lambda g: [
                g.dma_start(out=_r3(dts[:, tsl_], TC), in_=_s3(dt_sb[:], TC)),
                g.dma_start(out=_r3(xlns[:, tsl_], TC), in_=_s3(xln_sb[:], TC)),
                g.dma_start(out=bcs[:, tsl_], in_=bc_sb[:]),
            ])(tsl), deps=[h_sp, h_xln, h_ev], ndma=3)

        _run(nc, prog, ctx, [h_out])
    return nc



def build_l2():
    nc = bass.Bass("TRN2", target_bir_lowering=False)
    dtc = nc.dram_tensor("dtc", [CSL, TTOK], F32, kind="ExternalInput").ap()
    xlnc = nc.dram_tensor("xlnc", [CSL, TTOK], F32, kind="ExternalInput").ap()
    bc = nc.dram_tensor("bc", [32, TTOK], F32, kind="ExternalInput").ap()
    sel16 = nc.dram_tensor("sel16", [16, 16 * 128], F32, kind="ExternalInput").ap()
    alog = nc.dram_tensor("alog", [128, (CSL // 128) * 16], F32, kind="ExternalInput").ap()
    dvec = nc.dram_tensor("dvec", [128, CSL // 128], F32, kind="ExternalInput").ap()
    yc = nc.dram_tensor("yc", [CSL, TTOK], F32, kind="ExternalOutput").ap()

    TC = 1024
    NT = TTOK // TC
    NCH = CSL // 128
    NSUB = TC // 512
    prog = Prog()
    with ExitStack() as ctx:
        sb = lambda n_, sh, dt=F32: ctx.enter_context(nc.sbuf_tensor(n_, sh, dt))
        dt_sb = [sb(f"dt{i}", [128, TC]) for i in range(NCH)]
        xln_sb = [sb(f"xln{i}", [128, TC]) for i in range(NCH)]
        dtx_sb = [sb(f"dtx{i}", [128, TC]) for i in range(NCH)]
        y_sb = [sb(f"y{i}", [128, TC]) for i in range(NCH)]
        dA_sb = [sb(f"dA{i}", [128, TC]) for i in range(NCH)]
        dBx_sb = [sb(f"dBx{i}", [128, TC]) for i in range(NCH)]
        h_sb = [sb(f"h{i}", [128, TC]) for i in range(NCH)]
        b_sb = sb("b_sb", [16, TC])
        c_sb = sb("c_sb", [16, TC])
        sel_sb = sb("sel_sb", [16, 16 * 128])
        a_sb = sb("a_sb", [128, NCH * 16])
        d_sb = sb("d_sb", [128, NCH])
        carry = sb("carry", [128, NCH * 16])
        ones_row = sb("ones_row", [1, 128])
        ps_b = [ctx.enter_context(nc.psum_tensor(f"psb{i}", [128, 512], F32)) for i in range(NSUB)]
        ps_c = [ctx.enter_context(nc.psum_tensor(f"psc{i}", [128, 512], F32)) for i in range(NSUB)]

        prog.stage("v", lambda v: (v.memset(ones_row[:], 1.0), v.memset(carry[:], 0.0)))
        h_cst = prog.stage("q", lambda q: [
            q.dma_start(out=a_sb[:], in_=alog),
            q.dma_start(out=d_sb[:], in_=dvec),
            q.dma_start(out=sel_sb[:], in_=sel16),
        ], ndma=3)
        h_a = prog.stage("s", lambda s: (s.activation(out=a_sb[:], in_=a_sb[:], func=AF.Exp),
                                         s.drain(),
                                         s.mul(out=a_sb[:], in_=a_sb[:], mul=-1.0)),
                         deps=[h_cst])

        h_y = None
        h_vq = None
        h_sn = None
        for t in range(NT):
            tsl = slice(t * TC, (t + 1) * TC)
            h_ld = prog.stage("q", (lambda tsl_: lambda q: [
                q.dma_start(out=dt_sb[i][:], in_=dtc[128 * i:128 * (i + 1), tsl_])
                for i in range(NCH)] + [
                q.dma_start(out=xln_sb[i][:], in_=xlnc[128 * i:128 * (i + 1), tsl_])
                for i in range(NCH)] + [
                q.dma_start(out=b_sb[:], in_=bc[0:16, tsl_]),
                q.dma_start(out=c_sb[:], in_=bc[16:32, tsl_]),
            ])(tsl), deps=[h_vq, h_sn, h_y], ndma=2 * NCH + 2)

            def prep_emit(v):
                for i in range(NCH):
                    v.tensor_tensor(out=dtx_sb[i][:], in0=dt_sb[i][:], in1=xln_sb[i][:], op=ALU.mult)
                for i in range(NCH):
                    v.tensor_scalar_mul(y_sb[i][:], xln_sb[i][:], d_sb[:, i:i + 1])
            h_prep = prog.stage("v", prep_emit, deps=[h_ld, h_cst, h_y])

            h_pb = None
            h_vq = h_prep
            for n in range(16):
                def bcst_emit(p, n_=n):
                    sel = sel_sb[:, n_ * 128:(n_ + 1) * 128]
                    for u in range(NSUB):
                        cs = slice(u * 512, (u + 1) * 512)
                        p.matmul(ps_b[u][:], sel, b_sb[:, cs], start=True, stop=True)
                        p.matmul(ps_c[u][:], sel, c_sb[:, cs], start=True, stop=True)
                h_pb = prog.stage("p", bcst_emit, deps=[h_ld, h_vq, h_cst])

                def dA_emit(s, n_=n):
                    for i in range(NCH):
                        s.activation(out=dA_sb[i][:], in_=dt_sb[i][:], func=AF.Exp,
                                     scale=a_sb[:, i * 16 + n_:i * 16 + n_ + 1])
                h_sn = prog.stage("s", dA_emit, deps=[h_ld, h_a, h_vq])

                def scan_emit(v, n_=n, t_=t):
                    for i in range(NCH):
                        for u in range(NSUB):
                            cs = slice(u * 512, (u + 1) * 512)
                            v.tensor_tensor(out=dBx_sb[i][:, cs], in0=dtx_sb[i][:, cs],
                                            in1=ps_b[u][:], op=ALU.mult)
                    if t_ == NT // 2:
                        for i in range(NCH):
                            v.memset(carry[:, i * 16 + n_:i * 16 + n_ + 1], 0.0)
                    v.drain()
                    for i in range(NCH):
                        col = slice(i * 16 + n_, i * 16 + n_ + 1)
                        v.tensor_tensor_scan(out=h_sb[i][:], data0=dA_sb[i][:], data1=dBx_sb[i][:],
                                             initial=carry[:, col], op0=ALU.mult, op1=ALU.add)
                    v.drain()
                    for i in range(NCH):
                        col = slice(i * 16 + n_, i * 16 + n_ + 1)
                        v.tensor_copy(out=carry[:, col], in_=h_sb[i][:, TC - 1:TC])
                    for i in range(NCH):
                        for u in range(NSUB):
                            cs = slice(u * 512, (u + 1) * 512)
                            v.tensor_tensor(out=dBx_sb[i][:, cs], in0=h_sb[i][:, cs],
                                            in1=ps_c[u][:], op=ALU.mult)
                    v.drain()
                    for i in range(NCH):
                        v.tensor_tensor(out=y_sb[i][:], in0=y_sb[i][:], in1=dBx_sb[i][:], op=ALU.add)
                h_vq = prog.stage("v", scan_emit, deps=[h_pb, h_sn])

            h_y = prog.stage("g", (lambda tsl_: lambda g: [
                g.dma_start(out=yc[128 * i:128 * (i + 1), tsl_], in_=y_sb[i][:])
                for i in range(NCH)])(tsl), deps=[h_vq], ndma=NCH)

        _run(nc, prog, ctx, [h_y])
    return nc



def build_l3():
    nc = bass.Bass("TRN2", target_bir_lowering=False)
    xs = nc.dram_tensor("xs", [D, TSL], F32, kind="ExternalInput").ap()
    ys = nc.dram_tensor("ys", [D, TSL], F32, kind="ExternalInput").ap()
    wbT = nc.dram_tensor("wbT", [D, D], F32, kind="ExternalInput").ap()
    bb = nc.dram_tensor("bb", [128, 32], F32, kind="ExternalInput").ap()
    wrT = nc.dram_tensor("wrT", [D, 8], F32, kind="ExternalInput").ap()
    aaT = nc.dram_tensor("aaT", [D, 128], F32, kind="ExternalInput").ap()
    baT = nc.dram_tensor("baT", [128, D], F32, kind="ExternalInput").ap()
    ident = nc.dram_tensor("ident", [128, 128], F32, kind="ExternalInput").ap()
    e8m = nc.dram_tensor("e8m", [8, 128], F32, kind="ExternalInput").ap()
    outs = nc.dram_tensor("outs", [D, TSL], F32, kind="ExternalOutput").ap()

    TC = 256
    NT = TSL // TC
    NTI = TC // 128
    prog = Prog()
    with ExitStack() as ctx:
        sb = lambda n_, sh, dt=F32: ctx.enter_context(nc.sbuf_tensor(n_, sh, dt))
        x_sb = sb("x_sb", [128, 32 * TC])
        sq_sb = sb("sq_sb", [128, 32 * TC])
        y_sb = sb("y_sb", [128, 32 * TC])
        xln_sb = sb("xln_sb", [128, 32 * TC])
        wb_sb = sb("wb_sb", [128, 32 * 256])
        aa_sb = sb("aa_sb", [128, 32 * 128])
        wr_sb = sb("wr_sb", [128, 32 * 8])
        ba_sb = sb("ba_sb", [128, D])
        bb_sb = sb("bb_sb", [128, 32])
        id_sb = sb("id_sb", [128, 128])
        e8_sb = sb("e8_sb", [8, 128])
        lg_sb = sb("lg_sb", [8, TC])
        lgT_sb = sb("lgT_sb", [128, 8 * NTI])
        w2_sb = sb("w2_sb", [128, 8 * NTI])
        t2s = sb("t2s", [128, 16 * NTI])
        sc_sb = sb("sc_sb", [128, 4 * NTI])
        w8_sb = sb("w8_sb", [8, TC])
        g_sb = sb("g_sb", [128, TC])
        wbc_sb = sb("wbc_sb", [128, TC])
        ones_col = sb("ones_col", [128, 1])
        ones_row = sb("ones_row", [1, 128])
        st = sb("st", [1, 8 * TC])
        pb = [ctx.enter_context(nc.psum_tensor(f"pb{i}", [128, TC], F32)) for i in range(8)]
        ps_sx, ps_sq, ps_mb, ps_rb, ps_aux, ps_h, ps_o0, ps_o1 = pb

        h_ones = prog.stage("v", lambda v: (v.memset(ones_col[:], 1.0), v.memset(ones_row[:], 1.0)))
        h_w = prog.stage("q", lambda q: [
            q.dma_start(out=_s3(aa_sb[:], 128), in_=_r3(aaT, 128)),
            q.dma_start(out=_s3(wr_sb[:], 8), in_=_r3(wrT, 8)),
            q.dma_start(out=ba_sb[:], in_=baT),
            q.dma_start(out=bb_sb[:], in_=bb),
            q.dma_start(out=id_sb[:], in_=ident),
            q.dma_start(out=e8_sb[:], in_=e8m),
        ], ndma=6)

        h_out = None
        h_ffn_done = None
        h_wb_used = None
        for t in range(NT):
            tsl = slice(t * TC, (t + 1) * TC)
            h_ld = prog.stage("q", (lambda tsl_: lambda q: [
                q.dma_start(out=_s3(x_sb[:], TC), in_=_r3(xs[:, tsl_], TC)),
                q.dma_start(out=_s3(y_sb[:], TC), in_=_r3(ys[:, tsl_], TC)),
            ])(tsl), deps=[h_ffn_done, h_out], ndma=2)

            def res_emit(v):
                for k in range(32):
                    c = slice(k * TC, (k + 1) * TC)
                    v.tensor_tensor(out=x_sb[:, c], in0=x_sb[:, c], in1=y_sb[:, c], op=ALU.add)
            h_res = prog.stage("v", res_emit, deps=[h_ld])

            h_rt = ln_stats(prog, sq_sb, x_sb, ones_col, st, ps_sx, ps_sq, 32, TC,
                            deps=[h_res, h_wb_used], h_ones=h_ones)
            h_bc = bcast_mr(prog, ones_row, st, ps_mb, ps_rb, TC, deps=[h_rt], h_ones=h_ones)

            def xln_emit(v):
                for k in range(32):
                    c = slice(k * TC, (k + 1) * TC)
                    v.tensor_tensor(out=sq_sb[:, c], in0=x_sb[:, c], in1=ps_mb[:], op=ALU.subtract)
                v.drain()
                for k in range(32):
                    c = slice(k * TC, (k + 1) * TC)
                    v.tensor_tensor(out=xln_sb[:, c], in0=sq_sb[:, c], in1=ps_rb[:], op=ALU.mult)
            h_xln = prog.stage("v", xln_emit, deps=[h_bc])

            def lg_emit(p):
                for k in range(32):
                    c = slice(k * TC, (k + 1) * TC)
                    p.matmul(ps_aux[0:8, :], wr_sb[:, k * 8:(k + 1) * 8], xln_sb[:, c],
                             start=(k == 0), stop=(k == 31))
            h_lg = prog.stage("p", lg_emit, deps=[h_xln, h_w])
            h_lge = prog.stage("v", lambda v: v.tensor_copy(out=lg_sb[:], in_=ps_aux[0:8, :]),
                               deps=[h_lg])

            def tr_emit(p):
                for i in range(NTI):
                    dst = (ps_mb, ps_rb)[i]
                    p.transpose(dst[:, 0:8], lg_sb[:, i * 128:(i + 1) * 128], id_sb[0:8, 0:8])
            h_tr = prog.stage("p", tr_emit, deps=[h_lge, h_xln])

            def tre_emit(v):
                for i in range(NTI):
                    src = (ps_mb, ps_rb)[i]
                    v.tensor_copy(out=lgT_sb[:, i * 8:(i + 1) * 8], in_=src[:, 0:8])
            h_tre = prog.stage("v", tre_emit, deps=[h_tr])

            lgT3 = lgT_sb[:].rearrange("p (i e) -> p i e", e=8)
            w23 = w2_sb[:].rearrange("p (i e) -> p i e", e=8)
            eq23 = t2s[:, 0:8 * NTI].rearrange("p (i e) -> p i e", e=8)
            eq2b = t2s[:, 8 * NTI:16 * NTI].rearrange("p (i e) -> p i e", e=8)
            m1 = sc_sb[:, 0 * NTI:1 * NTI]
            m2 = sc_sb[:, 1 * NTI:2 * NTI]
            dd = sc_sb[:, 2 * NTI:3 * NTI]
            e1 = sc_sb[:, 3 * NTI:4 * NTI]

            def top2a_emit(v):
                v.reduce_max(m1, lgT3, axis=mybir.AxisListType.X)
                v.drain()
                a_, b_ = bass.broadcast_tensor_aps(lgT3, m1[:, :, None])
                v.tensor_tensor(out=w23, in0=a_, in1=b_, op=ALU.is_equal)
                v.drain()
                v.scalar_tensor_tensor(out=lgT3, in0=w23, scalar=-1e30, in1=lgT3,
                                       op0=ALU.mult, op1=ALU.add)
                v.drain()
                v.reduce_max(m2, lgT3, axis=mybir.AxisListType.X)
                v.drain()
                v.tensor_tensor(out=dd, in0=m1, in1=m2, op=ALU.subtract)
            h_t2a = prog.stage("v", top2a_emit, deps=[h_tre])

            h_sg = prog.stage("s", lambda s: s.activation(out=e1, in_=dd, func=AF.Sigmoid),
                              deps=[h_t2a])

            def top2b_emit(v):
                a_, b_ = bass.broadcast_tensor_aps(lgT3, m2[:, :, None])
                v.tensor_tensor(out=eq23, in0=a_, in1=b_, op=ALU.is_equal)
                a_, b_ = bass.broadcast_tensor_aps(w23, e1[:, :, None])
                v.tensor_tensor(out=w23, in0=a_, in1=b_, op=ALU.mult)
                v.drain()
                a_, b_ = bass.broadcast_tensor_aps(eq23, e1[:, :, None])
                v.tensor_tensor(out=eq2b, in0=a_, in1=b_, op=ALU.mult)
                v.drain()
                v.tensor_tensor(out=eq23, in0=eq23, in1=eq2b, op=ALU.subtract)
                v.drain()
                v.tensor_tensor(out=w2_sb[:], in0=w2_sb[:], in1=t2s[:, 0:8 * NTI], op=ALU.add)
            h_t2b = prog.stage("v", top2b_emit, deps=[h_sg])

            def trw_emit(p):
                for i in range(NTI):
                    dst = (ps_mb, ps_rb)[i]
                    p.transpose(dst[0:8, 0:128], w2_sb[:, i * 8:(i + 1) * 8], id_sb[:, :])
            h_trw = prog.stage("p", trw_emit, deps=[h_t2b, h_tre])

            def w8_emit(v):
                for i in range(NTI):
                    src = (ps_mb, ps_rb)[i]
                    v.tensor_copy(out=w8_sb[:, i * 128:(i + 1) * 128], in_=src[0:8, 0:128])
            h_w8 = prog.stage("v", w8_emit, deps=[h_trw])

            def hall_emit(p):
                for k in range(32):
                    c = slice(k * TC, (k + 1) * TC)
                    p.matmul(ps_h[:], aa_sb[:, k * 128:(k + 1) * 128], xln_sb[:, c],
                             start=(k == 0), stop=(k == 31))
                p.matmul(ps_aux[:], e8_sb[:], w8_sb[:], start=True, stop=True)
            h_hall = prog.stage("p", hall_emit, deps=[h_xln, h_w8, h_w, h_lge])

            def g_emit(v):
                v.tensor_copy(out=wbc_sb[:], in_=ps_aux[:])
                v.drain()
                v.scalar_tensor_tensor(out=g_sb[:], in0=ps_h[:], scalar=2.0, in1=wbc_sb[:],
                                       op0=ALU.mult, op1=ALU.mult)
            h_g = prog.stage("v", g_emit, deps=[h_hall])

            h_ev = None
            h_mm = None
            h_wb = None
            for ofb in range(16):
                prev_reader = h_mm
                h_wb = prog.stage("q", (lambda ofb_: lambda q: [
                    q.dma_start(out=_s3(wb_sb[:], 256), in_=_r3(wbT[:, ofb_ * 256:(ofb_ + 1) * 256], 256)),
                ])(ofb), deps=[h_xln, prev_reader], ndma=1)

                def mm_emit(p, ofb_=ofb):
                    for j in range(2):
                        pso = ps_o0 if j == 0 else ps_o1
                        ofc = ofb_ * 2 + j
                        for k in range(32):
                            c = slice(k * TC, (k + 1) * TC)
                            p.matmul(pso[:], wb_sb[:, (k * 2 + j) * 128:(k * 2 + j + 1) * 128],
                                     xln_sb[:, c], start=(k == 0), stop=False)
                        p.matmul(pso[:], ba_sb[:, ofc * 128:(ofc + 1) * 128], g_sb[:],
                                 start=False, stop=True)
                h_mm = prog.stage("p", mm_emit, deps=[h_wb, h_g, h_ev])

                def ev_emit(v, ofb_=ofb):
                    for j in range(2):
                        pso = ps_o0 if j == 0 else ps_o1
                        ofc = ofb_ * 2 + j
                        c = slice(ofc * TC, (ofc + 1) * TC)
                        v.scalar_tensor_tensor(out=y_sb[:, c], in0=pso[:],
                                               scalar=bb_sb[:, ofc:ofc + 1],
                                               in1=x_sb[:, c], op0=ALU.add, op1=ALU.add)
                h_ev = prog.stage("v", ev_emit, deps=[h_mm, h_out])
            h_ffn_done = h_ev
            h_wb_used = h_mm

            h_out = prog.stage("g", (lambda tsl_: lambda g: [
                g.dma_start(out=_r3(outs[:, tsl_], TC), in_=_s3(y_sb[:], TC)),
            ])(tsl), deps=[h_ffn_done], ndma=1)

        _run(nc, prog, ctx, [h_out])
    return nc



def build_fused():
    nc = bass.Bass("TRN2", target_bir_lowering=False)
    xs = nc.dram_tensor("xs", [D, TSL], F32, kind="ExternalInput").ap()
    xc = nc.dram_tensor("xc", [CSL, TTOK], F32, kind="ExternalInput").ap()
    xpw_my = nc.dram_tensor("xpw_my", [128, 4 * 160], F32, kind="ExternalInput").ap()
    dtw_my = nc.dram_tensor("dtw_my", [128, CSL], F32, kind="ExternalInput").ap()
    dtb_my = nc.dram_tensor("dtb_my", [128, 4], F32, kind="ExternalInput").ap()
    alog_my = nc.dram_tensor("alog_my", [128, 64], F32, kind="ExternalInput").ap()
    dvec_my = nc.dram_tensor("dvec_my", [128, 4], F32, kind="ExternalInput").ap()
    wbT = nc.dram_tensor("wbT", [D, D], F32, kind="ExternalInput").ap()
    bb = nc.dram_tensor("bb", [128, 32], F32, kind="ExternalInput").ap()
    wrT = nc.dram_tensor("wrT", [D, 8], F32, kind="ExternalInput").ap()
    aaT = nc.dram_tensor("aaT", [D, 128], F32, kind="ExternalInput").ap()
    baT = nc.dram_tensor("baT", [128, D], F32, kind="ExternalInput").ap()
    identb = nc.dram_tensor("identb", [128, 128], BF16, kind="ExternalInput").ap()
    ident8 = nc.dram_tensor("ident8", [8, 8], F32, kind="ExternalInput").ap()
    identf = nc.dram_tensor("identf", [128, 128], F32, kind="ExternalInput").ap()
    e8m = nc.dram_tensor("e8m", [8, 128], F32, kind="ExternalInput").ap()
    sel32 = nc.dram_tensor("sel32", [32, 32 * 128], BF16, kind="ExternalInput").ap()
    outs = nc.dram_tensor("outs", [D, TSL], F32, kind="ExternalOutput").ap()
    ar_in = nc.dram_tensor("ar_in", [163, TTOK], F32, kind="Internal").ap()
    ar_out = nc.dram_tensor("ar_out", [163, TTOK], F32, kind="Internal").ap()
    mrows = nc.dram_tensor("mrows", [3, TTOK], F32, kind="Internal").ap()
    ylocal = nc.dram_tensor("ylocal", [8 * CSL, TSL], F32, kind="Internal").ap()
    ygath = nc.dram_tensor("ygath", [8 * D, TSL], F32, kind="Internal").ap()
    wbh = nc.dram_tensor("wbh", [D, D], BF16, kind="Internal").ap()
    aah = nc.dram_tensor("aah", [D, 128], BF16, kind="Internal").ap()
    bah = nc.dram_tensor("bah", [128, D], BF16, kind="Internal").ap()
    wrh = nc.dram_tensor("wrh", [D, 8], BF16, kind="Internal").ap()

    RG = [[0, 1, 2, 3, 4, 5, 6, 7]]
    prog = Prog()
    ncc = {"c": 0}

    with ExitStack() as ctx:
        sb = lambda n_, sh, dt=F32: ctx.enter_context(nc.sbuf_tensor(n_, sh, dt))
        xpw_sb = sb("xpw_sb", [128, 4 * 160])
        dtw_sb = sb("dtw_sb", [128, CSL])
        dtb_sb = sb("dtb_sb", [128, 4])
        a_sb = sb("a_sb", [128, 64])
        d_sb = sb("d_sb", [128, 4])
        carry = sb("carry", [128, 64])
        wr_sb = sb("wr_sb", [128, 32 * 8], BF16)
        bb_sb = sb("bb_sb", [128, 32])
        idb_sb = sb("idb_sb", [128, 128], BF16)
        id8_sb = sb("id8_sb", [8, 8])
        idf_sb = sb("idf_sb", [128, 128])
        e8_sb = sb("e8_sb", [8, 128])
        sel_sb = sb("sel_sb", [32, 32 * 128], BF16)
        ones_col = sb("ones_col", [128, 1])
        ones_row = sb("ones_row", [1, 128])
        csd_sb = sb("csd_sb", [128, 1])
        csb_sb = sb("csb_sb", [32, 1])
        stt_a = sb("stt_a", [128, 128])
        stt_b = sb("stt_b", [128, 128])
        stt_c = sb("stt_c", [128, 128])
        stt_d = sb("stt_d", [128, 128])
        zrow = sb("zrow", [1, 2048])
        st = sb("st", [1, 8 * 256])
        cast_in = sb("cast_in", [128, 2048])
        cast_out = sb("cast_out", [128, 2048], BF16)
        ARN = 36864
        arena = sb("arena", [128, ARN])
        apos = {"o": 0}

        def aslc(cols, parts=128, dt=F32):
            w = cols if dt == F32 else (cols + 1) // 2
            o = apos["o"]
            apos["o"] = o + w
            assert apos["o"] <= ARN, (o, w)
            if dt == F32:
                return arena[0:parts, o:o + cols]
            return arena[0:parts, o:o + w].bitcast(dt)[0:parts, 0:cols]

        pbank = [ctx.enter_context(nc.psum_tensor(f"pb{i}", [128, 512], F32)) for i in range(8)]

        h_ones = prog.stage("v", lambda v: (v.memset(ones_col[:], 1.0),
                                            v.memset(ones_row[:], 1.0),
                                            v.memset(carry[:], 0.0),
                                            v.memset(zrow[:], 0.0)))
        cidbox = {}

        def w0_emit(q):
            cidbox["cid"] = q.partition_id()
            return [
                q.dma_start(out=xpw_sb[:], in_=xpw_my),
                q.dma_start(out=dtw_sb[:], in_=dtw_my),
                q.dma_start(out=dtb_sb[:], in_=dtb_my),
                q.dma_start(out=a_sb[:], in_=alog_my),
                q.dma_start(out=d_sb[:], in_=dvec_my),
                q.dma_start(out=bb_sb[:], in_=bb),
                q.dma_start(out=idb_sb[:], in_=identb),
                q.dma_start(out=id8_sb[:], in_=ident8),
                q.dma_start(out=idf_sb[:], in_=identf),
                q.dma_start(out=e8_sb[:], in_=e8m),
                q.dma_start(out=sel_sb[:], in_=sel32),
            ]
        h_w0 = prog.stage("q", w0_emit, ndma=11)
        h_a = prog.stage("s", lambda s: (s.activation(out=a_sb[:], in_=a_sb[:], func=AF.Exp),
                                         s.drain(),
                                         s.mul(out=a_sb[:], in_=a_sb[:], mul=-1.0)),
                         deps=[h_w0])

        def emit_cast(srcT, dstT, col0, ncols, dep, gdep):
            h1 = prog.stage("q", (lambda: lambda q: [
                q.dma_start(out=cast_in[:, 0:32 * ncols].rearrange("p (k c) -> p k c", c=ncols),
                            in_=_r3(srcT[:, col0:col0 + ncols], ncols)),
            ])(), deps=[dep], ndma=1)
            h2 = prog.stage("s", (lambda: lambda s: s.copy(
                out=cast_out[:, 0:32 * ncols], in_=cast_in[:, 0:32 * ncols]))(),
                deps=[h1, gdep])
            h3 = prog.stage("g", (lambda: lambda g: [
                g.dma_start(out=_r3(dstT[:, col0:col0 + ncols], ncols),
                            in_=cast_out[:, 0:32 * ncols].rearrange("p (k c) -> p k c", c=ncols)),
            ])(), deps=[h2], ndma=1)
            return h3

        h_cast = None
        for cb in range(64):
            h_cast = emit_cast(wbT, wbh, cb * 64, 64, h_cast, h_cast)
        for cb in range(2):
            h_cast = emit_cast(aaT, aah, cb * 64, 64, h_cast, h_cast)
        h_cast = emit_cast(wrT, wrh, 0, 8, h_cast, h_cast)
        for half in range(2):
            h1 = prog.stage("q", (lambda half_: lambda q: [
                q.dma_start(out=cast_in[:], in_=baT[:, half_ * 2048:(half_ + 1) * 2048])])(half),
                deps=[h_cast], ndma=1)
            h2 = prog.stage("s", lambda s: s.copy(out=cast_out[:], in_=cast_in[:]), deps=[h1])
            h_cast = prog.stage("g", (lambda half_: lambda g: [
                g.dma_start(out=bah[:, half_ * 2048:(half_ + 1) * 2048], in_=cast_out[:])])(half),
                deps=[h2], ndma=1)
        h_castall = h_cast

        TA = 2048
        NA = TTOK // TA
        xa = aslc(4 * TA)
        sqa = aslc(4 * TA)
        xpd_st = aslc(TA)
        xpbc_st = aslc(TA, 32)
        srow_x = aslc(TA, 1)
        srow_q = aslc(TA, 1)

        def cs_emit(p):
            for k in range(4):
                p.matmul(pbank[6][:, 0:1], xpw_sb[:, k * 160:k * 160 + 128], ones_col[:],
                         start=(k == 0), stop=(k == 3))
            for k in range(4):
                p.matmul(pbank[7][0:32, 0:1], xpw_sb[:, k * 160 + 128:(k + 1) * 160], ones_col[:],
                         start=(k == 0), stop=(k == 3))
        h_cs = prog.stage("p", cs_emit, deps=[h_w0, h_ones])
        h_cse = prog.stage("v", lambda v: (v.tensor_copy(out=csd_sb[:], in_=pbank[6][:, 0:1]),
                                           v.tensor_copy(out=csb_sb[:], in_=pbank[7][0:32, 0:1])),
                           deps=[h_cs])
        h_z = prog.stage("g", lambda g: [
            g.dma_start(out=ar_in[162:163, k * 2048:(k + 1) * 2048], in_=zrow[:])
            for k in range(8)], deps=[h_ones], ndma=8)
        h_csw = prog.stage("g", lambda g: [
            g.dma_start(out=ar_in[162:163, 0:128].rearrange("one (p c) -> (one p) c", c=1), in_=csd_sb[:]),
            g.dma_start(out=ar_in[162:163, 128:160].rearrange("one (p c) -> (one p) c", c=1), in_=csb_sb[0:32, :]),
        ], deps=[h_cse, h_z], ndma=2)

        h_xa_prev = None
        h_arw = h_csw
        for t in range(NA):
            tsl = slice(t * TA, (t + 1) * TA)
            h_xa = prog.stage("q", (lambda tsl_: lambda q: [
                q.dma_start(out=xa.rearrange("p (k c) -> p k c", c=TA),
                            in_=xc[:, tsl_].rearrange("(k p) c -> p k c", p=128)),
            ])(tsl), deps=[h_xa_prev], ndma=1)
            h_sq = prog.stage("s", (lambda: lambda s: s.activation(out=sqa, in_=xa, func=AF.Square))(),
                              deps=[h_xa])
            h_pa = None
            h_ve = None
            h_ve2 = None
            for u in range(4):
                def pa_u(p, u_=u):
                    b0, b1, b2, b3 = (0, 1, 2, 3) if u_ % 2 == 0 else (4, 5, 6, 7)
                    for k in range(4):
                        kc = slice(k * TA + u_ * 512, k * TA + (u_ + 1) * 512)
                        p.matmul(pbank[b0][0:1, :], ones_col[:], xa[:, kc],
                                 start=(k == 0), stop=(k == 3))
                    for k in range(4):
                        kc = slice(k * TA + u_ * 512, k * TA + (u_ + 1) * 512)
                        p.matmul(pbank[b1][0:1, :], ones_col[:], sqa[:, kc],
                                 start=(k == 0), stop=(k == 3))
                    for k in range(4):
                        kc = slice(k * TA + u_ * 512, k * TA + (u_ + 1) * 512)
                        p.matmul(pbank[b2][:, :], xpw_sb[:, k * 160:k * 160 + 128], xa[:, kc],
                                 start=(k == 0), stop=(k == 3))
                    for k in range(4):
                        kc = slice(k * TA + u_ * 512, k * TA + (u_ + 1) * 512)
                        p.matmul(pbank[b3][0:32, :], xpw_sb[:, k * 160 + 128:(k + 1) * 160], xa[:, kc],
                                 start=(k == 0), stop=(k == 3))
                prev2 = h_ve2
                h_pa = prog.stage("p", pa_u, deps=[h_sq, h_xa, prev2, h_cse])

                def ve_u(v, u_=u):
                    b0, b1, b2, b3 = (0, 1, 2, 3) if u_ % 2 == 0 else (4, 5, 6, 7)
                    cs = slice(u_ * 512, (u_ + 1) * 512)
                    v.tensor_copy(out=srow_x[:, cs], in_=pbank[b0][0:1, :])
                    v.tensor_copy(out=srow_q[:, cs], in_=pbank[b1][0:1, :])
                    v.tensor_copy(out=xpd_st[:, cs], in_=pbank[b2][:, :])
                    v.tensor_copy(out=xpbc_st[:, cs], in_=pbank[b3][0:32, :])
                h_ve2 = h_ve
                h_ve = prog.stage("v", (ve_u), deps=[h_pa, h_arw])
            h_arw = prog.stage("g", (lambda tsl_: lambda g: [
                g.dma_start(out=ar_in[0:128, tsl_], in_=xpd_st),
                g.dma_start(out=ar_in[128:160, tsl_], in_=xpbc_st),
                g.dma_start(out=ar_in[160:161, tsl_], in_=srow_x),
                g.dma_start(out=ar_in[161:162, tsl_], in_=srow_q),
            ])(tsl), deps=[h_ve], ndma=4)
            h_xa_prev = h_ve

        csem = ctx.enter_context(nc.semaphore("sem_c"))

        def ar_emit(g):
            g.collective_compute("AllReduce", ALU.add, replica_groups=RG,
                                 ins=[ar_in], outs=[ar_out]).then_inc(csem, 1)
        h_arq = prog.stage("g", ar_emit, deps=[h_arw, h_csw], raw=True)
        ncc["c"] += 1
        AR_DONE = ("c", ncc["c"])

        h_strd = prog.stage("q", lambda q: [
            q.dma_start(out=stt_a[:], in_=ar_out[160:161, :].rearrange("one (p c) -> (one p) c", c=128)),
            q.dma_start(out=stt_b[:], in_=ar_out[161:162, :].rearrange("one (p c) -> (one p) c", c=128)),
            q.dma_start(out=csd_sb[:], in_=ar_out[162:163, 0:128].rearrange("one (p c) -> (one p) c", c=1)),
            q.dma_start(out=csb_sb[0:32, :], in_=ar_out[162:163, 128:160].rearrange("one (p c) -> (one p) c", c=1)),
        ], deps=[AR_DONE, h_cse], ndma=4)

        def stats_emit(v):
            v.tensor_scalar_mul(stt_a[:], stt_a[:], 1.0 / D)
            v.tensor_scalar(stt_b[:], stt_b[:], 1.0 / D, EPS, op0=ALU.mult, op1=ALU.add)
            v.drain()
            v.tensor_tensor(out=stt_c[:], in0=stt_a[:], in1=stt_a[:], op=ALU.mult)
            v.drain()
            v.tensor_tensor(out=stt_c[:], in0=stt_b[:], in1=stt_c[:], op=ALU.subtract)
            v.drain()
            v.reciprocal(out=stt_d[:], in_=stt_c[:])
        h_st1 = prog.stage("v", stats_emit, deps=[h_strd])
        h_st2 = prog.stage("s", lambda s: s.activation(out=stt_b[:], in_=stt_d[:], func=AF.Sqrt),
                           deps=[h_st1])
        h_st3 = prog.stage("v", lambda v: v.tensor_tensor(out=stt_c[:], in0=stt_a[:], in1=stt_b[:],
                                                          op=ALU.mult), deps=[h_st2])
        h_mw = prog.stage("g", lambda g: [
            g.dma_start(out=mrows[0:1, :].rearrange("one (p c) -> (one p) c", c=128), in_=stt_a[:]),
            g.dma_start(out=mrows[1:2, :].rearrange("one (p c) -> (one p) c", c=128), in_=stt_b[:]),
            g.dma_start(out=mrows[2:3, :].rearrange("one (p c) -> (one p) c", c=128), in_=stt_c[:]),
        ], deps=[h_st3], ndma=3)

        apos["o"] = 0
        TB = 512
        NB = TTOK // TB
        NCH = 4
        xbp = [[aslc(TB) for _ in range(NCH)] for _ in range(2)]
        dtp = [[aslc(TB) for _ in range(NCH)] for _ in range(2)]
        dtxp = [[aslc(TB) for _ in range(NCH)] for _ in range(2)]
        xlnb = [aslc(TB) for _ in range(NCH)]
        hbt = xlnb
        seedb = [aslc(TB) for _ in range(NCH)]
        dAn = [[aslc(TB) for _ in range(NCH)] for _ in range(2)]
        dBxb = [aslc(TB) for _ in range(NCH)]
        qbf = [[aslc(TB, 128, BF16) for _ in range(NCH)] for _ in range(2)]
        xpdtp = [aslc(TB) for _ in range(2)]
        xpdcp = [aslc(TB) for _ in range(2)]
        xpbtp = [aslc(TB, 32) for _ in range(2)]
        xpbcp = [aslc(TB, 32, BF16) for _ in range(2)]
        t1s = aslc(TB)
        t2s_ = aslc(TB)
        mrowp = [aslc(TB, 1) for _ in range(2)]
        rrowp = [aslc(TB, 1) for _ in range(2)]
        mrrowp = [aslc(TB, 1) for _ in range(2)]

        h_scan_hist = []
        h_yacc_hist = []
        h_scan_done = {}
        h_prep_done = {}
        h_dt_done = {}
        h_gy = {-1: h_mw}
        h_yev = {-1: None}
        h_corr = None
        h_sp = None
        ag_handles = []

        def sh(idx):
            return h_scan_hist[idx] if 0 <= idx < len(h_scan_hist) else None

        def yh(idx):
            return h_yacc_hist[idx] if 0 <= idx < len(h_yacc_hist) else None

        for t in range(NB):
            par = t % 2
            tsl = slice(t * TB, (t + 1) * TB)
            h_ld = prog.stage("q", (lambda tsl_, par_: lambda q: [
                q.dma_start(out=xbp[par_][i], in_=xc[128 * i:128 * (i + 1), tsl_])
                for i in range(NCH)] + [
                q.dma_start(out=xpdtp[par_], in_=ar_out[0:128, tsl_]),
                q.dma_start(out=xpbtp[par_], in_=ar_out[128:160, tsl_]),
                q.dma_start(out=mrowp[par_][:], in_=mrows[0:1, tsl_]),
                q.dma_start(out=rrowp[par_][:], in_=mrows[1:2, tsl_]),
                q.dma_start(out=mrrowp[par_][:], in_=mrows[2:3, tsl_]),
            ])(tsl, par), deps=[h_prep_done.get(t - 2), h_scan_done.get(t - 2),
                                h_dt_done.get(t - 2), AR_DONE, h_mw], ndma=NCH + 5)

            def bcrm_emit(p, par_=par):
                p.matmul(pbank[6][:, :], ones_row[:], rrowp[par_][:], start=True, stop=True)
                p.matmul(pbank[7][:, :], ones_row[:], mrrowp[par_][:], start=True, stop=True)
            h_bcrm = prog.stage("p", bcrm_emit, deps=[h_ld, h_scan_done.get(t - 1), h_ones])

            def corr_emit(v, par_=par):
                v.tensor_tensor(out=t1s, in0=xpdtp[par_], in1=pbank[6][:, :], op=ALU.mult)
                v.tensor_scalar_mul(t2s_, pbank[7][:, :], csd_sb[:, 0:1])
                v.drain()
                v.tensor_tensor(out=xpdcp[par_], in0=t1s, in1=t2s_, op=ALU.subtract)
                v.drain()
                v.tensor_tensor(out=t1s[0:32, :], in0=xpbtp[par_], in1=pbank[6][0:32, :], op=ALU.mult)
                v.tensor_scalar_mul(t2s_[0:32, :], pbank[7][0:32, :], csb_sb[:, 0:1])
                v.drain()
                v.tensor_tensor(out=xpbcp[par_], in0=t1s[0:32, :], in1=t2s_[0:32, :], op=ALU.subtract)
            h_corr = prog.stage("v", corr_emit, deps=[h_bcrm, h_cse])

            h_pdp = None
            h_sp = None
            for i in range(NCH):
                def pd3(p, i_=i, par_=par):
                    bk = pbank[6] if i_ % 2 == 0 else pbank[7]
                    p.matmul(bk[:, :], dtw_sb[:, i_ * 128:(i_ + 1) * 128], xpdcp[par_],
                             start=True, stop=True)
                h_pdp2 = prog.stage("p", pd3, deps=[h_corr, h_sp, h_w0])

                def sp3(s, i_=i, par_=par):
                    bk = pbank[6] if i_ % 2 == 0 else pbank[7]
                    s.activation(out=dtp[par_][i_], in_=bk[:, :], func=AF.Exp,
                                 bias=dtb_sb[:, i_:i_ + 1])
                    s.drain()
                    s.activation(out=dtp[par_][i_], in_=dtp[par_][i_], func=AF.Ln, bias=1.0)
                h_sp = prog.stage("s", sp3, deps=[h_pdp, h_pdp2])
                h_pdp = h_pdp2
            h_dt_done[t] = h_sp

            def bcmr_emit(p, par_=par):
                p.matmul(pbank[4][:, :], ones_row[:], mrowp[par_][:], start=True, stop=True)
                p.matmul(pbank[5][:, :], ones_row[:], rrowp[par_][:], start=True, stop=True)
            h_bcmr = prog.stage("p", bcmr_emit, deps=[h_ld, h_scan_done.get(t - 1), h_ones])

            def xln_emit(v, par_=par):
                for i in range(NCH):
                    v.tensor_tensor(out=dBxb[i], in0=xbp[par_][i], in1=pbank[4][:, :], op=ALU.subtract)
                v.drain()
                for i in range(NCH):
                    v.tensor_tensor(out=xlnb[i], in0=dBxb[i], in1=pbank[5][:, :], op=ALU.mult)
            h_xln = prog.stage("v", xln_emit, deps=[h_bcmr, h_scan_done.get(t - 1),
                                                    h_yev.get(t - 1)])

            def prep_emit(v, par_=par):
                for i in range(NCH):
                    v.tensor_tensor(out=dtxp[par_][i], in0=dtp[par_][i], in1=xlnb[i], op=ALU.mult)
            h_prep = prog.stage("v", prep_emit, deps=[h_xln, h_sp])
            h_prep_done[t] = h_prep

            def seed_emit(s, par_=par):
                for i in range(NCH):
                    s.activation(out=seedb[i], in_=xlnb[i], func=AF.Copy, scale=d_sb[:, i:i + 1])
            h_seed = prog.stage("s", seed_emit, deps=[h_xln, h_sp, h_gy.get(t - 1), h_w0])

            h_yacc = None
            base = len(h_scan_hist)
            for n in range(16):
                np2 = n % 2
                bB, bC = (4, 5) if np2 == 0 else (6, 7)

                def bcb_emit(p, n_=n, bB_=bB, bC_=bC, par_=par):
                    p.matmul(pbank[bB_][:, :], sel_sb[:, n_ * 128:(n_ + 1) * 128],
                             xpbcp[par_], start=True, stop=True)
                    p.matmul(pbank[bC_][:, :], sel_sb[:, (16 + n_) * 128:(17 + n_) * 128],
                             xpbcp[par_], start=True, stop=True)
                bdeps = [h_corr, sh(base + n - 2), h_w0]
                if np2 == 0:
                    bdeps.append(h_xln if n == 0 else None)
                else:
                    bdeps.append(h_dt_done[t])
                h_pbB = prog.stage("p", bcb_emit, deps=bdeps)

                def dA_emit(s, n_=n, par_=par, np2_=np2):
                    for i in range(NCH):
                        s.activation(out=dAn[np2_][i], in_=dtp[par_][i], func=AF.Exp,
                                     scale=a_sb[:, i * 16 + n_:i * 16 + n_ + 1])
                h_sn = prog.stage("s", dA_emit, deps=[h_sp, h_a, sh(base + n - 2), h_seed])

                def scan_emit(v, n_=n, t_=t, par_=par, np2_=np2, bB_=bB, bC_=bC):
                    for i in range(NCH):
                        v.tensor_tensor(out=dBxb[i], in0=dtxp[par_][i], in1=pbank[bB_][:, :],
                                        op=ALU.mult)
                    if t_ == NB // 2:
                        for i in range(NCH):
                            v.memset(carry[:, i * 16 + n_:i * 16 + n_ + 1], 0.0)
                    v.drain()
                    for i in range(NCH):
                        col = slice(i * 16 + n_, i * 16 + n_ + 1)
                        v.tensor_tensor_scan(out=hbt[i], data0=dAn[np2_][i], data1=dBxb[i],
                                             initial=carry[:, col], op0=ALU.mult, op1=ALU.add)
                    v.drain()
                    for i in range(NCH):
                        col = slice(i * 16 + n_, i * 16 + n_ + 1)
                        v.tensor_copy(out=carry[:, col], in_=hbt[i][:, TB - 1:TB])
                    for i in range(NCH):
                        v.tensor_tensor(out=qbf[np2_][i], in0=hbt[i], in1=pbank[bC_][:, :],
                                        op=ALU.mult)
                sdeps = [h_pbB, h_sn, h_prep, yh(base + n - 2)]
                if n == 0:
                    sdeps.append(h_seed)
                h_vq = prog.stage("v", scan_emit, deps=sdeps)
                h_scan_hist.append(h_vq)

                def yacc_emit(p, n_=n, np2_=np2):
                    for i in range(NCH):
                        p.matmul(pbank[i][:, :], idb_sb[:], qbf[np2_][i],
                                 start=(n_ == 0), stop=(n_ == 15))
                h_yacc = prog.stage("p", yacc_emit, deps=[h_vq, h_yev.get(t - 1), h_w0])
                h_yacc_hist.append(h_yacc)
            h_scan_done[t] = h_vq

            def yev_emit(v):
                for i in range(NCH):
                    v.tensor_tensor(out=seedb[i], in0=pbank[i][:, :], in1=seedb[i], op=ALU.add)
            h_yev[t] = prog.stage("v", yev_emit, deps=[h_yacc, h_seed])

            j = t // 4
            co = (t % 4) * TB
            h_gy[t] = prog.stage("g", (lambda j_, co_: lambda g: [
                g.dma_start(out=ylocal[CSL * j_ + 128 * i:CSL * j_ + 128 * (i + 1), co_:co_ + TB],
                            in_=seedb[i]) for i in range(NCH)])(j, co),
                deps=[h_yev[t]], ndma=NCH)

            if t % 4 == 3:
                agdep = ag_handles[-1] if ag_handles else None

                def ag_emit(g, j_=j):
                    g.collective_compute("AllGather", ALU.bypass, replica_groups=RG,
                                         ins=[ylocal[CSL * j_:CSL * (j_ + 1), :]],
                                         outs=[ygath[D * j_:D * (j_ + 1), :]]).then_inc(csem, 1)
                prog.stage("g", ag_emit, deps=[h_gy[t], agdep], raw=True)
                ncc["c"] += 1
                ag_handles.append(("c", ncc["c"]))

        AG_ALL = ag_handles[-1]
        B_TAIL = [h_scan_done[NB - 1], h_prep_done[NB - 1], h_dt_done[NB - 1],
                  h_yev[NB - 1], h_gy[NB - 1]]

        apos["o"] = 0
        TC3 = 512
        NC3 = TSL // TC3
        NTI = TC3 // 128
        x3 = aslc(32 * TC3)
        xln3 = aslc(32 * TC3, 128, BF16)
        sqpp = [aslc(1024) for _ in range(2)]
        lg3 = sqpp[1][0:8, 0:512]
        wbc3 = sqpp[0][:, 0:512]
        wb3p = [aslc(32 * 128, 128, BF16) for _ in range(2)]
        aa3 = aslc(32 * 128, 128, BF16)
        ba3 = aslc(D, 128, BF16)
        evb = [aslc(TC3) for _ in range(2)]
        lgT3t = aslc(8 * NTI)
        w23t = aslc(8 * NTI)
        t2s3 = aslc(16 * NTI)
        sc3 = aslc(4 * NTI)
        w83 = aslc(TC3, 8)
        g3 = aslc(TC3, 128, BF16)

        ps_sx, ps_sq, ps_mb, ps_rb, ps_aux, ps_h, ps_o0, ps_o1 = pbank

        def w2_emit(q):
            return [
                q.dma_start(out=aa3.rearrange("p (k m) -> p k m", m=128), in_=_r3(aah, 128)),
                q.dma_start(out=ba3, in_=bah),
                q.dma_start(out=wr_sb[:].rearrange("p (k m) -> p k m", m=8), in_=_r3(wrh, 8)),
            ]
        h_w2 = prog.stage("q", w2_emit, deps=[AG_ALL, h_castall] + B_TAIL, ndma=3)

        ygv = ygath.rearrange("(b p) c -> p b c", p=128)
        h_out = {-1: None}
        h_ffn_done = None
        h_acc = None
        h_wb_used = [None, None]
        h_ev_hist = {}
        h_go_hist = {}
        for t in range(NC3):
            tsl = slice(t * TC3, (t + 1) * TC3)

            def ld3(q, tsl_=tsl):
                return [q.dma_start(out=_s3(x3, TC3), in_=_r3(xs[:, tsl_], TC3))]
            h_ld = prog.stage("q", ld3, deps=[h_ffn_done, h_out[t - 1], h_w2], ndma=1)

            def acc3(g, tsl_=tsl):
                if "cid_g" not in cidbox:
                    cidbox["cid_g"] = g.partition_id()
                cid = cidbox["cid_g"]
                return [g.dma_start(out=y3v, in_=ygv[:, bass.ts(cid, 32), tsl_],
                                    accum_op=ALU.add)
                        for y3v in [x3.rearrange("p (k c) -> p k c", c=TC3)]]
            h_acc = prog.stage("g", acc3, deps=[h_ld, AG_ALL, h_out[t - 1]], ndma=1)

            h_sqs = None
            h_sums = None
            h_sums2 = None
            for jp in range(16):
                def sq_emit(s, jp_=jp):
                    c = slice(jp_ * 2 * TC3, (jp_ * 2 + 2) * TC3)
                    s.activation(out=sqpp[jp_ % 2][:, 0:1024], in_=x3[:, c], func=AF.Square)
                h_sqs = prog.stage("s", sq_emit, deps=[h_acc, h_sums2, h_out[t - 1]])

                def sum_emit(p, jp_=jp):
                    for kk in range(2):
                        k = jp_ * 2 + kk
                        c = slice(k * TC3, (k + 1) * TC3)
                        p.matmul(ps_sx[0:1, :], ones_col[:], x3[:, c],
                                 start=(k == 0), stop=(k == 31))
                        p.matmul(ps_sq[0:1, :], ones_col[:],
                                 sqpp[jp_ % 2][:, kk * TC3:(kk + 1) * TC3],
                                 start=(k == 0), stop=(k == 31))
                h_sums2 = h_sums
                h_sums = prog.stage("p", sum_emit,
                                    deps=[h_sqs, h_ones, h_sums2] + h_wb_used)
            h_sum3 = h_sums

            def st31(v):
                v.tensor_scalar_mul(st[:, 0:512], ps_sx[0:1, :], 1.0 / D)
                v.tensor_scalar(st[:, 512:1024], ps_sq[0:1, :], 1.0 / D, EPS,
                                op0=ALU.mult, op1=ALU.add)
                v.drain()
                v.tensor_tensor(out=st[:, 1024:1536], in0=st[:, 0:512],
                                in1=st[:, 0:512], op=ALU.mult)
                v.drain()
                v.tensor_tensor(out=st[:, 1024:1536], in0=st[:, 512:1024],
                                in1=st[:, 1024:1536], op=ALU.subtract)
                v.drain()
                v.reciprocal(out=st[:, 512:1024], in_=st[:, 1024:1536])
            h_st31 = prog.stage("v", st31, deps=[h_sum3])
            h_st32 = prog.stage("s", lambda s: s.activation(out=st[:, 1536:2048],
                                                            in_=st[:, 512:1024], func=AF.Sqrt),
                                deps=[h_st31])

            def bc3(p):
                p.matmul(ps_mb[:, :], ones_row[:], st[0:1, 0:512], start=True, stop=True)
                p.matmul(ps_rb[:, :], ones_row[:], st[0:1, 1536:2048], start=True, stop=True)
            h_bc3 = prog.stage("p", bc3, deps=[h_st32, h_ones])

            def xln3_emit(v):
                for k in range(32):
                    c = slice(k * TC3, (k + 1) * TC3)
                    v.tensor_tensor(out=xln3[:, c], in0=x3[:, c], in1=ps_mb[:, :], op=ALU.subtract)
                v.drain()
                for k in range(32):
                    c = slice(k * TC3, (k + 1) * TC3)
                    v.tensor_tensor(out=xln3[:, c], in0=xln3[:, c], in1=ps_rb[:, :], op=ALU.mult)
            h_xln3 = prog.stage("v", xln3_emit, deps=[h_bc3])

            def lg_emit(p):
                for k in range(32):
                    c = slice(k * TC3, (k + 1) * TC3)
                    p.matmul(ps_aux[0:8, :], wr_sb[:, k * 8:(k + 1) * 8], xln3[:, c],
                             start=(k == 0), stop=(k == 31))
            h_lg = prog.stage("p", lg_emit, deps=[h_xln3, h_w2])
            h_lge = prog.stage("v", lambda v: v.tensor_copy(out=lg3, in_=ps_aux[0:8, :]),
                               deps=[h_lg])

            trb = (ps_mb, ps_rb, ps_o0, ps_o1)

            def tr_emit(p):
                for i in range(NTI):
                    p.transpose(trb[i][:, 0:8], lg3[:, i * 128:(i + 1) * 128], id8_sb[:])
            h_tr = prog.stage("p", tr_emit, deps=[h_lge, h_xln3] + h_wb_used)

            def tre_emit(v):
                for i in range(NTI):
                    v.tensor_copy(out=lgT3t[:, i * 8:(i + 1) * 8], in_=trb[i][:, 0:8])
            h_tre = prog.stage("v", tre_emit, deps=[h_tr])

            lgT3v = lgT3t.rearrange("p (i e) -> p i e", e=8)
            w23v = w23t.rearrange("p (i e) -> p i e", e=8)
            eq23 = t2s3[:, 0:8 * NTI].rearrange("p (i e) -> p i e", e=8)
            eq2b = t2s3[:, 8 * NTI:16 * NTI].rearrange("p (i e) -> p i e", e=8)
            m1 = sc3[:, 0 * NTI:1 * NTI]
            m2 = sc3[:, 1 * NTI:2 * NTI]
            dd = sc3[:, 2 * NTI:3 * NTI]
            e1 = sc3[:, 3 * NTI:4 * NTI]

            def top2a_emit(v):
                v.reduce_max(m1, lgT3v, axis=mybir.AxisListType.X)
                v.drain()
                a_, b_ = bass.broadcast_tensor_aps(lgT3v, m1[:, :, None])
                v.tensor_tensor(out=w23v, in0=a_, in1=b_, op=ALU.is_equal)
                v.drain()
                v.scalar_tensor_tensor(out=lgT3v, in0=w23v, scalar=-1e30, in1=lgT3v,
                                       op0=ALU.mult, op1=ALU.add)
                v.drain()
                v.reduce_max(m2, lgT3v, axis=mybir.AxisListType.X)
                v.drain()
                v.tensor_tensor(out=dd, in0=m1, in1=m2, op=ALU.subtract)
            h_t2a = prog.stage("v", top2a_emit, deps=[h_tre])

            h_sg = prog.stage("s", lambda s: s.activation(out=e1, in_=dd, func=AF.Sigmoid),
                              deps=[h_t2a])

            def top2b_emit(v):
                a_, b_ = bass.broadcast_tensor_aps(lgT3v, m2[:, :, None])
                v.tensor_tensor(out=eq23, in0=a_, in1=b_, op=ALU.is_equal)
                a_, b_ = bass.broadcast_tensor_aps(w23v, e1[:, :, None])
                v.tensor_tensor(out=w23v, in0=a_, in1=b_, op=ALU.mult)
                v.drain()
                a_, b_ = bass.broadcast_tensor_aps(eq23, e1[:, :, None])
                v.tensor_tensor(out=eq2b, in0=a_, in1=b_, op=ALU.mult)
                v.drain()
                v.tensor_tensor(out=eq23, in0=eq23, in1=eq2b, op=ALU.subtract)
                v.drain()
                v.tensor_tensor(out=w23t, in0=w23t, in1=t2s3[:, 0:8 * NTI], op=ALU.add)
            h_t2b = prog.stage("v", top2b_emit, deps=[h_sg])

            def trw_emit(p):
                for i in range(NTI):
                    p.transpose(trb[i][0:8, 0:128], w23t[:, i * 8:(i + 1) * 8], idf_sb[:])
            h_trw = prog.stage("p", trw_emit, deps=[h_t2b, h_tre])

            def w8_emit(v):
                for i in range(NTI):
                    v.tensor_copy(out=w83[:, i * 128:(i + 1) * 128], in_=trb[i][0:8, 0:128])
            h_w8 = prog.stage("v", w8_emit, deps=[h_trw])

            def hall_emit(p):
                for k in range(32):
                    c = slice(k * TC3, (k + 1) * TC3)
                    p.matmul(ps_h[:, :], aa3[:, k * 128:(k + 1) * 128], xln3[:, c],
                             start=(k == 0), stop=(k == 31))
                p.matmul(ps_aux[:, :], e8_sb[:], w83, start=True, stop=True)
            h_hall = prog.stage("p", hall_emit, deps=[h_xln3, h_w8, h_w2, h_lge])

            def g_emit(v):
                v.tensor_copy(out=wbc3, in_=ps_aux[:, :])
                v.drain()
                v.scalar_tensor_tensor(out=g3, in0=ps_h[:, :], scalar=2.0, in1=wbc3,
                                       op0=ALU.mult, op1=ALU.mult)
            h_g = prog.stage("v", g_emit, deps=[h_hall])

            h_ev = None
            h_mm = None
            for ofb in range(32):
                wpar = ofb % 2
                h_wb = prog.stage("q", (lambda ofb_, wpar_: lambda q: [
                    q.dma_start(out=wb3p[wpar_].rearrange("p (k c) -> p k c", c=128),
                                in_=_r3(wbh[:, ofb_ * 128:(ofb_ + 1) * 128], 128)),
                ])(ofb, wpar), deps=[h_xln3, h_wb_used[wpar], h_castall], ndma=1)

                def mm_emit(p, ofb_=ofb, wpar_=wpar):
                    pso = ps_o0 if ofb_ % 2 == 0 else ps_o1
                    for k in range(32):
                        c = slice(k * TC3, (k + 1) * TC3)
                        p.matmul(pso[:, :], wb3p[wpar_][:, k * 128:(k + 1) * 128],
                                 xln3[:, c], start=(k == 0), stop=False)
                    p.matmul(pso[:, :], ba3[:, ofb_ * 128:(ofb_ + 1) * 128], g3,
                             start=False, stop=True)
                h_mm = prog.stage("p", mm_emit, deps=[h_wb, h_g, h_ev_hist.get(ofb - 2),
                                                      h_trw])
                h_wb_used[wpar] = h_mm

                def ev_emit(v, ofb_=ofb):
                    pso = ps_o0 if ofb_ % 2 == 0 else ps_o1
                    c = slice(ofb_ * TC3, (ofb_ + 1) * TC3)
                    v.scalar_tensor_tensor(out=evb[ofb_ % 2], in0=pso[:, :],
                                           scalar=bb_sb[:, ofb_:ofb_ + 1],
                                           in1=x3[:, c], op0=ALU.add, op1=ALU.add)
                h_ev = prog.stage("v", ev_emit, deps=[h_mm, h_go_hist.get(ofb - 2)])
                h_ev_hist[ofb] = h_ev

                h_go = prog.stage("g", (lambda ofb_, tsl_: lambda g: [
                    g.dma_start(out=outs[ofb_ * 128:(ofb_ + 1) * 128, tsl_],
                                in_=evb[ofb_ % 2]),
                ])(ofb, tsl), deps=[h_ev], ndma=1)
                h_go_hist[ofb] = h_go
            h_ffn_done = h_ev
            h_out[t] = h_go
            h_ev_hist = {}
            h_go_hist = {}

        _run(nc, prog, ctx, [h_out[NC3 - 1]], extra_sems={"c": csem})
    return nc



_CACHE = {}


def _get(name, builder):
    if name not in _CACHE:
        _CACHE[name] = builder()
    return _CACHE[name]


def kernel(L0_attn_x, L0_attn_A_log, L0_attn_D, L0_attn_x_proj_w, L0_attn_dt_proj_w,
           L0_attn_dt_proj_b, L0_ffn_W_base, L0_ffn_b_base, L0_ffn_W_router,
           L0_ffn_A, L0_ffn_B):
    f32 = np.float32
    xT = np.ascontiguousarray(np.asarray(L0_attn_x, f32).reshape(TTOK, D).T)
    xpwT = np.ascontiguousarray(np.asarray(L0_attn_x_proj_w, f32).T)
    dtwT = np.ascontiguousarray(np.asarray(L0_attn_dt_proj_w, f32).T)
    dtb = np.ascontiguousarray(np.asarray(L0_attn_dt_proj_b, f32).reshape(32, 128).T)
    alog = np.ascontiguousarray(np.asarray(L0_attn_A_log, f32))
    dvec = np.asarray(L0_attn_D, f32)
    wbT = np.ascontiguousarray(np.asarray(L0_ffn_W_base, f32).T)
    bb = np.ascontiguousarray(np.asarray(L0_ffn_b_base, f32).reshape(32, 128).T)
    wrT = np.ascontiguousarray(np.asarray(L0_ffn_W_router, f32).T)
    aaT = np.ascontiguousarray(np.asarray(L0_ffn_A, f32).reshape(128, D).T)
    baT = np.ascontiguousarray(np.asarray(L0_ffn_B, f32).transpose(0, 2, 1).reshape(128, D))
    import ml_dtypes
    identb = np.eye(128, dtype=ml_dtypes.bfloat16)
    ident8 = np.eye(8, dtype=f32)
    identf = np.eye(128, dtype=f32)
    e8m = np.repeat(np.eye(8, dtype=f32), 16, axis=1)
    sel32 = np.zeros((32, 32 * 128), dtype=ml_dtypes.bfloat16)
    for n in range(32):
        sel32[n, n * 128:(n + 1) * 128] = 1.0

    cores = list(range(NCORE))
    nc = _get("fused", build_fused)
    in_maps = []
    for c in cores:
        ch = slice(c * CSL, (c + 1) * CSL)
        in_maps.append(dict(
            xs=np.ascontiguousarray(xT[:, c * TSL:(c + 1) * TSL]),
            xc=np.ascontiguousarray(xT[ch]),
            xpw_my=np.ascontiguousarray(
                xpwT[ch].reshape(4, 128, 160).transpose(1, 0, 2).reshape(128, 640)),
            dtw_my=np.ascontiguousarray(dtwT[:, ch]),
            dtb_my=np.ascontiguousarray(dtb[:, 4 * c:4 * (c + 1)]),
            alog_my=np.ascontiguousarray(
                alog[ch].reshape(4, 128, 16).transpose(1, 0, 2).reshape(128, 64)),
            dvec_my=np.ascontiguousarray(dvec[ch].reshape(4, 128).T),
            wbT=wbT, bb=bb, wrT=wrT, aaT=aaT, baT=baT,
            identb=identb, ident8=ident8, identf=identf, e8m=e8m, sel32=sel32,
        ))
    r = run_bass_kernel_spmd(nc, in_maps, core_ids=cores)
    out_full = np.concatenate([r.results[c]["outs"] for c in cores], axis=1)
    return np.ascontiguousarray(out_full.T).reshape(2, SEQ, D).astype(np.float32)


# version 08
# speedup vs baseline: 1.2699x; 1.1438x over previous
import numpy as np
import concourse.bass as bass
import concourse.mybir as mybir
from concourse.bass_utils import run_bass_kernel_spmd
from contextlib import ExitStack

F32 = mybir.dt.float32
BF16 = mybir.dt.bfloat16
F32R = mybir.dt.float32r
AF = mybir.ActivationFunctionType
ALU = mybir.AluOpType

D = 4096
TTOK = 16384
NCORE = 8
TSL = TTOK // NCORE
CSL = D // NCORE
SEQ = 8192
EPS = 1e-5

ENGS = ("q", "v", "s", "p", "g")


class _LastInstRecorder:

    def __init__(self, engine):
        self._e = engine
        self.last = None

    def __getattr__(self, name):
        f = getattr(self._e, name)
        if not callable(f):
            return f

        def wrap(*a, **k):
            r = f(*a, **k)
            if hasattr(r, "then_inc"):
                self.last = r
            return r
        return wrap


class Prog:

    def __init__(self):
        self.ctr = {e: 0 for e in ENGS}
        self.stream = {e: [] for e in ENGS}

    def stage(self, eng, emit, deps=(), ndma=0, raw=False):
        deps = [d for d in deps if d is not None]
        if ndma and self.ctr[eng]:
            deps.append((eng, self.ctr[eng]))
        self.stream[eng].append((list(deps), emit, "raw" if raw else ndma))
        if raw:
            return None
        if ndma:
            self.ctr[eng] += 16 * ndma
        else:
            self.ctr[eng] += 1
        return (eng, self.ctr[eng])

    def replay(self, eng_key, engine, sems):
        rec = _LastInstRecorder(engine)
        for deps, emit, ndma in self.stream[eng_key]:
            for (sk, val) in deps:
                engine.wait_ge(sems[sk], val)
            if ndma == "raw":
                emit(engine)
            elif ndma:
                insts = emit(engine)
                assert len(insts) == ndma, (eng_key, len(insts), ndma)
                for inst in insts:
                    inst.then_inc(sems[eng_key], 16)
            else:
                rec.last = None
                emit(rec)
                if rec.last is not None:
                    rec.last.then_inc(sems[eng_key], 1)
                else:
                    engine.drain().then_inc(sems[eng_key], 1)


def _run(nc, prog, ctx, final_deps, extra_sems=None):
    sems = {e: ctx.enter_context(nc.semaphore("sem_" + e)) for e in ENGS}
    if extra_sems:
        sems.update(extra_sems)
    block = ctx.enter_context(nc.Block())

    @block.sync
    def _(sync):
        prog.replay("q", sync, sems)

    @block.vector
    def _(vector):
        prog.replay("v", vector, sems)

    @block.scalar
    def _(scalar):
        prog.replay("s", scalar, sems)

    @block.tensor
    def _(tensor):
        prog.replay("p", tensor, sems)

    @block.gpsimd
    def _(gpsimd):
        prog.replay("g", gpsimd, sems)
        for (sk, val) in final_deps:
            gpsimd.wait_ge(sems[sk], val)



def _r3(dram_ap, inner):
    return dram_ap.rearrange("(k p) c -> p k c", p=128)


def _s3(sb_ap, inner):
    return sb_ap.rearrange("p (k c) -> p k c", c=inner)


def ln_stats(prog, sq_tile, x_tile, ones_col, st, ps_a, ps_b, nk, TC, deps, h_ones):
    h_sq = prog.stage("s", lambda s: s.activation(out=sq_tile[:], in_=x_tile[:], func=AF.Square),
                      deps=deps)

    def sums(p):
        for k in range(nk):
            c = slice(k * TC, (k + 1) * TC)
            p.matmul(ps_a[0:1, :], ones_col[:], x_tile[:, c], start=(k == 0), stop=(k == nk - 1))
        for k in range(nk):
            c = slice(k * TC, (k + 1) * TC)
            p.matmul(ps_b[0:1, :], ones_col[:], sq_tile[:, c], start=(k == 0), stop=(k == nk - 1))
    h_sum = prog.stage("p", sums, deps=list(deps) + [h_sq, h_ones])

    def stats1(v):
        v.tensor_scalar_mul(st[:, 0 * TC:1 * TC], ps_a[0:1, :], 1.0 / D)
        v.tensor_scalar(st[:, 1 * TC:2 * TC], ps_b[0:1, :], 1.0 / D, EPS,
                        op0=ALU.mult, op1=ALU.add)
        v.drain()
        v.tensor_tensor(out=st[:, 2 * TC:3 * TC], in0=st[:, 0 * TC:1 * TC],
                        in1=st[:, 0 * TC:1 * TC], op=ALU.mult)
        v.drain()
        v.tensor_tensor(out=st[:, 2 * TC:3 * TC], in0=st[:, 1 * TC:2 * TC],
                        in1=st[:, 2 * TC:3 * TC], op=ALU.subtract)
        v.drain()
        v.reciprocal(out=st[:, 3 * TC:4 * TC], in_=st[:, 2 * TC:3 * TC])
    h_st1 = prog.stage("v", stats1, deps=[h_sum])

    h_rt = prog.stage("s", lambda s: s.activation(out=st[:, 4 * TC:5 * TC],
                                                  in_=st[:, 3 * TC:4 * TC], func=AF.Sqrt),
                      deps=[h_st1])
    return h_rt


def bcast_mr(prog, ones_row, st, ps_m, ps_r, TC, deps, h_ones):
    def bc(p):
        p.matmul(ps_m[:, 0:TC], ones_row[:], st[0:1, 0 * TC:1 * TC], start=True, stop=True)
        p.matmul(ps_r[:, 0:TC], ones_row[:], st[0:1, 4 * TC:5 * TC], start=True, stop=True)
    return prog.stage("p", bc, deps=list(deps) + [h_ones])



def build_l1():
    nc = bass.Bass("TRN2", target_bir_lowering=False)
    xs = nc.dram_tensor("xs", [D, TSL], F32, kind="ExternalInput").ap()
    xpwT = nc.dram_tensor("xpwT", [D, 160], F32, kind="ExternalInput").ap()
    dtwT = nc.dram_tensor("dtwT", [128, D], F32, kind="ExternalInput").ap()
    dtb = nc.dram_tensor("dtb", [128, 32], F32, kind="ExternalInput").ap()
    dts = nc.dram_tensor("dts", [D, TSL], F32, kind="ExternalOutput").ap()
    xlns = nc.dram_tensor("xlns", [D, TSL], F32, kind="ExternalOutput").ap()
    bcs = nc.dram_tensor("bcs", [32, TSL], F32, kind="ExternalOutput").ap()

    TC = 256
    NT = TSL // TC
    prog = Prog()
    with ExitStack() as ctx:
        sb = lambda n_, sh, dt=F32: ctx.enter_context(nc.sbuf_tensor(n_, sh, dt))
        x_sb = sb("x_sb", [128, 32 * TC])
        sq_sb = sb("sq_sb", [128, 32 * TC])
        xln_sb = sb("xln_sb", [128, 32 * TC])
        dt_sb = sb("dt_sb", [128, 32 * TC])
        dtin_sb = sb("dtin_sb", [128, TC])
        bc_sb = sb("bc_sb", [32, TC])
        xpw_sb = sb("xpw_sb", [128, 32 * 160])
        dtw_sb = sb("dtw_sb", [128, 32 * 128])
        dtb_sb = sb("dtb_sb", [128, 32])
        ones_col = sb("ones_col", [128, 1])
        ones_row = sb("ones_row", [1, 128])
        st = sb("st", [1, 8 * TC])
        pb = [ctx.enter_context(nc.psum_tensor(f"pb{i}", [128, TC], F32)) for i in range(8)]
        ps_sx, ps_sq, ps_mb, ps_rb, ps_dt, ps_bc, ps_d0, ps_d1 = pb

        h_ones = prog.stage("v", lambda v: (v.memset(ones_col[:], 1.0), v.memset(ones_row[:], 1.0)))
        h_w = prog.stage("q", lambda q: [
            q.dma_start(out=_s3(xpw_sb[:], 160), in_=_r3(xpwT, 160)),
            q.dma_start(out=dtw_sb[:], in_=dtwT),
            q.dma_start(out=dtb_sb[:], in_=dtb),
        ], ndma=3)

        h_out = None
        h_xln_prev = None
        for t in range(NT):
            tsl = slice(t * TC, (t + 1) * TC)
            h_xld = prog.stage("q", (lambda tsl_: lambda q: [
                q.dma_start(out=_s3(x_sb[:], TC), in_=_r3(xs[:, tsl_], TC))
            ])(tsl), deps=[h_xln_prev], ndma=1)

            h_rt = ln_stats(prog, sq_sb, x_sb, ones_col, st, ps_sx, ps_sq, 32, TC,
                            deps=[h_xld], h_ones=h_ones)
            h_bc = bcast_mr(prog, ones_row, st, ps_mb, ps_rb, TC, deps=[h_rt], h_ones=h_ones)

            def xln_emit(v):
                for k in range(32):
                    c = slice(k * TC, (k + 1) * TC)
                    v.tensor_tensor(out=sq_sb[:, c], in0=x_sb[:, c], in1=ps_mb[:], op=ALU.subtract)
                v.drain()
                for k in range(32):
                    c = slice(k * TC, (k + 1) * TC)
                    v.tensor_tensor(out=xln_sb[:, c], in0=sq_sb[:, c], in1=ps_rb[:], op=ALU.mult)
            h_xln = prog.stage("v", xln_emit, deps=[h_bc, h_out])
            h_xln_prev = h_xln

            def xp_emit(p):
                for k in range(32):
                    c = slice(k * TC, (k + 1) * TC)
                    p.matmul(ps_dt[:], xpw_sb[:, k * 160:k * 160 + 128], xln_sb[:, c],
                             start=(k == 0), stop=(k == 31))
                for k in range(32):
                    c = slice(k * TC, (k + 1) * TC)
                    p.matmul(ps_bc[0:32, :], xpw_sb[:, k * 160 + 128:(k + 1) * 160], xln_sb[:, c],
                             start=(k == 0), stop=(k == 31))
            h_xp = prog.stage("p", xp_emit, deps=[h_xln, h_w])

            def evac_emit(v):
                v.tensor_copy(out=dtin_sb[:], in_=ps_dt[:])
                v.tensor_copy(out=bc_sb[:], in_=ps_bc[0:32, :])
            h_ev = prog.stage("v", evac_emit, deps=[h_xp, h_out])

            h_pe = None
            h_sp = None
            for gidx in range(16):
                def pe_emit(p, gi=gidx):
                    for j in range(2):
                        dm = gi * 2 + j
                        psd = ps_d0 if j == 0 else ps_d1
                        p.matmul(psd[:], dtw_sb[:, dm * 128:(dm + 1) * 128], dtin_sb[:],
                                 start=True, stop=True)
                h_pe = prog.stage("p", pe_emit, deps=[h_ev, h_sp])

                def sp_emit(s, gi=gidx):
                    for j in range(2):
                        dm = gi * 2 + j
                        c = slice(dm * TC, (dm + 1) * TC)
                        psd = ps_d0 if j == 0 else ps_d1
                        s.activation(out=dt_sb[:, c], in_=psd[:], func=AF.Exp,
                                     bias=dtb_sb[:, dm:dm + 1])
                        s.drain()
                        s.activation(out=dt_sb[:, c], in_=dt_sb[:, c], func=AF.Ln, bias=1.0)
                h_sp = prog.stage("s", sp_emit, deps=[h_pe, h_out])

            h_out = prog.stage("g", (lambda tsl_: lambda g: [
                g.dma_start(out=_r3(dts[:, tsl_], TC), in_=_s3(dt_sb[:], TC)),
                g.dma_start(out=_r3(xlns[:, tsl_], TC), in_=_s3(xln_sb[:], TC)),
                g.dma_start(out=bcs[:, tsl_], in_=bc_sb[:]),
            ])(tsl), deps=[h_sp, h_xln, h_ev], ndma=3)

        _run(nc, prog, ctx, [h_out])
    return nc



def build_l2():
    nc = bass.Bass("TRN2", target_bir_lowering=False)
    dtc = nc.dram_tensor("dtc", [CSL, TTOK], F32, kind="ExternalInput").ap()
    xlnc = nc.dram_tensor("xlnc", [CSL, TTOK], F32, kind="ExternalInput").ap()
    bc = nc.dram_tensor("bc", [32, TTOK], F32, kind="ExternalInput").ap()
    sel16 = nc.dram_tensor("sel16", [16, 16 * 128], F32, kind="ExternalInput").ap()
    alog = nc.dram_tensor("alog", [128, (CSL // 128) * 16], F32, kind="ExternalInput").ap()
    dvec = nc.dram_tensor("dvec", [128, CSL // 128], F32, kind="ExternalInput").ap()
    yc = nc.dram_tensor("yc", [CSL, TTOK], F32, kind="ExternalOutput").ap()

    TC = 1024
    NT = TTOK // TC
    NCH = CSL // 128
    NSUB = TC // 512
    prog = Prog()
    with ExitStack() as ctx:
        sb = lambda n_, sh, dt=F32: ctx.enter_context(nc.sbuf_tensor(n_, sh, dt))
        dt_sb = [sb(f"dt{i}", [128, TC]) for i in range(NCH)]
        xln_sb = [sb(f"xln{i}", [128, TC]) for i in range(NCH)]
        dtx_sb = [sb(f"dtx{i}", [128, TC]) for i in range(NCH)]
        y_sb = [sb(f"y{i}", [128, TC]) for i in range(NCH)]
        dA_sb = [sb(f"dA{i}", [128, TC]) for i in range(NCH)]
        dBx_sb = [sb(f"dBx{i}", [128, TC]) for i in range(NCH)]
        h_sb = [sb(f"h{i}", [128, TC]) for i in range(NCH)]
        b_sb = sb("b_sb", [16, TC])
        c_sb = sb("c_sb", [16, TC])
        sel_sb = sb("sel_sb", [16, 16 * 128])
        a_sb = sb("a_sb", [128, NCH * 16])
        d_sb = sb("d_sb", [128, NCH])
        carry = sb("carry", [128, NCH * 16])
        ones_row = sb("ones_row", [1, 128])
        ps_b = [ctx.enter_context(nc.psum_tensor(f"psb{i}", [128, 512], F32)) for i in range(NSUB)]
        ps_c = [ctx.enter_context(nc.psum_tensor(f"psc{i}", [128, 512], F32)) for i in range(NSUB)]

        prog.stage("v", lambda v: (v.memset(ones_row[:], 1.0), v.memset(carry[:], 0.0)))
        h_cst = prog.stage("q", lambda q: [
            q.dma_start(out=a_sb[:], in_=alog),
            q.dma_start(out=d_sb[:], in_=dvec),
            q.dma_start(out=sel_sb[:], in_=sel16),
        ], ndma=3)
        h_a = prog.stage("s", lambda s: (s.activation(out=a_sb[:], in_=a_sb[:], func=AF.Exp),
                                         s.drain(),
                                         s.mul(out=a_sb[:], in_=a_sb[:], mul=-1.0)),
                         deps=[h_cst])

        h_y = None
        h_vq = None
        h_sn = None
        for t in range(NT):
            tsl = slice(t * TC, (t + 1) * TC)
            h_ld = prog.stage("q", (lambda tsl_: lambda q: [
                q.dma_start(out=dt_sb[i][:], in_=dtc[128 * i:128 * (i + 1), tsl_])
                for i in range(NCH)] + [
                q.dma_start(out=xln_sb[i][:], in_=xlnc[128 * i:128 * (i + 1), tsl_])
                for i in range(NCH)] + [
                q.dma_start(out=b_sb[:], in_=bc[0:16, tsl_]),
                q.dma_start(out=c_sb[:], in_=bc[16:32, tsl_]),
            ])(tsl), deps=[h_vq, h_sn, h_y], ndma=2 * NCH + 2)

            def prep_emit(v):
                for i in range(NCH):
                    v.tensor_tensor(out=dtx_sb[i][:], in0=dt_sb[i][:], in1=xln_sb[i][:], op=ALU.mult)
                for i in range(NCH):
                    v.tensor_scalar_mul(y_sb[i][:], xln_sb[i][:], d_sb[:, i:i + 1])
            h_prep = prog.stage("v", prep_emit, deps=[h_ld, h_cst, h_y])

            h_pb = None
            h_vq = h_prep
            for n in range(16):
                def bcst_emit(p, n_=n):
                    sel = sel_sb[:, n_ * 128:(n_ + 1) * 128]
                    for u in range(NSUB):
                        cs = slice(u * 512, (u + 1) * 512)
                        p.matmul(ps_b[u][:], sel, b_sb[:, cs], start=True, stop=True)
                        p.matmul(ps_c[u][:], sel, c_sb[:, cs], start=True, stop=True)
                h_pb = prog.stage("p", bcst_emit, deps=[h_ld, h_vq, h_cst])

                def dA_emit(s, n_=n):
                    for i in range(NCH):
                        s.activation(out=dA_sb[i][:], in_=dt_sb[i][:], func=AF.Exp,
                                     scale=a_sb[:, i * 16 + n_:i * 16 + n_ + 1])
                h_sn = prog.stage("s", dA_emit, deps=[h_ld, h_a, h_vq])

                def scan_emit(v, n_=n, t_=t):
                    for i in range(NCH):
                        for u in range(NSUB):
                            cs = slice(u * 512, (u + 1) * 512)
                            v.tensor_tensor(out=dBx_sb[i][:, cs], in0=dtx_sb[i][:, cs],
                                            in1=ps_b[u][:], op=ALU.mult)
                    if t_ == NT // 2:
                        for i in range(NCH):
                            v.memset(carry[:, i * 16 + n_:i * 16 + n_ + 1], 0.0)
                    v.drain()
                    for i in range(NCH):
                        col = slice(i * 16 + n_, i * 16 + n_ + 1)
                        v.tensor_tensor_scan(out=h_sb[i][:], data0=dA_sb[i][:], data1=dBx_sb[i][:],
                                             initial=carry[:, col], op0=ALU.mult, op1=ALU.add)
                    v.drain()
                    for i in range(NCH):
                        col = slice(i * 16 + n_, i * 16 + n_ + 1)
                        v.tensor_copy(out=carry[:, col], in_=h_sb[i][:, TC - 1:TC])
                    for i in range(NCH):
                        for u in range(NSUB):
                            cs = slice(u * 512, (u + 1) * 512)
                            v.tensor_tensor(out=dBx_sb[i][:, cs], in0=h_sb[i][:, cs],
                                            in1=ps_c[u][:], op=ALU.mult)
                    v.drain()
                    for i in range(NCH):
                        v.tensor_tensor(out=y_sb[i][:], in0=y_sb[i][:], in1=dBx_sb[i][:], op=ALU.add)
                h_vq = prog.stage("v", scan_emit, deps=[h_pb, h_sn])

            h_y = prog.stage("g", (lambda tsl_: lambda g: [
                g.dma_start(out=yc[128 * i:128 * (i + 1), tsl_], in_=y_sb[i][:])
                for i in range(NCH)])(tsl), deps=[h_vq], ndma=NCH)

        _run(nc, prog, ctx, [h_y])
    return nc



def build_l3():
    nc = bass.Bass("TRN2", target_bir_lowering=False)
    xs = nc.dram_tensor("xs", [D, TSL], F32, kind="ExternalInput").ap()
    ys = nc.dram_tensor("ys", [D, TSL], F32, kind="ExternalInput").ap()
    wbT = nc.dram_tensor("wbT", [D, D], F32, kind="ExternalInput").ap()
    bb = nc.dram_tensor("bb", [128, 32], F32, kind="ExternalInput").ap()
    wrT = nc.dram_tensor("wrT", [D, 8], F32, kind="ExternalInput").ap()
    aaT = nc.dram_tensor("aaT", [D, 128], F32, kind="ExternalInput").ap()
    baT = nc.dram_tensor("baT", [128, D], F32, kind="ExternalInput").ap()
    ident = nc.dram_tensor("ident", [128, 128], F32, kind="ExternalInput").ap()
    e8m = nc.dram_tensor("e8m", [8, 128], F32, kind="ExternalInput").ap()
    outs = nc.dram_tensor("outs", [D, TSL], F32, kind="ExternalOutput").ap()

    TC = 256
    NT = TSL // TC
    NTI = TC // 128
    prog = Prog()
    with ExitStack() as ctx:
        sb = lambda n_, sh, dt=F32: ctx.enter_context(nc.sbuf_tensor(n_, sh, dt))
        x_sb = sb("x_sb", [128, 32 * TC])
        sq_sb = sb("sq_sb", [128, 32 * TC])
        y_sb = sb("y_sb", [128, 32 * TC])
        xln_sb = sb("xln_sb", [128, 32 * TC])
        wb_sb = sb("wb_sb", [128, 32 * 256])
        aa_sb = sb("aa_sb", [128, 32 * 128])
        wr_sb = sb("wr_sb", [128, 32 * 8])
        ba_sb = sb("ba_sb", [128, D])
        bb_sb = sb("bb_sb", [128, 32])
        id_sb = sb("id_sb", [128, 128])
        e8_sb = sb("e8_sb", [8, 128])
        lg_sb = sb("lg_sb", [8, TC])
        lgT_sb = sb("lgT_sb", [128, 8 * NTI])
        w2_sb = sb("w2_sb", [128, 8 * NTI])
        t2s = sb("t2s", [128, 16 * NTI])
        sc_sb = sb("sc_sb", [128, 4 * NTI])
        w8_sb = sb("w8_sb", [8, TC])
        g_sb = sb("g_sb", [128, TC])
        wbc_sb = sb("wbc_sb", [128, TC])
        ones_col = sb("ones_col", [128, 1])
        ones_row = sb("ones_row", [1, 128])
        st = sb("st", [1, 8 * TC])
        pb = [ctx.enter_context(nc.psum_tensor(f"pb{i}", [128, TC], F32)) for i in range(8)]
        ps_sx, ps_sq, ps_mb, ps_rb, ps_aux, ps_h, ps_o0, ps_o1 = pb

        h_ones = prog.stage("v", lambda v: (v.memset(ones_col[:], 1.0), v.memset(ones_row[:], 1.0)))
        h_w = prog.stage("q", lambda q: [
            q.dma_start(out=_s3(aa_sb[:], 128), in_=_r3(aaT, 128)),
            q.dma_start(out=_s3(wr_sb[:], 8), in_=_r3(wrT, 8)),
            q.dma_start(out=ba_sb[:], in_=baT),
            q.dma_start(out=bb_sb[:], in_=bb),
            q.dma_start(out=id_sb[:], in_=ident),
            q.dma_start(out=e8_sb[:], in_=e8m),
        ], ndma=6)

        h_out = None
        h_ffn_done = None
        h_wb_used = None
        for t in range(NT):
            tsl = slice(t * TC, (t + 1) * TC)
            h_ld = prog.stage("q", (lambda tsl_: lambda q: [
                q.dma_start(out=_s3(x_sb[:], TC), in_=_r3(xs[:, tsl_], TC)),
                q.dma_start(out=_s3(y_sb[:], TC), in_=_r3(ys[:, tsl_], TC)),
            ])(tsl), deps=[h_ffn_done, h_out], ndma=2)

            def res_emit(v):
                for k in range(32):
                    c = slice(k * TC, (k + 1) * TC)
                    v.tensor_tensor(out=x_sb[:, c], in0=x_sb[:, c], in1=y_sb[:, c], op=ALU.add)
            h_res = prog.stage("v", res_emit, deps=[h_ld])

            h_rt = ln_stats(prog, sq_sb, x_sb, ones_col, st, ps_sx, ps_sq, 32, TC,
                            deps=[h_res, h_wb_used], h_ones=h_ones)
            h_bc = bcast_mr(prog, ones_row, st, ps_mb, ps_rb, TC, deps=[h_rt], h_ones=h_ones)

            def xln_emit(v):
                for k in range(32):
                    c = slice(k * TC, (k + 1) * TC)
                    v.tensor_tensor(out=sq_sb[:, c], in0=x_sb[:, c], in1=ps_mb[:], op=ALU.subtract)
                v.drain()
                for k in range(32):
                    c = slice(k * TC, (k + 1) * TC)
                    v.tensor_tensor(out=xln_sb[:, c], in0=sq_sb[:, c], in1=ps_rb[:], op=ALU.mult)
            h_xln = prog.stage("v", xln_emit, deps=[h_bc])

            def lg_emit(p):
                for k in range(32):
                    c = slice(k * TC, (k + 1) * TC)
                    p.matmul(ps_aux[0:8, :], wr_sb[:, k * 8:(k + 1) * 8], xln_sb[:, c],
                             start=(k == 0), stop=(k == 31))
            h_lg = prog.stage("p", lg_emit, deps=[h_xln, h_w])
            h_lge = prog.stage("v", lambda v: v.tensor_copy(out=lg_sb[:], in_=ps_aux[0:8, :]),
                               deps=[h_lg])

            def tr_emit(p):
                for i in range(NTI):
                    dst = (ps_mb, ps_rb)[i]
                    p.transpose(dst[:, 0:8], lg_sb[:, i * 128:(i + 1) * 128], id_sb[0:8, 0:8])
            h_tr = prog.stage("p", tr_emit, deps=[h_lge, h_xln])

            def tre_emit(v):
                for i in range(NTI):
                    src = (ps_mb, ps_rb)[i]
                    v.tensor_copy(out=lgT_sb[:, i * 8:(i + 1) * 8], in_=src[:, 0:8])
            h_tre = prog.stage("v", tre_emit, deps=[h_tr])

            lgT3 = lgT_sb[:].rearrange("p (i e) -> p i e", e=8)
            w23 = w2_sb[:].rearrange("p (i e) -> p i e", e=8)
            eq23 = t2s[:, 0:8 * NTI].rearrange("p (i e) -> p i e", e=8)
            eq2b = t2s[:, 8 * NTI:16 * NTI].rearrange("p (i e) -> p i e", e=8)
            m1 = sc_sb[:, 0 * NTI:1 * NTI]
            m2 = sc_sb[:, 1 * NTI:2 * NTI]
            dd = sc_sb[:, 2 * NTI:3 * NTI]
            e1 = sc_sb[:, 3 * NTI:4 * NTI]

            def top2a_emit(v):
                v.reduce_max(m1, lgT3, axis=mybir.AxisListType.X)
                v.drain()
                a_, b_ = bass.broadcast_tensor_aps(lgT3, m1[:, :, None])
                v.tensor_tensor(out=w23, in0=a_, in1=b_, op=ALU.is_equal)
                v.drain()
                v.scalar_tensor_tensor(out=lgT3, in0=w23, scalar=-1e30, in1=lgT3,
                                       op0=ALU.mult, op1=ALU.add)
                v.drain()
                v.reduce_max(m2, lgT3, axis=mybir.AxisListType.X)
                v.drain()
                v.tensor_tensor(out=dd, in0=m1, in1=m2, op=ALU.subtract)
            h_t2a = prog.stage("v", top2a_emit, deps=[h_tre])

            h_sg = prog.stage("s", lambda s: s.activation(out=e1, in_=dd, func=AF.Sigmoid),
                              deps=[h_t2a])

            def top2b_emit(v):
                a_, b_ = bass.broadcast_tensor_aps(lgT3, m2[:, :, None])
                v.tensor_tensor(out=eq23, in0=a_, in1=b_, op=ALU.is_equal)
                a_, b_ = bass.broadcast_tensor_aps(w23, e1[:, :, None])
                v.tensor_tensor(out=w23, in0=a_, in1=b_, op=ALU.mult)
                v.drain()
                a_, b_ = bass.broadcast_tensor_aps(eq23, e1[:, :, None])
                v.tensor_tensor(out=eq2b, in0=a_, in1=b_, op=ALU.mult)
                v.drain()
                v.tensor_tensor(out=eq23, in0=eq23, in1=eq2b, op=ALU.subtract)
                v.drain()
                v.tensor_tensor(out=w2_sb[:], in0=w2_sb[:], in1=t2s[:, 0:8 * NTI], op=ALU.add)
            h_t2b = prog.stage("v", top2b_emit, deps=[h_sg])

            def trw_emit(p):
                for i in range(NTI):
                    dst = (ps_mb, ps_rb)[i]
                    p.transpose(dst[0:8, 0:128], w2_sb[:, i * 8:(i + 1) * 8], id_sb[:, :])
            h_trw = prog.stage("p", trw_emit, deps=[h_t2b, h_tre])

            def w8_emit(v):
                for i in range(NTI):
                    src = (ps_mb, ps_rb)[i]
                    v.tensor_copy(out=w8_sb[:, i * 128:(i + 1) * 128], in_=src[0:8, 0:128])
            h_w8 = prog.stage("v", w8_emit, deps=[h_trw])

            def hall_emit(p):
                for k in range(32):
                    c = slice(k * TC, (k + 1) * TC)
                    p.matmul(ps_h[:], aa_sb[:, k * 128:(k + 1) * 128], xln_sb[:, c],
                             start=(k == 0), stop=(k == 31))
                p.matmul(ps_aux[:], e8_sb[:], w8_sb[:], start=True, stop=True)
            h_hall = prog.stage("p", hall_emit, deps=[h_xln, h_w8, h_w, h_lge])

            def g_emit(v):
                v.tensor_copy(out=wbc_sb[:], in_=ps_aux[:])
                v.drain()
                v.scalar_tensor_tensor(out=g_sb[:], in0=ps_h[:], scalar=2.0, in1=wbc_sb[:],
                                       op0=ALU.mult, op1=ALU.mult)
            h_g = prog.stage("v", g_emit, deps=[h_hall])

            h_ev = None
            h_mm = None
            h_wb = None
            for ofb in range(16):
                prev_reader = h_mm
                h_wb = prog.stage("q", (lambda ofb_: lambda q: [
                    q.dma_start(out=_s3(wb_sb[:], 256), in_=_r3(wbT[:, ofb_ * 256:(ofb_ + 1) * 256], 256)),
                ])(ofb), deps=[h_xln, prev_reader], ndma=1)

                def mm_emit(p, ofb_=ofb):
                    for j in range(2):
                        pso = ps_o0 if j == 0 else ps_o1
                        ofc = ofb_ * 2 + j
                        for k in range(32):
                            c = slice(k * TC, (k + 1) * TC)
                            p.matmul(pso[:], wb_sb[:, (k * 2 + j) * 128:(k * 2 + j + 1) * 128],
                                     xln_sb[:, c], start=(k == 0), stop=False)
                        p.matmul(pso[:], ba_sb[:, ofc * 128:(ofc + 1) * 128], g_sb[:],
                                 start=False, stop=True)
                h_mm = prog.stage("p", mm_emit, deps=[h_wb, h_g, h_ev])

                def ev_emit(v, ofb_=ofb):
                    for j in range(2):
                        pso = ps_o0 if j == 0 else ps_o1
                        ofc = ofb_ * 2 + j
                        c = slice(ofc * TC, (ofc + 1) * TC)
                        v.scalar_tensor_tensor(out=y_sb[:, c], in0=pso[:],
                                               scalar=bb_sb[:, ofc:ofc + 1],
                                               in1=x_sb[:, c], op0=ALU.add, op1=ALU.add)
                h_ev = prog.stage("v", ev_emit, deps=[h_mm, h_out])
            h_ffn_done = h_ev
            h_wb_used = h_mm

            h_out = prog.stage("g", (lambda tsl_: lambda g: [
                g.dma_start(out=_r3(outs[:, tsl_], TC), in_=_s3(y_sb[:], TC)),
            ])(tsl), deps=[h_ffn_done], ndma=1)

        _run(nc, prog, ctx, [h_out])
    return nc



def build_fused():
    nc = bass.Bass("TRN2", target_bir_lowering=False)
    xs = nc.dram_tensor("xs", [D, TSL], F32, kind="ExternalInput").ap()
    xc = nc.dram_tensor("xc", [CSL, TTOK], F32, kind="ExternalInput").ap()
    xpw_my = nc.dram_tensor("xpw_my", [128, 4 * 160], F32, kind="ExternalInput").ap()
    dtw_my = nc.dram_tensor("dtw_my", [128, CSL], F32, kind="ExternalInput").ap()
    dtb_my = nc.dram_tensor("dtb_my", [128, 4], F32, kind="ExternalInput").ap()
    alog_my = nc.dram_tensor("alog_my", [128, 64], F32, kind="ExternalInput").ap()
    dvec_my = nc.dram_tensor("dvec_my", [128, 4], F32, kind="ExternalInput").ap()
    wbT = nc.dram_tensor("wbT", [D, D], F32, kind="ExternalInput").ap()
    bb = nc.dram_tensor("bb", [128, 32], F32, kind="ExternalInput").ap()
    wrT = nc.dram_tensor("wrT", [D, 8], F32, kind="ExternalInput").ap()
    aaT = nc.dram_tensor("aaT", [D, 128], F32, kind="ExternalInput").ap()
    baT = nc.dram_tensor("baT", [128, D], F32, kind="ExternalInput").ap()
    identb = nc.dram_tensor("identb", [128, 128], BF16, kind="ExternalInput").ap()
    ident8 = nc.dram_tensor("ident8", [8, 8], F32, kind="ExternalInput").ap()
    identf = nc.dram_tensor("identf", [128, 128], F32, kind="ExternalInput").ap()
    e8m = nc.dram_tensor("e8m", [8, 128], F32, kind="ExternalInput").ap()
    sel32 = nc.dram_tensor("sel32", [32, 32 * 128], BF16, kind="ExternalInput").ap()
    outs = nc.dram_tensor("outs", [D, TSL], F32, kind="ExternalOutput").ap()
    ar_in = nc.dram_tensor("ar_in", [163, TTOK], F32, kind="Internal").ap()
    ar_out = nc.dram_tensor("ar_out", [163, TTOK], F32, kind="Internal").ap()
    mrows = nc.dram_tensor("mrows", [3, TTOK], F32, kind="Internal").ap()
    ylocal = nc.dram_tensor("ylocal", [8 * CSL, TSL], F32, kind="Internal").ap()
    ygath = nc.dram_tensor("ygath", [8 * D, TSL], F32, kind="Internal").ap()
    wbh = nc.dram_tensor("wbh", [D, D], BF16, kind="Internal").ap()
    aah = nc.dram_tensor("aah", [D, 128], BF16, kind="Internal").ap()
    bah = nc.dram_tensor("bah", [128, D], BF16, kind="Internal").ap()
    wrh = nc.dram_tensor("wrh", [D, 8], BF16, kind="Internal").ap()

    RG = [[0, 1, 2, 3, 4, 5, 6, 7]]
    prog = Prog()
    ncc = {"c": 0}

    with ExitStack() as ctx:
        sb = lambda n_, sh, dt=F32: ctx.enter_context(nc.sbuf_tensor(n_, sh, dt))
        xpw_sb = sb("xpw_sb", [128, 4 * 160])
        dtw_sb = sb("dtw_sb", [128, CSL])
        dtb_sb = sb("dtb_sb", [128, 4])
        a_sb = sb("a_sb", [128, 64])
        d_sb = sb("d_sb", [128, 4])
        carry = sb("carry", [128, 64])
        wr_sb = sb("wr_sb", [128, 32 * 8], BF16)
        bb_sb = sb("bb_sb", [128, 32])
        idb_sb = sb("idb_sb", [128, 128], BF16)
        id8_sb = sb("id8_sb", [8, 8])
        idf_sb = sb("idf_sb", [128, 128])
        e8_sb = sb("e8_sb", [8, 128])
        sel_sb = sb("sel_sb", [32, 32 * 128], BF16)
        ones_col = sb("ones_col", [128, 1])
        ones_row = sb("ones_row", [1, 128])
        csd_sb = sb("csd_sb", [128, 1])
        csb_sb = sb("csb_sb", [32, 1])
        stt_a = sb("stt_a", [128, 128])
        stt_b = sb("stt_b", [128, 128])
        stt_c = sb("stt_c", [128, 128])
        stt_d = sb("stt_d", [128, 128])
        zrow = sb("zrow", [1, 2048])
        st = sb("st", [1, 8 * 256])
        cast_in = sb("cast_in", [128, 2048])
        cast_out = sb("cast_out", [128, 2048], BF16)
        ARN = 36864
        arena = sb("arena", [128, ARN])
        apos = {"o": 0}

        def aslc(cols, parts=128, dt=F32):
            w = cols if dt == F32 else (cols + 1) // 2
            o = apos["o"]
            apos["o"] = o + w
            assert apos["o"] <= ARN, (o, w)
            if dt == F32:
                return arena[0:parts, o:o + cols]
            return arena[0:parts, o:o + w].bitcast(dt)[0:parts, 0:cols]

        pbank = [ctx.enter_context(nc.psum_tensor(f"pb{i}", [128, 512], F32)) for i in range(8)]

        h_ones = prog.stage("v", lambda v: (v.memset(ones_col[:], 1.0),
                                            v.memset(ones_row[:], 1.0),
                                            v.memset(carry[:], 0.0),
                                            v.memset(zrow[:], 0.0)))
        cidbox = {}

        def w0_emit(q):
            cidbox["cid"] = q.partition_id()
            return [
                q.dma_start(out=xpw_sb[:], in_=xpw_my),
                q.dma_start(out=dtw_sb[:], in_=dtw_my),
                q.dma_start(out=dtb_sb[:], in_=dtb_my),
                q.dma_start(out=a_sb[:], in_=alog_my),
                q.dma_start(out=d_sb[:], in_=dvec_my),
                q.dma_start(out=bb_sb[:], in_=bb),
                q.dma_start(out=idb_sb[:], in_=identb),
                q.dma_start(out=id8_sb[:], in_=ident8),
                q.dma_start(out=idf_sb[:], in_=identf),
                q.dma_start(out=e8_sb[:], in_=e8m),
                q.dma_start(out=sel_sb[:], in_=sel32),
            ]
        h_w0 = prog.stage("q", w0_emit, ndma=11)
        h_a = prog.stage("s", lambda s: (s.activation(out=a_sb[:], in_=a_sb[:], func=AF.Exp),
                                         s.drain(),
                                         s.mul(out=a_sb[:], in_=a_sb[:], mul=-1.0)),
                         deps=[h_w0])

        def emit_cast(srcT, dstT, col0, ncols, dep, gdep):
            h1 = prog.stage("q", (lambda: lambda q: [
                q.dma_start(out=cast_in[:, 0:32 * ncols].rearrange("p (k c) -> p k c", c=ncols),
                            in_=_r3(srcT[:, col0:col0 + ncols], ncols)),
            ])(), deps=[dep], ndma=1)
            h2 = prog.stage("s", (lambda: lambda s: s.copy(
                out=cast_out[:, 0:32 * ncols], in_=cast_in[:, 0:32 * ncols]))(),
                deps=[h1, gdep])
            h3 = prog.stage("g", (lambda: lambda g: [
                g.dma_start(out=_r3(dstT[:, col0:col0 + ncols], ncols),
                            in_=cast_out[:, 0:32 * ncols].rearrange("p (k c) -> p k c", c=ncols)),
            ])(), deps=[h2], ndma=1)
            return h3

        h_cast = None
        for cb in range(64):
            h_cast = emit_cast(wbT, wbh, cb * 64, 64, h_cast, h_cast)
        for cb in range(2):
            h_cast = emit_cast(aaT, aah, cb * 64, 64, h_cast, h_cast)
        h_cast = emit_cast(wrT, wrh, 0, 8, h_cast, h_cast)
        for half in range(2):
            h1 = prog.stage("q", (lambda half_: lambda q: [
                q.dma_start(out=cast_in[:], in_=baT[:, half_ * 2048:(half_ + 1) * 2048])])(half),
                deps=[h_cast], ndma=1)
            h2 = prog.stage("s", lambda s: s.copy(out=cast_out[:], in_=cast_in[:]), deps=[h1])
            h_cast = prog.stage("g", (lambda half_: lambda g: [
                g.dma_start(out=bah[:, half_ * 2048:(half_ + 1) * 2048], in_=cast_out[:])])(half),
                deps=[h2], ndma=1)
        h_castall = h_cast

        TA = 2048
        NA = TTOK // TA
        xa = aslc(4 * TA)
        sqa = aslc(4 * TA)
        xpd_st = aslc(TA)
        xpbc_st = aslc(TA, 32)
        srow_x = aslc(TA, 1)
        srow_q = aslc(TA, 1)

        def cs_emit(p):
            for k in range(4):
                p.matmul(pbank[6][:, 0:1], xpw_sb[:, k * 160:k * 160 + 128], ones_col[:],
                         start=(k == 0), stop=(k == 3))
            for k in range(4):
                p.matmul(pbank[7][0:32, 0:1], xpw_sb[:, k * 160 + 128:(k + 1) * 160], ones_col[:],
                         start=(k == 0), stop=(k == 3))
        h_cs = prog.stage("p", cs_emit, deps=[h_w0, h_ones])
        h_cse = prog.stage("v", lambda v: (v.tensor_copy(out=csd_sb[:], in_=pbank[6][:, 0:1]),
                                           v.tensor_copy(out=csb_sb[:], in_=pbank[7][0:32, 0:1])),
                           deps=[h_cs])
        h_z = prog.stage("g", lambda g: [
            g.dma_start(out=ar_in[162:163, k * 2048:(k + 1) * 2048], in_=zrow[:])
            for k in range(8)], deps=[h_ones], ndma=8)
        h_csw = prog.stage("g", lambda g: [
            g.dma_start(out=ar_in[162:163, 0:128].rearrange("one (p c) -> (one p) c", c=1), in_=csd_sb[:]),
            g.dma_start(out=ar_in[162:163, 128:160].rearrange("one (p c) -> (one p) c", c=1), in_=csb_sb[0:32, :]),
        ], deps=[h_cse, h_z], ndma=2)

        h_xa_prev = None
        h_arw = h_csw
        for t in range(NA):
            tsl = slice(t * TA, (t + 1) * TA)
            h_xa = prog.stage("q", (lambda tsl_: lambda q: [
                q.dma_start(out=xa.rearrange("p (k c) -> p k c", c=TA),
                            in_=xc[:, tsl_].rearrange("(k p) c -> p k c", p=128)),
            ])(tsl), deps=[h_xa_prev], ndma=1)
            h_sq = prog.stage("s", (lambda: lambda s: s.activation(out=sqa, in_=xa, func=AF.Square))(),
                              deps=[h_xa])
            h_pa = None
            h_ve = None
            h_ve2 = None
            for u in range(4):
                def pa_u(p, u_=u):
                    b0, b1, b2, b3 = (0, 1, 2, 3) if u_ % 2 == 0 else (4, 5, 6, 7)
                    for k in range(4):
                        kc = slice(k * TA + u_ * 512, k * TA + (u_ + 1) * 512)
                        p.matmul(pbank[b0][0:1, :], ones_col[:], xa[:, kc],
                                 start=(k == 0), stop=(k == 3))
                    for k in range(4):
                        kc = slice(k * TA + u_ * 512, k * TA + (u_ + 1) * 512)
                        p.matmul(pbank[b1][0:1, :], ones_col[:], sqa[:, kc],
                                 start=(k == 0), stop=(k == 3))
                    for k in range(4):
                        kc = slice(k * TA + u_ * 512, k * TA + (u_ + 1) * 512)
                        p.matmul(pbank[b2][:, :], xpw_sb[:, k * 160:k * 160 + 128], xa[:, kc],
                                 start=(k == 0), stop=(k == 3))
                    for k in range(4):
                        kc = slice(k * TA + u_ * 512, k * TA + (u_ + 1) * 512)
                        p.matmul(pbank[b3][0:32, :], xpw_sb[:, k * 160 + 128:(k + 1) * 160], xa[:, kc],
                                 start=(k == 0), stop=(k == 3))
                prev2 = h_ve2
                h_pa = prog.stage("p", pa_u, deps=[h_sq, h_xa, prev2, h_cse])

                def ve_u(v, u_=u):
                    b0, b1, b2, b3 = (0, 1, 2, 3) if u_ % 2 == 0 else (4, 5, 6, 7)
                    cs = slice(u_ * 512, (u_ + 1) * 512)
                    v.tensor_copy(out=srow_x[:, cs], in_=pbank[b0][0:1, :])
                    v.tensor_copy(out=srow_q[:, cs], in_=pbank[b1][0:1, :])
                    v.tensor_copy(out=xpd_st[:, cs], in_=pbank[b2][:, :])
                    v.tensor_copy(out=xpbc_st[:, cs], in_=pbank[b3][0:32, :])
                h_ve2 = h_ve
                h_ve = prog.stage("v", (ve_u), deps=[h_pa, h_arw])
            h_arw = prog.stage("g", (lambda tsl_: lambda g: [
                g.dma_start(out=ar_in[0:128, tsl_], in_=xpd_st),
                g.dma_start(out=ar_in[128:160, tsl_], in_=xpbc_st),
                g.dma_start(out=ar_in[160:161, tsl_], in_=srow_x),
                g.dma_start(out=ar_in[161:162, tsl_], in_=srow_q),
            ])(tsl), deps=[h_ve], ndma=4)
            h_xa_prev = h_ve

        csem = ctx.enter_context(nc.semaphore("sem_c"))

        def ar_emit(g):
            g.collective_compute("AllReduce", ALU.add, replica_groups=RG,
                                 ins=[ar_in], outs=[ar_out]).then_inc(csem, 1)
        h_arq = prog.stage("g", ar_emit, deps=[h_arw, h_csw], raw=True)
        ncc["c"] += 1
        AR_DONE = ("c", ncc["c"])

        h_strd = prog.stage("q", lambda q: [
            q.dma_start(out=stt_a[:], in_=ar_out[160:161, :].rearrange("one (p c) -> (one p) c", c=128)),
            q.dma_start(out=stt_b[:], in_=ar_out[161:162, :].rearrange("one (p c) -> (one p) c", c=128)),
            q.dma_start(out=csd_sb[:], in_=ar_out[162:163, 0:128].rearrange("one (p c) -> (one p) c", c=1)),
            q.dma_start(out=csb_sb[0:32, :], in_=ar_out[162:163, 128:160].rearrange("one (p c) -> (one p) c", c=1)),
        ], deps=[AR_DONE, h_cse], ndma=4)

        def stats_emit(v):
            v.tensor_scalar_mul(stt_a[:], stt_a[:], 1.0 / D)
            v.tensor_scalar(stt_b[:], stt_b[:], 1.0 / D, EPS, op0=ALU.mult, op1=ALU.add)
            v.drain()
            v.tensor_tensor(out=stt_c[:], in0=stt_a[:], in1=stt_a[:], op=ALU.mult)
            v.drain()
            v.tensor_tensor(out=stt_c[:], in0=stt_b[:], in1=stt_c[:], op=ALU.subtract)
            v.drain()
            v.reciprocal(out=stt_d[:], in_=stt_c[:])
        h_st1 = prog.stage("v", stats_emit, deps=[h_strd])
        h_st2 = prog.stage("s", lambda s: s.activation(out=stt_b[:], in_=stt_d[:], func=AF.Sqrt),
                           deps=[h_st1])
        h_st3 = prog.stage("v", lambda v: v.tensor_tensor(out=stt_c[:], in0=stt_a[:], in1=stt_b[:],
                                                          op=ALU.mult), deps=[h_st2])
        h_mw = prog.stage("g", lambda g: [
            g.dma_start(out=mrows[0:1, :].rearrange("one (p c) -> (one p) c", c=128), in_=stt_a[:]),
            g.dma_start(out=mrows[1:2, :].rearrange("one (p c) -> (one p) c", c=128), in_=stt_b[:]),
            g.dma_start(out=mrows[2:3, :].rearrange("one (p c) -> (one p) c", c=128), in_=stt_c[:]),
        ], deps=[h_st3], ndma=3)

        apos["o"] = 0
        TB = 512
        NB = TTOK // TB
        NCH = 4
        xbp = [[aslc(TB) for _ in range(NCH)] for _ in range(2)]
        dtp = [[aslc(TB) for _ in range(NCH)] for _ in range(2)]
        dtxp = [[aslc(TB) for _ in range(NCH)] for _ in range(2)]
        xlnb = [aslc(TB) for _ in range(NCH)]
        hbt = xlnb
        seedb = [aslc(TB) for _ in range(NCH)]
        dAn = [[aslc(TB) for _ in range(NCH)] for _ in range(2)]
        dBxb = [aslc(TB) for _ in range(NCH)]
        qbf = [[aslc(TB, 128, BF16) for _ in range(NCH)] for _ in range(2)]
        xpdtp = [aslc(TB) for _ in range(2)]
        xpdcp = [aslc(TB) for _ in range(2)]
        xpbtp = [aslc(TB, 32) for _ in range(2)]
        xpbcp = [aslc(TB, 32, BF16) for _ in range(2)]
        t1s = aslc(TB)
        t2s_ = aslc(TB)
        mrowp = [aslc(TB, 1) for _ in range(2)]
        rrowp = [aslc(TB, 1) for _ in range(2)]
        mrrowp = [aslc(TB, 1) for _ in range(2)]

        h_scan_hist = []
        h_yacc_hist = []
        h_scan_done = {}
        h_prep_done = {}
        h_dt_done = {}
        h_gy = {-1: h_mw}
        h_yev = {-1: None}
        h_corr = None
        h_sp = None
        ag_handles = []

        def sh(idx):
            return h_scan_hist[idx] if 0 <= idx < len(h_scan_hist) else None

        def yh(idx):
            return h_yacc_hist[idx] if 0 <= idx < len(h_yacc_hist) else None

        for t in range(NB):
            par = t % 2
            tsl = slice(t * TB, (t + 1) * TB)
            h_ld = prog.stage("q", (lambda tsl_, par_: lambda q: [
                q.dma_start(out=xbp[par_][i], in_=xc[128 * i:128 * (i + 1), tsl_])
                for i in range(NCH)] + [
                q.dma_start(out=xpdtp[par_], in_=ar_out[0:128, tsl_]),
                q.dma_start(out=xpbtp[par_], in_=ar_out[128:160, tsl_]),
                q.dma_start(out=mrowp[par_][:], in_=mrows[0:1, tsl_]),
                q.dma_start(out=rrowp[par_][:], in_=mrows[1:2, tsl_]),
                q.dma_start(out=mrrowp[par_][:], in_=mrows[2:3, tsl_]),
            ])(tsl, par), deps=[h_prep_done.get(t - 2), h_scan_done.get(t - 2),
                                h_dt_done.get(t - 2), AR_DONE, h_mw], ndma=NCH + 5)

            def bcrm_emit(p, par_=par):
                p.matmul(pbank[6][:, :], ones_row[:], rrowp[par_][:], start=True, stop=True)
                p.matmul(pbank[7][:, :], ones_row[:], mrrowp[par_][:], start=True, stop=True)
            h_bcrm = prog.stage("p", bcrm_emit, deps=[h_ld, h_scan_done.get(t - 1), h_ones])

            def corr_emit(v, par_=par):
                v.tensor_tensor(out=t1s, in0=xpdtp[par_], in1=pbank[6][:, :], op=ALU.mult)
                v.tensor_scalar_mul(t2s_, pbank[7][:, :], csd_sb[:, 0:1])
                v.drain()
                v.tensor_tensor(out=xpdcp[par_], in0=t1s, in1=t2s_, op=ALU.subtract)
                v.drain()
                v.tensor_tensor(out=t1s[0:32, :], in0=xpbtp[par_], in1=pbank[6][0:32, :], op=ALU.mult)
                v.tensor_scalar_mul(t2s_[0:32, :], pbank[7][0:32, :], csb_sb[:, 0:1])
                v.drain()
                v.tensor_tensor(out=xpbcp[par_], in0=t1s[0:32, :], in1=t2s_[0:32, :], op=ALU.subtract)
            h_corr = prog.stage("v", corr_emit, deps=[h_bcrm, h_cse])

            h_pdp = None
            h_sp = None
            for i in range(NCH):
                def pd3(p, i_=i, par_=par):
                    bk = pbank[6] if i_ % 2 == 0 else pbank[7]
                    p.matmul(bk[:, :], dtw_sb[:, i_ * 128:(i_ + 1) * 128], xpdcp[par_],
                             start=True, stop=True)
                h_pdp2 = prog.stage("p", pd3, deps=[h_corr, h_sp, h_w0])

                def sp3(s, i_=i, par_=par):
                    bk = pbank[6] if i_ % 2 == 0 else pbank[7]
                    s.activation(out=dtp[par_][i_], in_=bk[:, :], func=AF.Exp,
                                 bias=dtb_sb[:, i_:i_ + 1])
                    s.drain()
                    s.activation(out=dtp[par_][i_], in_=dtp[par_][i_], func=AF.Ln, bias=1.0)
                h_sp = prog.stage("s", sp3, deps=[h_pdp, h_pdp2])
                h_pdp = h_pdp2
            h_dt_done[t] = h_sp

            def bcmr_emit(p, par_=par):
                p.matmul(pbank[4][:, :], ones_row[:], mrowp[par_][:], start=True, stop=True)
                p.matmul(pbank[5][:, :], ones_row[:], rrowp[par_][:], start=True, stop=True)
            h_bcmr = prog.stage("p", bcmr_emit, deps=[h_ld, h_scan_done.get(t - 1), h_ones])

            def xln_emit(v, par_=par):
                for i in range(NCH):
                    v.tensor_tensor(out=dBxb[i], in0=xbp[par_][i], in1=pbank[4][:, :], op=ALU.subtract)
                v.drain()
                for i in range(NCH):
                    v.tensor_tensor(out=xlnb[i], in0=dBxb[i], in1=pbank[5][:, :], op=ALU.mult)
            h_xln = prog.stage("v", xln_emit, deps=[h_bcmr, h_scan_done.get(t - 1),
                                                    h_yev.get(t - 1)])

            def prep_emit(v, par_=par):
                for i in range(NCH):
                    v.tensor_tensor(out=dtxp[par_][i], in0=dtp[par_][i], in1=xlnb[i], op=ALU.mult)
            h_prep = prog.stage("v", prep_emit, deps=[h_xln, h_sp])
            h_prep_done[t] = h_prep

            def seed_emit(s, par_=par):
                for i in range(NCH):
                    s.activation(out=seedb[i], in_=xlnb[i], func=AF.Copy, scale=d_sb[:, i:i + 1])
            h_seed = prog.stage("s", seed_emit, deps=[h_xln, h_sp, h_gy.get(t - 1), h_w0])

            h_yacc = None
            base = len(h_scan_hist)
            for n in range(16):
                np2 = n % 2
                bB, bC = (4, 5) if np2 == 0 else (6, 7)

                def bcb_emit(p, n_=n, bB_=bB, bC_=bC, par_=par):
                    p.matmul(pbank[bB_][:, :], sel_sb[:, n_ * 128:(n_ + 1) * 128],
                             xpbcp[par_], start=True, stop=True)
                    p.matmul(pbank[bC_][:, :], sel_sb[:, (16 + n_) * 128:(17 + n_) * 128],
                             xpbcp[par_], start=True, stop=True)
                bdeps = [h_corr, sh(base + n - 2), h_w0]
                if np2 == 0:
                    bdeps.append(h_xln if n == 0 else None)
                else:
                    bdeps.append(h_dt_done[t])
                h_pbB = prog.stage("p", bcb_emit, deps=bdeps)

                def dA_emit(s, n_=n, par_=par, np2_=np2):
                    for i in range(NCH):
                        s.activation(out=dAn[np2_][i], in_=dtp[par_][i], func=AF.Exp,
                                     scale=a_sb[:, i * 16 + n_:i * 16 + n_ + 1])
                h_sn = prog.stage("s", dA_emit, deps=[h_sp, h_a, sh(base + n - 2), h_seed])

                def scan_emit(v, n_=n, t_=t, par_=par, np2_=np2, bB_=bB, bC_=bC):
                    for i in range(NCH):
                        v.tensor_tensor(out=dBxb[i], in0=dtxp[par_][i], in1=pbank[bB_][:, :],
                                        op=ALU.mult)
                    if t_ == NB // 2:
                        for i in range(NCH):
                            v.memset(carry[:, i * 16 + n_:i * 16 + n_ + 1], 0.0)
                    v.drain()
                    for i in range(NCH):
                        col = slice(i * 16 + n_, i * 16 + n_ + 1)
                        v.tensor_tensor_scan(out=hbt[i], data0=dAn[np2_][i], data1=dBxb[i],
                                             initial=carry[:, col], op0=ALU.mult, op1=ALU.add)
                    v.drain()
                    for i in range(NCH):
                        col = slice(i * 16 + n_, i * 16 + n_ + 1)
                        v.tensor_copy(out=carry[:, col], in_=hbt[i][:, TB - 1:TB])
                    for i in range(NCH):
                        v.tensor_tensor(out=qbf[np2_][i], in0=hbt[i], in1=pbank[bC_][:, :],
                                        op=ALU.mult)
                sdeps = [h_pbB, h_sn, h_prep, yh(base + n - 2)]
                if n == 0:
                    sdeps.append(h_seed)
                h_vq = prog.stage("v", scan_emit, deps=sdeps)
                h_scan_hist.append(h_vq)

                def yacc_emit(p, n_=n - 1, np2_=(n - 1) % 2):
                    for i in range(NCH):
                        p.matmul(pbank[i][:, :], idb_sb[:], qbf[np2_][i],
                                 start=(n_ == 0), stop=(n_ == 15))
                if n >= 1:
                    h_yacc = prog.stage("p", yacc_emit,
                                        deps=[sh(base + n - 1), h_yev.get(t - 1), h_w0])
                    h_yacc_hist.append(h_yacc)

            def yacc15_emit(p):
                for i in range(NCH):
                    p.matmul(pbank[i][:, :], idb_sb[:], qbf[15 % 2][i],
                             start=False, stop=True)
            h_yacc = prog.stage("p", yacc15_emit,
                                deps=[sh(base + 15), h_yev.get(t - 1), h_w0])
            h_yacc_hist.append(h_yacc)
            h_scan_done[t] = h_vq

            def yev_emit(v):
                for i in range(NCH):
                    v.tensor_tensor(out=seedb[i], in0=pbank[i][:, :], in1=seedb[i], op=ALU.add)
            h_yev[t] = prog.stage("v", yev_emit, deps=[h_yacc, h_seed])

            j = t // 4
            co = (t % 4) * TB
            h_gy[t] = prog.stage("g", (lambda j_, co_: lambda g: [
                g.dma_start(out=ylocal[CSL * j_ + 128 * i:CSL * j_ + 128 * (i + 1), co_:co_ + TB],
                            in_=seedb[i]) for i in range(NCH)])(j, co),
                deps=[h_yev[t]], ndma=NCH)

            if t % 4 == 3:
                agdep = ag_handles[-1] if ag_handles else None

                def ag_emit(g, j_=j):
                    g.collective_compute("AllGather", ALU.bypass, replica_groups=RG,
                                         ins=[ylocal[CSL * j_:CSL * (j_ + 1), :]],
                                         outs=[ygath[D * j_:D * (j_ + 1), :]]).then_inc(csem, 1)
                prog.stage("g", ag_emit, deps=[h_gy[t], agdep], raw=True)
                ncc["c"] += 1
                ag_handles.append(("c", ncc["c"]))

        AG_ALL = ag_handles[-1]
        B_TAIL = [h_scan_done[NB - 1], h_prep_done[NB - 1], h_dt_done[NB - 1],
                  h_yev[NB - 1], h_gy[NB - 1]]

        apos["o"] = 0
        TC3 = 512
        NC3 = TSL // TC3
        NTI = TC3 // 128
        x3 = aslc(32 * TC3)
        xln3 = aslc(32 * TC3, 128, BF16)
        sqpp = [aslc(1024) for _ in range(2)]
        lg3 = sqpp[1][0:8, 0:512]
        wbc3 = sqpp[0][:, 0:512]
        wb3p = [aslc(32 * 128, 128, BF16) for _ in range(2)]
        aa3 = aslc(32 * 128, 128, BF16)
        ba3 = aslc(D, 128, BF16)
        evb = [aslc(TC3) for _ in range(2)]
        lgT3t = aslc(8 * NTI)
        w23t = aslc(8 * NTI)
        t2s3 = aslc(16 * NTI)
        sc3 = aslc(4 * NTI)
        w83 = aslc(TC3, 8)
        g3 = aslc(TC3, 128, BF16)

        ps_sx, ps_sq, ps_mb, ps_rb, ps_aux, ps_h, ps_o0, ps_o1 = pbank

        def w2_emit(q):
            return [
                q.dma_start(out=aa3.rearrange("p (k m) -> p k m", m=128), in_=_r3(aah, 128)),
                q.dma_start(out=ba3, in_=bah),
                q.dma_start(out=wr_sb[:].rearrange("p (k m) -> p k m", m=8), in_=_r3(wrh, 8)),
            ]
        h_w2 = prog.stage("q", w2_emit, deps=[AG_ALL, h_castall] + B_TAIL, ndma=3)

        ygv = ygath.rearrange("(b p) c -> p b c", p=128)
        h_out = {-1: None}
        h_ffn_done = None
        h_acc = None
        h_wb_used = [None, None]
        h_ev_hist = {}
        h_go_hist = {}
        for t in range(NC3):
            tsl = slice(t * TC3, (t + 1) * TC3)

            def ld3(q, tsl_=tsl):
                return [q.dma_start(out=_s3(x3, TC3), in_=_r3(xs[:, tsl_], TC3))]
            h_ld = prog.stage("q", ld3, deps=[h_ffn_done, h_out[t - 1], h_w2], ndma=1)

            def acc3(g, tsl_=tsl):
                if "cid_g" not in cidbox:
                    cidbox["cid_g"] = g.partition_id()
                cid = cidbox["cid_g"]
                return [g.dma_start(out=y3v, in_=ygv[:, bass.ts(cid, 32), tsl_],
                                    accum_op=ALU.add)
                        for y3v in [x3.rearrange("p (k c) -> p k c", c=TC3)]]
            h_acc = prog.stage("g", acc3, deps=[h_ld, AG_ALL, h_out[t - 1]], ndma=1)

            h_sqs = None
            h_sums = None
            h_sums2 = None
            for jp in range(16):
                def sq_emit(s, jp_=jp):
                    c = slice(jp_ * 2 * TC3, (jp_ * 2 + 2) * TC3)
                    s.activation(out=sqpp[jp_ % 2][:, 0:1024], in_=x3[:, c], func=AF.Square)
                h_sqs = prog.stage("s", sq_emit, deps=[h_acc, h_sums2, h_out[t - 1]])

                def sum_emit(p, jp_=jp):
                    for kk in range(2):
                        k = jp_ * 2 + kk
                        c = slice(k * TC3, (k + 1) * TC3)
                        p.matmul(ps_sx[0:1, :], ones_col[:], x3[:, c],
                                 start=(k == 0), stop=(k == 31))
                        p.matmul(ps_sq[0:1, :], ones_col[:],
                                 sqpp[jp_ % 2][:, kk * TC3:(kk + 1) * TC3],
                                 start=(k == 0), stop=(k == 31))
                h_sums2 = h_sums
                h_sums = prog.stage("p", sum_emit,
                                    deps=[h_sqs, h_ones, h_sums2] + h_wb_used)
            h_sum3 = h_sums

            def st31(v):
                v.tensor_scalar_mul(st[:, 0:512], ps_sx[0:1, :], 1.0 / D)
                v.tensor_scalar(st[:, 512:1024], ps_sq[0:1, :], 1.0 / D, EPS,
                                op0=ALU.mult, op1=ALU.add)
                v.drain()
                v.tensor_tensor(out=st[:, 1024:1536], in0=st[:, 0:512],
                                in1=st[:, 0:512], op=ALU.mult)
                v.drain()
                v.tensor_tensor(out=st[:, 1024:1536], in0=st[:, 512:1024],
                                in1=st[:, 1024:1536], op=ALU.subtract)
                v.drain()
                v.reciprocal(out=st[:, 512:1024], in_=st[:, 1024:1536])
            h_st31 = prog.stage("v", st31, deps=[h_sum3])
            h_st32 = prog.stage("s", lambda s: s.activation(out=st[:, 1536:2048],
                                                            in_=st[:, 512:1024], func=AF.Sqrt),
                                deps=[h_st31])

            def bc3(p):
                p.matmul(ps_mb[:, :], ones_row[:], st[0:1, 0:512], start=True, stop=True)
                p.matmul(ps_rb[:, :], ones_row[:], st[0:1, 1536:2048], start=True, stop=True)
            h_bc3 = prog.stage("p", bc3, deps=[h_st32, h_ones])

            def xln3_emit(v):
                for k in range(32):
                    c = slice(k * TC3, (k + 1) * TC3)
                    v.tensor_tensor(out=xln3[:, c], in0=x3[:, c], in1=ps_mb[:, :], op=ALU.subtract)
                v.drain()
                for k in range(32):
                    c = slice(k * TC3, (k + 1) * TC3)
                    v.tensor_tensor(out=xln3[:, c], in0=xln3[:, c], in1=ps_rb[:, :], op=ALU.mult)
            h_xln3 = prog.stage("v", xln3_emit, deps=[h_bc3])

            def lg_emit(p):
                for k in range(32):
                    c = slice(k * TC3, (k + 1) * TC3)
                    p.matmul(ps_aux[0:8, :], wr_sb[:, k * 8:(k + 1) * 8], xln3[:, c],
                             start=(k == 0), stop=(k == 31))
            h_lg = prog.stage("p", lg_emit, deps=[h_xln3, h_w2])
            h_lge = prog.stage("v", lambda v: v.tensor_copy(out=lg3, in_=ps_aux[0:8, :]),
                               deps=[h_lg])

            trb = (ps_mb, ps_rb, ps_o0, ps_o1)

            def tr_emit(p):
                for i in range(NTI):
                    p.transpose(trb[i][:, 0:8], lg3[:, i * 128:(i + 1) * 128], id8_sb[:])
            h_tr = prog.stage("p", tr_emit, deps=[h_lge, h_xln3] + h_wb_used)

            def tre_emit(v):
                for i in range(NTI):
                    v.tensor_copy(out=lgT3t[:, i * 8:(i + 1) * 8], in_=trb[i][:, 0:8])
            h_tre = prog.stage("v", tre_emit, deps=[h_tr])

            lgT3v = lgT3t.rearrange("p (i e) -> p i e", e=8)
            w23v = w23t.rearrange("p (i e) -> p i e", e=8)
            eq23 = t2s3[:, 0:8 * NTI].rearrange("p (i e) -> p i e", e=8)
            eq2b = t2s3[:, 8 * NTI:16 * NTI].rearrange("p (i e) -> p i e", e=8)
            m1 = sc3[:, 0 * NTI:1 * NTI]
            m2 = sc3[:, 1 * NTI:2 * NTI]
            dd = sc3[:, 2 * NTI:3 * NTI]
            e1 = sc3[:, 3 * NTI:4 * NTI]

            def top2a_emit(v):
                v.reduce_max(m1, lgT3v, axis=mybir.AxisListType.X)
                v.drain()
                a_, b_ = bass.broadcast_tensor_aps(lgT3v, m1[:, :, None])
                v.tensor_tensor(out=w23v, in0=a_, in1=b_, op=ALU.is_equal)
                v.drain()
                v.scalar_tensor_tensor(out=lgT3v, in0=w23v, scalar=-1e30, in1=lgT3v,
                                       op0=ALU.mult, op1=ALU.add)
                v.drain()
                v.reduce_max(m2, lgT3v, axis=mybir.AxisListType.X)
                v.drain()
                v.tensor_tensor(out=dd, in0=m1, in1=m2, op=ALU.subtract)
            h_t2a = prog.stage("v", top2a_emit, deps=[h_tre])

            h_sg = prog.stage("s", lambda s: s.activation(out=e1, in_=dd, func=AF.Sigmoid),
                              deps=[h_t2a])

            def top2b_emit(v):
                a_, b_ = bass.broadcast_tensor_aps(lgT3v, m2[:, :, None])
                v.tensor_tensor(out=eq23, in0=a_, in1=b_, op=ALU.is_equal)
                a_, b_ = bass.broadcast_tensor_aps(w23v, e1[:, :, None])
                v.tensor_tensor(out=w23v, in0=a_, in1=b_, op=ALU.mult)
                v.drain()
                a_, b_ = bass.broadcast_tensor_aps(eq23, e1[:, :, None])
                v.tensor_tensor(out=eq2b, in0=a_, in1=b_, op=ALU.mult)
                v.drain()
                v.tensor_tensor(out=eq23, in0=eq23, in1=eq2b, op=ALU.subtract)
                v.drain()
                v.tensor_tensor(out=w23t, in0=w23t, in1=t2s3[:, 0:8 * NTI], op=ALU.add)
            h_t2b = prog.stage("v", top2b_emit, deps=[h_sg])

            def trw_emit(p):
                for i in range(NTI):
                    p.transpose(trb[i][0:8, 0:128], w23t[:, i * 8:(i + 1) * 8], idf_sb[:])
            h_trw = prog.stage("p", trw_emit, deps=[h_t2b, h_tre])

            def w8_emit(v):
                for i in range(NTI):
                    v.tensor_copy(out=w83[:, i * 128:(i + 1) * 128], in_=trb[i][0:8, 0:128])
            h_w8 = prog.stage("v", w8_emit, deps=[h_trw])

            def hall_emit(p):
                for k in range(32):
                    c = slice(k * TC3, (k + 1) * TC3)
                    p.matmul(ps_h[:, :], aa3[:, k * 128:(k + 1) * 128], xln3[:, c],
                             start=(k == 0), stop=(k == 31))
                p.matmul(ps_aux[:, :], e8_sb[:], w83, start=True, stop=True)
            h_hall = prog.stage("p", hall_emit, deps=[h_xln3, h_w8, h_w2, h_lge])

            def g_emit(v):
                v.tensor_copy(out=wbc3, in_=ps_aux[:, :])
                v.drain()
                v.scalar_tensor_tensor(out=g3, in0=ps_h[:, :], scalar=2.0, in1=wbc3,
                                       op0=ALU.mult, op1=ALU.mult)
            h_g = prog.stage("v", g_emit, deps=[h_hall])

            h_ev = None
            h_mm = None
            for ofb in range(32):
                wpar = ofb % 2
                h_wb = prog.stage("q", (lambda ofb_, wpar_: lambda q: [
                    q.dma_start(out=wb3p[wpar_].rearrange("p (k c) -> p k c", c=128),
                                in_=_r3(wbh[:, ofb_ * 128:(ofb_ + 1) * 128], 128)),
                ])(ofb, wpar), deps=[h_xln3, h_wb_used[wpar], h_castall], ndma=1)

                def mm_emit(p, ofb_=ofb, wpar_=wpar):
                    pso = ps_o0 if ofb_ % 2 == 0 else ps_o1
                    for k in range(32):
                        c = slice(k * TC3, (k + 1) * TC3)
                        p.matmul(pso[:, :], wb3p[wpar_][:, k * 128:(k + 1) * 128],
                                 xln3[:, c], start=(k == 0), stop=False)
                    p.matmul(pso[:, :], ba3[:, ofb_ * 128:(ofb_ + 1) * 128], g3,
                             start=False, stop=True)
                h_mm = prog.stage("p", mm_emit, deps=[h_wb, h_g, h_ev_hist.get(ofb - 2),
                                                      h_trw])
                h_wb_used[wpar] = h_mm

                def ev_emit(v, ofb_=ofb):
                    pso = ps_o0 if ofb_ % 2 == 0 else ps_o1
                    c = slice(ofb_ * TC3, (ofb_ + 1) * TC3)
                    v.scalar_tensor_tensor(out=evb[ofb_ % 2], in0=pso[:, :],
                                           scalar=bb_sb[:, ofb_:ofb_ + 1],
                                           in1=x3[:, c], op0=ALU.add, op1=ALU.add)
                h_ev = prog.stage("v", ev_emit, deps=[h_mm, h_go_hist.get(ofb - 2)])
                h_ev_hist[ofb] = h_ev

                h_go = prog.stage("g", (lambda ofb_, tsl_: lambda g: [
                    g.dma_start(out=outs[ofb_ * 128:(ofb_ + 1) * 128, tsl_],
                                in_=evb[ofb_ % 2]),
                ])(ofb, tsl), deps=[h_ev], ndma=1)
                h_go_hist[ofb] = h_go
            h_ffn_done = h_ev
            h_out[t] = h_go
            h_ev_hist = {}
            h_go_hist = {}

        _run(nc, prog, ctx, [h_out[NC3 - 1]], extra_sems={"c": csem})
    return nc



_CACHE = {}


def _get(name, builder):
    if name not in _CACHE:
        _CACHE[name] = builder()
    return _CACHE[name]


def kernel(L0_attn_x, L0_attn_A_log, L0_attn_D, L0_attn_x_proj_w, L0_attn_dt_proj_w,
           L0_attn_dt_proj_b, L0_ffn_W_base, L0_ffn_b_base, L0_ffn_W_router,
           L0_ffn_A, L0_ffn_B):
    f32 = np.float32
    xT = np.ascontiguousarray(np.asarray(L0_attn_x, f32).reshape(TTOK, D).T)
    xpwT = np.ascontiguousarray(np.asarray(L0_attn_x_proj_w, f32).T)
    dtwT = np.ascontiguousarray(np.asarray(L0_attn_dt_proj_w, f32).T)
    dtb = np.ascontiguousarray(np.asarray(L0_attn_dt_proj_b, f32).reshape(32, 128).T)
    alog = np.ascontiguousarray(np.asarray(L0_attn_A_log, f32))
    dvec = np.asarray(L0_attn_D, f32)
    wbT = np.ascontiguousarray(np.asarray(L0_ffn_W_base, f32).T)
    bb = np.ascontiguousarray(np.asarray(L0_ffn_b_base, f32).reshape(32, 128).T)
    wrT = np.ascontiguousarray(np.asarray(L0_ffn_W_router, f32).T)
    aaT = np.ascontiguousarray(np.asarray(L0_ffn_A, f32).reshape(128, D).T)
    baT = np.ascontiguousarray(np.asarray(L0_ffn_B, f32).transpose(0, 2, 1).reshape(128, D))
    import ml_dtypes
    identb = np.eye(128, dtype=ml_dtypes.bfloat16)
    ident8 = np.eye(8, dtype=f32)
    identf = np.eye(128, dtype=f32)
    e8m = np.repeat(np.eye(8, dtype=f32), 16, axis=1)
    sel32 = np.zeros((32, 32 * 128), dtype=ml_dtypes.bfloat16)
    for n in range(32):
        sel32[n, n * 128:(n + 1) * 128] = 1.0

    cores = list(range(NCORE))
    nc = _get("fused", build_fused)
    in_maps = []
    for c in cores:
        ch = slice(c * CSL, (c + 1) * CSL)
        in_maps.append(dict(
            xs=np.ascontiguousarray(xT[:, c * TSL:(c + 1) * TSL]),
            xc=np.ascontiguousarray(xT[ch]),
            xpw_my=np.ascontiguousarray(
                xpwT[ch].reshape(4, 128, 160).transpose(1, 0, 2).reshape(128, 640)),
            dtw_my=np.ascontiguousarray(dtwT[:, ch]),
            dtb_my=np.ascontiguousarray(dtb[:, 4 * c:4 * (c + 1)]),
            alog_my=np.ascontiguousarray(
                alog[ch].reshape(4, 128, 16).transpose(1, 0, 2).reshape(128, 64)),
            dvec_my=np.ascontiguousarray(dvec[ch].reshape(4, 128).T),
            wbT=wbT, bb=bb, wrT=wrT, aaT=aaT, baT=baT,
            identb=identb, ident8=ident8, identf=identf, e8m=e8m, sel32=sel32,
        ))
    r = run_bass_kernel_spmd(nc, in_maps, core_ids=cores)
    out_full = np.concatenate([r.results[c]["outs"] for c in cores], axis=1)
    return np.ascontiguousarray(out_full.T).reshape(2, SEQ, D).astype(np.float32)


# version 09
# speedup vs baseline: 1.4414x; 1.1351x over previous
import numpy as np
import concourse.bass as bass
import concourse.mybir as mybir
from concourse.bass_utils import run_bass_kernel_spmd
from contextlib import ExitStack

F32 = mybir.dt.float32
BF16 = mybir.dt.bfloat16
F32R = mybir.dt.float32r
AF = mybir.ActivationFunctionType
ALU = mybir.AluOpType

D = 4096
TTOK = 16384
NCORE = 8
TSL = TTOK // NCORE
CSL = D // NCORE
SEQ = 8192
EPS = 1e-5

ENGS = ("q", "v", "s", "p", "g")


class _LastInstRecorder:

    def __init__(self, engine):
        self._e = engine
        self.last = None

    def __getattr__(self, name):
        f = getattr(self._e, name)
        if not callable(f):
            return f

        def wrap(*a, **k):
            r = f(*a, **k)
            if hasattr(r, "then_inc"):
                self.last = r
            return r
        return wrap


class Prog:

    def __init__(self):
        self.ctr = {e: 0 for e in ENGS}
        self.stream = {e: [] for e in ENGS}

    def stage(self, eng, emit, deps=(), ndma=0, raw=False):
        deps = [d for d in deps if d is not None]
        if ndma and self.ctr[eng]:
            deps.append((eng, self.ctr[eng]))
        self.stream[eng].append((list(deps), emit, "raw" if raw else ndma))
        if raw:
            return None
        if ndma:
            self.ctr[eng] += 16 * ndma
        else:
            self.ctr[eng] += 1
        return (eng, self.ctr[eng])

    def replay(self, eng_key, engine, sems):
        rec = _LastInstRecorder(engine)
        for deps, emit, ndma in self.stream[eng_key]:
            for (sk, val) in deps:
                engine.wait_ge(sems[sk], val)
            if ndma == "raw":
                emit(engine)
            elif ndma:
                insts = emit(engine)
                assert len(insts) == ndma, (eng_key, len(insts), ndma)
                for inst in insts:
                    inst.then_inc(sems[eng_key], 16)
            else:
                rec.last = None
                emit(rec)
                if rec.last is not None:
                    rec.last.then_inc(sems[eng_key], 1)
                else:
                    engine.drain().then_inc(sems[eng_key], 1)


def _run(nc, prog, ctx, final_deps, extra_sems=None):
    sems = {e: ctx.enter_context(nc.semaphore("sem_" + e)) for e in ENGS}
    if extra_sems:
        sems.update(extra_sems)
    block = ctx.enter_context(nc.Block())

    @block.sync
    def _(sync):
        prog.replay("q", sync, sems)

    @block.vector
    def _(vector):
        prog.replay("v", vector, sems)

    @block.scalar
    def _(scalar):
        prog.replay("s", scalar, sems)

    @block.tensor
    def _(tensor):
        prog.replay("p", tensor, sems)

    @block.gpsimd
    def _(gpsimd):
        prog.replay("g", gpsimd, sems)
        for (sk, val) in final_deps:
            gpsimd.wait_ge(sems[sk], val)



def _r3(dram_ap, inner):
    return dram_ap.rearrange("(k p) c -> p k c", p=128)


def _s3(sb_ap, inner):
    return sb_ap.rearrange("p (k c) -> p k c", c=inner)


def ln_stats(prog, sq_tile, x_tile, ones_col, st, ps_a, ps_b, nk, TC, deps, h_ones):
    h_sq = prog.stage("s", lambda s: s.activation(out=sq_tile[:], in_=x_tile[:], func=AF.Square),
                      deps=deps)

    def sums(p):
        for k in range(nk):
            c = slice(k * TC, (k + 1) * TC)
            p.matmul(ps_a[0:1, :], ones_col[:], x_tile[:, c], start=(k == 0), stop=(k == nk - 1))
        for k in range(nk):
            c = slice(k * TC, (k + 1) * TC)
            p.matmul(ps_b[0:1, :], ones_col[:], sq_tile[:, c], start=(k == 0), stop=(k == nk - 1))
    h_sum = prog.stage("p", sums, deps=list(deps) + [h_sq, h_ones])

    def stats1(v):
        v.tensor_scalar_mul(st[:, 0 * TC:1 * TC], ps_a[0:1, :], 1.0 / D)
        v.tensor_scalar(st[:, 1 * TC:2 * TC], ps_b[0:1, :], 1.0 / D, EPS,
                        op0=ALU.mult, op1=ALU.add)
        v.drain()
        v.tensor_tensor(out=st[:, 2 * TC:3 * TC], in0=st[:, 0 * TC:1 * TC],
                        in1=st[:, 0 * TC:1 * TC], op=ALU.mult)
        v.drain()
        v.tensor_tensor(out=st[:, 2 * TC:3 * TC], in0=st[:, 1 * TC:2 * TC],
                        in1=st[:, 2 * TC:3 * TC], op=ALU.subtract)
        v.drain()
        v.reciprocal(out=st[:, 3 * TC:4 * TC], in_=st[:, 2 * TC:3 * TC])
    h_st1 = prog.stage("v", stats1, deps=[h_sum])

    h_rt = prog.stage("s", lambda s: s.activation(out=st[:, 4 * TC:5 * TC],
                                                  in_=st[:, 3 * TC:4 * TC], func=AF.Sqrt),
                      deps=[h_st1])
    return h_rt


def bcast_mr(prog, ones_row, st, ps_m, ps_r, TC, deps, h_ones):
    def bc(p):
        p.matmul(ps_m[:, 0:TC], ones_row[:], st[0:1, 0 * TC:1 * TC], start=True, stop=True)
        p.matmul(ps_r[:, 0:TC], ones_row[:], st[0:1, 4 * TC:5 * TC], start=True, stop=True)
    return prog.stage("p", bc, deps=list(deps) + [h_ones])



def build_l1():
    nc = bass.Bass("TRN2", target_bir_lowering=False)
    xs = nc.dram_tensor("xs", [D, TSL], F32, kind="ExternalInput").ap()
    xpwT = nc.dram_tensor("xpwT", [D, 160], F32, kind="ExternalInput").ap()
    dtwT = nc.dram_tensor("dtwT", [128, D], F32, kind="ExternalInput").ap()
    dtb = nc.dram_tensor("dtb", [128, 32], F32, kind="ExternalInput").ap()
    dts = nc.dram_tensor("dts", [D, TSL], F32, kind="ExternalOutput").ap()
    xlns = nc.dram_tensor("xlns", [D, TSL], F32, kind="ExternalOutput").ap()
    bcs = nc.dram_tensor("bcs", [32, TSL], F32, kind="ExternalOutput").ap()

    TC = 256
    NT = TSL // TC
    prog = Prog()
    with ExitStack() as ctx:
        sb = lambda n_, sh, dt=F32: ctx.enter_context(nc.sbuf_tensor(n_, sh, dt))
        x_sb = sb("x_sb", [128, 32 * TC])
        sq_sb = sb("sq_sb", [128, 32 * TC])
        xln_sb = sb("xln_sb", [128, 32 * TC])
        dt_sb = sb("dt_sb", [128, 32 * TC])
        dtin_sb = sb("dtin_sb", [128, TC])
        bc_sb = sb("bc_sb", [32, TC])
        xpw_sb = sb("xpw_sb", [128, 32 * 160])
        dtw_sb = sb("dtw_sb", [128, 32 * 128])
        dtb_sb = sb("dtb_sb", [128, 32])
        ones_col = sb("ones_col", [128, 1])
        ones_row = sb("ones_row", [1, 128])
        st = sb("st", [1, 8 * TC])
        pb = [ctx.enter_context(nc.psum_tensor(f"pb{i}", [128, TC], F32)) for i in range(8)]
        ps_sx, ps_sq, ps_mb, ps_rb, ps_dt, ps_bc, ps_d0, ps_d1 = pb

        h_ones = prog.stage("v", lambda v: (v.memset(ones_col[:], 1.0), v.memset(ones_row[:], 1.0)))
        h_w = prog.stage("q", lambda q: [
            q.dma_start(out=_s3(xpw_sb[:], 160), in_=_r3(xpwT, 160)),
            q.dma_start(out=dtw_sb[:], in_=dtwT),
            q.dma_start(out=dtb_sb[:], in_=dtb),
        ], ndma=3)

        h_out = None
        h_xln_prev = None
        for t in range(NT):
            tsl = slice(t * TC, (t + 1) * TC)
            h_xld = prog.stage("q", (lambda tsl_: lambda q: [
                q.dma_start(out=_s3(x_sb[:], TC), in_=_r3(xs[:, tsl_], TC))
            ])(tsl), deps=[h_xln_prev], ndma=1)

            h_rt = ln_stats(prog, sq_sb, x_sb, ones_col, st, ps_sx, ps_sq, 32, TC,
                            deps=[h_xld], h_ones=h_ones)
            h_bc = bcast_mr(prog, ones_row, st, ps_mb, ps_rb, TC, deps=[h_rt], h_ones=h_ones)

            def xln_emit(v):
                for k in range(32):
                    c = slice(k * TC, (k + 1) * TC)
                    v.tensor_tensor(out=sq_sb[:, c], in0=x_sb[:, c], in1=ps_mb[:], op=ALU.subtract)
                v.drain()
                for k in range(32):
                    c = slice(k * TC, (k + 1) * TC)
                    v.tensor_tensor(out=xln_sb[:, c], in0=sq_sb[:, c], in1=ps_rb[:], op=ALU.mult)
            h_xln = prog.stage("v", xln_emit, deps=[h_bc, h_out])
            h_xln_prev = h_xln

            def xp_emit(p):
                for k in range(32):
                    c = slice(k * TC, (k + 1) * TC)
                    p.matmul(ps_dt[:], xpw_sb[:, k * 160:k * 160 + 128], xln_sb[:, c],
                             start=(k == 0), stop=(k == 31))
                for k in range(32):
                    c = slice(k * TC, (k + 1) * TC)
                    p.matmul(ps_bc[0:32, :], xpw_sb[:, k * 160 + 128:(k + 1) * 160], xln_sb[:, c],
                             start=(k == 0), stop=(k == 31))
            h_xp = prog.stage("p", xp_emit, deps=[h_xln, h_w])

            def evac_emit(v):
                v.tensor_copy(out=dtin_sb[:], in_=ps_dt[:])
                v.tensor_copy(out=bc_sb[:], in_=ps_bc[0:32, :])
            h_ev = prog.stage("v", evac_emit, deps=[h_xp, h_out])

            h_pe = None
            h_sp = None
            for gidx in range(16):
                def pe_emit(p, gi=gidx):
                    for j in range(2):
                        dm = gi * 2 + j
                        psd = ps_d0 if j == 0 else ps_d1
                        p.matmul(psd[:], dtw_sb[:, dm * 128:(dm + 1) * 128], dtin_sb[:],
                                 start=True, stop=True)
                h_pe = prog.stage("p", pe_emit, deps=[h_ev, h_sp])

                def sp_emit(s, gi=gidx):
                    for j in range(2):
                        dm = gi * 2 + j
                        c = slice(dm * TC, (dm + 1) * TC)
                        psd = ps_d0 if j == 0 else ps_d1
                        s.activation(out=dt_sb[:, c], in_=psd[:], func=AF.Exp,
                                     bias=dtb_sb[:, dm:dm + 1])
                        s.drain()
                        s.activation(out=dt_sb[:, c], in_=dt_sb[:, c], func=AF.Ln, bias=1.0)
                h_sp = prog.stage("s", sp_emit, deps=[h_pe, h_out])

            h_out = prog.stage("g", (lambda tsl_: lambda g: [
                g.dma_start(out=_r3(dts[:, tsl_], TC), in_=_s3(dt_sb[:], TC)),
                g.dma_start(out=_r3(xlns[:, tsl_], TC), in_=_s3(xln_sb[:], TC)),
                g.dma_start(out=bcs[:, tsl_], in_=bc_sb[:]),
            ])(tsl), deps=[h_sp, h_xln, h_ev], ndma=3)

        _run(nc, prog, ctx, [h_out])
    return nc



def build_l2():
    nc = bass.Bass("TRN2", target_bir_lowering=False)
    dtc = nc.dram_tensor("dtc", [CSL, TTOK], F32, kind="ExternalInput").ap()
    xlnc = nc.dram_tensor("xlnc", [CSL, TTOK], F32, kind="ExternalInput").ap()
    bc = nc.dram_tensor("bc", [32, TTOK], F32, kind="ExternalInput").ap()
    sel16 = nc.dram_tensor("sel16", [16, 16 * 128], F32, kind="ExternalInput").ap()
    alog = nc.dram_tensor("alog", [128, (CSL // 128) * 16], F32, kind="ExternalInput").ap()
    dvec = nc.dram_tensor("dvec", [128, CSL // 128], F32, kind="ExternalInput").ap()
    yc = nc.dram_tensor("yc", [CSL, TTOK], F32, kind="ExternalOutput").ap()

    TC = 1024
    NT = TTOK // TC
    NCH = CSL // 128
    NSUB = TC // 512
    prog = Prog()
    with ExitStack() as ctx:
        sb = lambda n_, sh, dt=F32: ctx.enter_context(nc.sbuf_tensor(n_, sh, dt))
        dt_sb = [sb(f"dt{i}", [128, TC]) for i in range(NCH)]
        xln_sb = [sb(f"xln{i}", [128, TC]) for i in range(NCH)]
        dtx_sb = [sb(f"dtx{i}", [128, TC]) for i in range(NCH)]
        y_sb = [sb(f"y{i}", [128, TC]) for i in range(NCH)]
        dA_sb = [sb(f"dA{i}", [128, TC]) for i in range(NCH)]
        dBx_sb = [sb(f"dBx{i}", [128, TC]) for i in range(NCH)]
        h_sb = [sb(f"h{i}", [128, TC]) for i in range(NCH)]
        b_sb = sb("b_sb", [16, TC])
        c_sb = sb("c_sb", [16, TC])
        sel_sb = sb("sel_sb", [16, 16 * 128])
        a_sb = sb("a_sb", [128, NCH * 16])
        d_sb = sb("d_sb", [128, NCH])
        carry = sb("carry", [128, NCH * 16])
        ones_row = sb("ones_row", [1, 128])
        ps_b = [ctx.enter_context(nc.psum_tensor(f"psb{i}", [128, 512], F32)) for i in range(NSUB)]
        ps_c = [ctx.enter_context(nc.psum_tensor(f"psc{i}", [128, 512], F32)) for i in range(NSUB)]

        prog.stage("v", lambda v: (v.memset(ones_row[:], 1.0), v.memset(carry[:], 0.0)))
        h_cst = prog.stage("q", lambda q: [
            q.dma_start(out=a_sb[:], in_=alog),
            q.dma_start(out=d_sb[:], in_=dvec),
            q.dma_start(out=sel_sb[:], in_=sel16),
        ], ndma=3)
        h_a = prog.stage("s", lambda s: (s.activation(out=a_sb[:], in_=a_sb[:], func=AF.Exp),
                                         s.drain(),
                                         s.mul(out=a_sb[:], in_=a_sb[:], mul=-1.0)),
                         deps=[h_cst])

        h_y = None
        h_vq = None
        h_sn = None
        for t in range(NT):
            tsl = slice(t * TC, (t + 1) * TC)
            h_ld = prog.stage("q", (lambda tsl_: lambda q: [
                q.dma_start(out=dt_sb[i][:], in_=dtc[128 * i:128 * (i + 1), tsl_])
                for i in range(NCH)] + [
                q.dma_start(out=xln_sb[i][:], in_=xlnc[128 * i:128 * (i + 1), tsl_])
                for i in range(NCH)] + [
                q.dma_start(out=b_sb[:], in_=bc[0:16, tsl_]),
                q.dma_start(out=c_sb[:], in_=bc[16:32, tsl_]),
            ])(tsl), deps=[h_vq, h_sn, h_y], ndma=2 * NCH + 2)

            def prep_emit(v):
                for i in range(NCH):
                    v.tensor_tensor(out=dtx_sb[i][:], in0=dt_sb[i][:], in1=xln_sb[i][:], op=ALU.mult)
                for i in range(NCH):
                    v.tensor_scalar_mul(y_sb[i][:], xln_sb[i][:], d_sb[:, i:i + 1])
            h_prep = prog.stage("v", prep_emit, deps=[h_ld, h_cst, h_y])

            h_pb = None
            h_vq = h_prep
            for n in range(16):
                def bcst_emit(p, n_=n):
                    sel = sel_sb[:, n_ * 128:(n_ + 1) * 128]
                    for u in range(NSUB):
                        cs = slice(u * 512, (u + 1) * 512)
                        p.matmul(ps_b[u][:], sel, b_sb[:, cs], start=True, stop=True)
                        p.matmul(ps_c[u][:], sel, c_sb[:, cs], start=True, stop=True)
                h_pb = prog.stage("p", bcst_emit, deps=[h_ld, h_vq, h_cst])

                def dA_emit(s, n_=n):
                    for i in range(NCH):
                        s.activation(out=dA_sb[i][:], in_=dt_sb[i][:], func=AF.Exp,
                                     scale=a_sb[:, i * 16 + n_:i * 16 + n_ + 1])
                h_sn = prog.stage("s", dA_emit, deps=[h_ld, h_a, h_vq])

                def scan_emit(v, n_=n, t_=t):
                    for i in range(NCH):
                        for u in range(NSUB):
                            cs = slice(u * 512, (u + 1) * 512)
                            v.tensor_tensor(out=dBx_sb[i][:, cs], in0=dtx_sb[i][:, cs],
                                            in1=ps_b[u][:], op=ALU.mult)
                    if t_ == NT // 2:
                        for i in range(NCH):
                            v.memset(carry[:, i * 16 + n_:i * 16 + n_ + 1], 0.0)
                    v.drain()
                    for i in range(NCH):
                        col = slice(i * 16 + n_, i * 16 + n_ + 1)
                        v.tensor_tensor_scan(out=h_sb[i][:], data0=dA_sb[i][:], data1=dBx_sb[i][:],
                                             initial=carry[:, col], op0=ALU.mult, op1=ALU.add)
                    v.drain()
                    for i in range(NCH):
                        col = slice(i * 16 + n_, i * 16 + n_ + 1)
                        v.tensor_copy(out=carry[:, col], in_=h_sb[i][:, TC - 1:TC])
                    for i in range(NCH):
                        for u in range(NSUB):
                            cs = slice(u * 512, (u + 1) * 512)
                            v.tensor_tensor(out=dBx_sb[i][:, cs], in0=h_sb[i][:, cs],
                                            in1=ps_c[u][:], op=ALU.mult)
                    v.drain()
                    for i in range(NCH):
                        v.tensor_tensor(out=y_sb[i][:], in0=y_sb[i][:], in1=dBx_sb[i][:], op=ALU.add)
                h_vq = prog.stage("v", scan_emit, deps=[h_pb, h_sn])

            h_y = prog.stage("g", (lambda tsl_: lambda g: [
                g.dma_start(out=yc[128 * i:128 * (i + 1), tsl_], in_=y_sb[i][:])
                for i in range(NCH)])(tsl), deps=[h_vq], ndma=NCH)

        _run(nc, prog, ctx, [h_y])
    return nc



def build_l3():
    nc = bass.Bass("TRN2", target_bir_lowering=False)
    xs = nc.dram_tensor("xs", [D, TSL], F32, kind="ExternalInput").ap()
    ys = nc.dram_tensor("ys", [D, TSL], F32, kind="ExternalInput").ap()
    wbT = nc.dram_tensor("wbT", [D, D], F32, kind="ExternalInput").ap()
    bb = nc.dram_tensor("bb", [128, 32], F32, kind="ExternalInput").ap()
    wrT = nc.dram_tensor("wrT", [D, 8], F32, kind="ExternalInput").ap()
    aaT = nc.dram_tensor("aaT", [D, 128], F32, kind="ExternalInput").ap()
    baT = nc.dram_tensor("baT", [128, D], F32, kind="ExternalInput").ap()
    ident = nc.dram_tensor("ident", [128, 128], F32, kind="ExternalInput").ap()
    e8m = nc.dram_tensor("e8m", [8, 128], F32, kind="ExternalInput").ap()
    outs = nc.dram_tensor("outs", [D, TSL], F32, kind="ExternalOutput").ap()

    TC = 256
    NT = TSL // TC
    NTI = TC // 128
    prog = Prog()
    with ExitStack() as ctx:
        sb = lambda n_, sh, dt=F32: ctx.enter_context(nc.sbuf_tensor(n_, sh, dt))
        x_sb = sb("x_sb", [128, 32 * TC])
        sq_sb = sb("sq_sb", [128, 32 * TC])
        y_sb = sb("y_sb", [128, 32 * TC])
        xln_sb = sb("xln_sb", [128, 32 * TC])
        wb_sb = sb("wb_sb", [128, 32 * 256])
        aa_sb = sb("aa_sb", [128, 32 * 128])
        wr_sb = sb("wr_sb", [128, 32 * 8])
        ba_sb = sb("ba_sb", [128, D])
        bb_sb = sb("bb_sb", [128, 32])
        id_sb = sb("id_sb", [128, 128])
        e8_sb = sb("e8_sb", [8, 128])
        lg_sb = sb("lg_sb", [8, TC])
        lgT_sb = sb("lgT_sb", [128, 8 * NTI])
        w2_sb = sb("w2_sb", [128, 8 * NTI])
        t2s = sb("t2s", [128, 16 * NTI])
        sc_sb = sb("sc_sb", [128, 4 * NTI])
        w8_sb = sb("w8_sb", [8, TC])
        g_sb = sb("g_sb", [128, TC])
        wbc_sb = sb("wbc_sb", [128, TC])
        ones_col = sb("ones_col", [128, 1])
        ones_row = sb("ones_row", [1, 128])
        st = sb("st", [1, 8 * TC])
        pb = [ctx.enter_context(nc.psum_tensor(f"pb{i}", [128, TC], F32)) for i in range(8)]
        ps_sx, ps_sq, ps_mb, ps_rb, ps_aux, ps_h, ps_o0, ps_o1 = pb

        h_ones = prog.stage("v", lambda v: (v.memset(ones_col[:], 1.0), v.memset(ones_row[:], 1.0)))
        h_w = prog.stage("q", lambda q: [
            q.dma_start(out=_s3(aa_sb[:], 128), in_=_r3(aaT, 128)),
            q.dma_start(out=_s3(wr_sb[:], 8), in_=_r3(wrT, 8)),
            q.dma_start(out=ba_sb[:], in_=baT),
            q.dma_start(out=bb_sb[:], in_=bb),
            q.dma_start(out=id_sb[:], in_=ident),
            q.dma_start(out=e8_sb[:], in_=e8m),
        ], ndma=6)

        h_out = None
        h_ffn_done = None
        h_wb_used = None
        for t in range(NT):
            tsl = slice(t * TC, (t + 1) * TC)
            h_ld = prog.stage("q", (lambda tsl_: lambda q: [
                q.dma_start(out=_s3(x_sb[:], TC), in_=_r3(xs[:, tsl_], TC)),
                q.dma_start(out=_s3(y_sb[:], TC), in_=_r3(ys[:, tsl_], TC)),
            ])(tsl), deps=[h_ffn_done, h_out], ndma=2)

            def res_emit(v):
                for k in range(32):
                    c = slice(k * TC, (k + 1) * TC)
                    v.tensor_tensor(out=x_sb[:, c], in0=x_sb[:, c], in1=y_sb[:, c], op=ALU.add)
            h_res = prog.stage("v", res_emit, deps=[h_ld])

            h_rt = ln_stats(prog, sq_sb, x_sb, ones_col, st, ps_sx, ps_sq, 32, TC,
                            deps=[h_res, h_wb_used], h_ones=h_ones)
            h_bc = bcast_mr(prog, ones_row, st, ps_mb, ps_rb, TC, deps=[h_rt], h_ones=h_ones)

            def xln_emit(v):
                for k in range(32):
                    c = slice(k * TC, (k + 1) * TC)
                    v.tensor_tensor(out=sq_sb[:, c], in0=x_sb[:, c], in1=ps_mb[:], op=ALU.subtract)
                v.drain()
                for k in range(32):
                    c = slice(k * TC, (k + 1) * TC)
                    v.tensor_tensor(out=xln_sb[:, c], in0=sq_sb[:, c], in1=ps_rb[:], op=ALU.mult)
            h_xln = prog.stage("v", xln_emit, deps=[h_bc])

            def lg_emit(p):
                for k in range(32):
                    c = slice(k * TC, (k + 1) * TC)
                    p.matmul(ps_aux[0:8, :], wr_sb[:, k * 8:(k + 1) * 8], xln_sb[:, c],
                             start=(k == 0), stop=(k == 31))
            h_lg = prog.stage("p", lg_emit, deps=[h_xln, h_w])
            h_lge = prog.stage("v", lambda v: v.tensor_copy(out=lg_sb[:], in_=ps_aux[0:8, :]),
                               deps=[h_lg])

            def tr_emit(p):
                for i in range(NTI):
                    dst = (ps_mb, ps_rb)[i]
                    p.transpose(dst[:, 0:8], lg_sb[:, i * 128:(i + 1) * 128], id_sb[0:8, 0:8])
            h_tr = prog.stage("p", tr_emit, deps=[h_lge, h_xln])

            def tre_emit(v):
                for i in range(NTI):
                    src = (ps_mb, ps_rb)[i]
                    v.tensor_copy(out=lgT_sb[:, i * 8:(i + 1) * 8], in_=src[:, 0:8])
            h_tre = prog.stage("v", tre_emit, deps=[h_tr])

            lgT3 = lgT_sb[:].rearrange("p (i e) -> p i e", e=8)
            w23 = w2_sb[:].rearrange("p (i e) -> p i e", e=8)
            eq23 = t2s[:, 0:8 * NTI].rearrange("p (i e) -> p i e", e=8)
            eq2b = t2s[:, 8 * NTI:16 * NTI].rearrange("p (i e) -> p i e", e=8)
            m1 = sc_sb[:, 0 * NTI:1 * NTI]
            m2 = sc_sb[:, 1 * NTI:2 * NTI]
            dd = sc_sb[:, 2 * NTI:3 * NTI]
            e1 = sc_sb[:, 3 * NTI:4 * NTI]

            def top2a_emit(v):
                v.reduce_max(m1, lgT3, axis=mybir.AxisListType.X)
                v.drain()
                a_, b_ = bass.broadcast_tensor_aps(lgT3, m1[:, :, None])
                v.tensor_tensor(out=w23, in0=a_, in1=b_, op=ALU.is_equal)
                v.drain()
                v.scalar_tensor_tensor(out=lgT3, in0=w23, scalar=-1e30, in1=lgT3,
                                       op0=ALU.mult, op1=ALU.add)
                v.drain()
                v.reduce_max(m2, lgT3, axis=mybir.AxisListType.X)
                v.drain()
                v.tensor_tensor(out=dd, in0=m1, in1=m2, op=ALU.subtract)
            h_t2a = prog.stage("v", top2a_emit, deps=[h_tre])

            h_sg = prog.stage("s", lambda s: s.activation(out=e1, in_=dd, func=AF.Sigmoid),
                              deps=[h_t2a])

            def top2b_emit(v):
                a_, b_ = bass.broadcast_tensor_aps(lgT3, m2[:, :, None])
                v.tensor_tensor(out=eq23, in0=a_, in1=b_, op=ALU.is_equal)
                a_, b_ = bass.broadcast_tensor_aps(w23, e1[:, :, None])
                v.tensor_tensor(out=w23, in0=a_, in1=b_, op=ALU.mult)
                v.drain()
                a_, b_ = bass.broadcast_tensor_aps(eq23, e1[:, :, None])
                v.tensor_tensor(out=eq2b, in0=a_, in1=b_, op=ALU.mult)
                v.drain()
                v.tensor_tensor(out=eq23, in0=eq23, in1=eq2b, op=ALU.subtract)
                v.drain()
                v.tensor_tensor(out=w2_sb[:], in0=w2_sb[:], in1=t2s[:, 0:8 * NTI], op=ALU.add)
            h_t2b = prog.stage("v", top2b_emit, deps=[h_sg])

            def trw_emit(p):
                for i in range(NTI):
                    dst = (ps_mb, ps_rb)[i]
                    p.transpose(dst[0:8, 0:128], w2_sb[:, i * 8:(i + 1) * 8], id_sb[:, :])
            h_trw = prog.stage("p", trw_emit, deps=[h_t2b, h_tre])

            def w8_emit(v):
                for i in range(NTI):
                    src = (ps_mb, ps_rb)[i]
                    v.tensor_copy(out=w8_sb[:, i * 128:(i + 1) * 128], in_=src[0:8, 0:128])
            h_w8 = prog.stage("v", w8_emit, deps=[h_trw])

            def hall_emit(p):
                for k in range(32):
                    c = slice(k * TC, (k + 1) * TC)
                    p.matmul(ps_h[:], aa_sb[:, k * 128:(k + 1) * 128], xln_sb[:, c],
                             start=(k == 0), stop=(k == 31))
                p.matmul(ps_aux[:], e8_sb[:], w8_sb[:], start=True, stop=True)
            h_hall = prog.stage("p", hall_emit, deps=[h_xln, h_w8, h_w, h_lge])

            def g_emit(v):
                v.tensor_copy(out=wbc_sb[:], in_=ps_aux[:])
                v.drain()
                v.scalar_tensor_tensor(out=g_sb[:], in0=ps_h[:], scalar=2.0, in1=wbc_sb[:],
                                       op0=ALU.mult, op1=ALU.mult)
            h_g = prog.stage("v", g_emit, deps=[h_hall])

            h_ev = None
            h_mm = None
            h_wb = None
            for ofb in range(16):
                prev_reader = h_mm
                h_wb = prog.stage("q", (lambda ofb_: lambda q: [
                    q.dma_start(out=_s3(wb_sb[:], 256), in_=_r3(wbT[:, ofb_ * 256:(ofb_ + 1) * 256], 256)),
                ])(ofb), deps=[h_xln, prev_reader], ndma=1)

                def mm_emit(p, ofb_=ofb):
                    for j in range(2):
                        pso = ps_o0 if j == 0 else ps_o1
                        ofc = ofb_ * 2 + j
                        for k in range(32):
                            c = slice(k * TC, (k + 1) * TC)
                            p.matmul(pso[:], wb_sb[:, (k * 2 + j) * 128:(k * 2 + j + 1) * 128],
                                     xln_sb[:, c], start=(k == 0), stop=False)
                        p.matmul(pso[:], ba_sb[:, ofc * 128:(ofc + 1) * 128], g_sb[:],
                                 start=False, stop=True)
                h_mm = prog.stage("p", mm_emit, deps=[h_wb, h_g, h_ev])

                def ev_emit(v, ofb_=ofb):
                    for j in range(2):
                        pso = ps_o0 if j == 0 else ps_o1
                        ofc = ofb_ * 2 + j
                        c = slice(ofc * TC, (ofc + 1) * TC)
                        v.scalar_tensor_tensor(out=y_sb[:, c], in0=pso[:],
                                               scalar=bb_sb[:, ofc:ofc + 1],
                                               in1=x_sb[:, c], op0=ALU.add, op1=ALU.add)
                h_ev = prog.stage("v", ev_emit, deps=[h_mm, h_out])
            h_ffn_done = h_ev
            h_wb_used = h_mm

            h_out = prog.stage("g", (lambda tsl_: lambda g: [
                g.dma_start(out=_r3(outs[:, tsl_], TC), in_=_s3(y_sb[:], TC)),
            ])(tsl), deps=[h_ffn_done], ndma=1)

        _run(nc, prog, ctx, [h_out])
    return nc



def build_fused():
    nc = bass.Bass("TRN2", target_bir_lowering=False)
    xs = nc.dram_tensor("xs", [D, TSL], F32, kind="ExternalInput").ap()
    xc = nc.dram_tensor("xc", [CSL, TTOK], F32, kind="ExternalInput").ap()
    xpw_my = nc.dram_tensor("xpw_my", [128, 4 * 160], F32, kind="ExternalInput").ap()
    dtw_my = nc.dram_tensor("dtw_my", [128, CSL], F32, kind="ExternalInput").ap()
    dtb_my = nc.dram_tensor("dtb_my", [128, 4], F32, kind="ExternalInput").ap()
    alog_my = nc.dram_tensor("alog_my", [128, 64], F32, kind="ExternalInput").ap()
    dvec_my = nc.dram_tensor("dvec_my", [128, 4], F32, kind="ExternalInput").ap()
    wbT = nc.dram_tensor("wbT", [D, D], F32, kind="ExternalInput").ap()
    bb = nc.dram_tensor("bb", [128, 32], F32, kind="ExternalInput").ap()
    wrT = nc.dram_tensor("wrT", [D, 8], F32, kind="ExternalInput").ap()
    aaT = nc.dram_tensor("aaT", [D, 128], F32, kind="ExternalInput").ap()
    baT = nc.dram_tensor("baT", [128, D], F32, kind="ExternalInput").ap()
    identb = nc.dram_tensor("identb", [128, 128], BF16, kind="ExternalInput").ap()
    ident8 = nc.dram_tensor("ident8", [8, 8], F32, kind="ExternalInput").ap()
    identf = nc.dram_tensor("identf", [128, 128], F32, kind="ExternalInput").ap()
    e8m = nc.dram_tensor("e8m", [8, 128], F32, kind="ExternalInput").ap()
    sel32 = nc.dram_tensor("sel32", [32, 32 * 128], BF16, kind="ExternalInput").ap()
    outs = nc.dram_tensor("outs", [D, TSL], F32, kind="ExternalOutput").ap()
    ar_in = nc.dram_tensor("ar_in", [163, TTOK], F32, kind="Internal").ap()
    ar_out = nc.dram_tensor("ar_out", [163, TTOK], F32, kind="Internal").ap()
    mrows = nc.dram_tensor("mrows", [3, TTOK], F32, kind="Internal").ap()
    ylocal = nc.dram_tensor("ylocal", [8 * CSL, TSL], F32, kind="Internal").ap()
    ygath = nc.dram_tensor("ygath", [8 * D, TSL], F32, kind="Internal").ap()
    wbh = nc.dram_tensor("wbh", [D, D], BF16, kind="Internal").ap()
    aah = nc.dram_tensor("aah", [D, 128], BF16, kind="Internal").ap()
    bah = nc.dram_tensor("bah", [128, D], BF16, kind="Internal").ap()
    wrh = nc.dram_tensor("wrh", [D, 8], BF16, kind="Internal").ap()

    RG = [[0, 1, 2, 3, 4, 5, 6, 7]]
    prog = Prog()
    ncc = {"c": 0}

    with ExitStack() as ctx:
        sb = lambda n_, sh, dt=F32: ctx.enter_context(nc.sbuf_tensor(n_, sh, dt))
        xpw_sb = sb("xpw_sb", [128, 4 * 160])
        dtw_sb = sb("dtw_sb", [128, CSL])
        dtwb_sb = sb("dtwb_sb", [128, CSL], BF16)
        dtb_sb = sb("dtb_sb", [128, 4])
        a_sb = sb("a_sb", [128, 64])
        d_sb = sb("d_sb", [128, 4])
        carry = sb("carry", [128, 64])
        wr_sb = sb("wr_sb", [128, 32 * 8], BF16)
        bb_sb = sb("bb_sb", [128, 32])
        idb_sb = sb("idb_sb", [128, 128], BF16)
        id8_sb = sb("id8_sb", [8, 8])
        idf_sb = sb("idf_sb", [128, 128])
        e8_sb = sb("e8_sb", [8, 128])
        sel_sb = sb("sel_sb", [32, 32 * 128], BF16)
        ones_col = sb("ones_col", [128, 1])
        ones_row = sb("ones_row", [1, 128])
        csd_sb = sb("csd_sb", [128, 1])
        csb_sb = sb("csb_sb", [32, 1])
        stt_a = sb("stt_a", [128, 128])
        stt_b = sb("stt_b", [128, 128])
        stt_c = sb("stt_c", [128, 128])
        stt_d = sb("stt_d", [128, 128])
        zrow = sb("zrow", [1, 2048])
        st = sb("st", [1, 8 * 256])
        cast_in = sb("cast_in", [128, 2048])
        cast_out = sb("cast_out", [128, 2048], BF16)
        ARN = 36864
        arena = sb("arena", [128, ARN])
        apos = {"o": 0}

        def aslc(cols, parts=128, dt=F32):
            w = cols if dt == F32 else (cols + 1) // 2
            o = apos["o"]
            apos["o"] = o + w
            assert apos["o"] <= ARN, (o, w)
            if dt == F32:
                return arena[0:parts, o:o + cols]
            return arena[0:parts, o:o + w].bitcast(dt)[0:parts, 0:cols]

        pbank = [ctx.enter_context(nc.psum_tensor(f"pb{i}", [128, 512], F32)) for i in range(8)]

        h_ones = prog.stage("v", lambda v: (v.memset(ones_col[:], 1.0),
                                            v.memset(ones_row[:], 1.0),
                                            v.memset(carry[:], 0.0),
                                            v.memset(zrow[:], 0.0)))
        cidbox = {}

        def w0_emit(q):
            cidbox["cid"] = q.partition_id()
            return [
                q.dma_start(out=xpw_sb[:], in_=xpw_my),
                q.dma_start(out=dtw_sb[:], in_=dtw_my),
                q.dma_start(out=dtb_sb[:], in_=dtb_my),
                q.dma_start(out=a_sb[:], in_=alog_my),
                q.dma_start(out=d_sb[:], in_=dvec_my),
                q.dma_start(out=bb_sb[:], in_=bb),
                q.dma_start(out=idb_sb[:], in_=identb),
                q.dma_start(out=id8_sb[:], in_=ident8),
                q.dma_start(out=idf_sb[:], in_=identf),
                q.dma_start(out=e8_sb[:], in_=e8m),
                q.dma_start(out=sel_sb[:], in_=sel32),
            ]
        h_w0 = prog.stage("q", w0_emit, ndma=11)
        h_dtwb = prog.stage("v", lambda v: v.tensor_copy(out=dtwb_sb[:], in_=dtw_sb[:]),
                            deps=[h_w0])
        h_a = prog.stage("s", lambda s: (s.activation(out=a_sb[:], in_=a_sb[:], func=AF.Exp),
                                         s.drain(),
                                         s.mul(out=a_sb[:], in_=a_sb[:], mul=-1.0)),
                         deps=[h_w0])

        def emit_cast(srcT, dstT, col0, ncols, dep, gdep):
            h1 = prog.stage("q", (lambda: lambda q: [
                q.dma_start(out=cast_in[:, 0:32 * ncols].rearrange("p (k c) -> p k c", c=ncols),
                            in_=_r3(srcT[:, col0:col0 + ncols], ncols)),
            ])(), deps=[dep], ndma=1)
            h2 = prog.stage("s", (lambda: lambda s: s.copy(
                out=cast_out[:, 0:32 * ncols], in_=cast_in[:, 0:32 * ncols]))(),
                deps=[h1, gdep])
            h3 = prog.stage("g", (lambda: lambda g: [
                g.dma_start(out=_r3(dstT[:, col0:col0 + ncols], ncols),
                            in_=cast_out[:, 0:32 * ncols].rearrange("p (k c) -> p k c", c=ncols)),
            ])(), deps=[h2], ndma=1)
            return h3

        h_cast = None
        for cb in range(64):
            h_cast = emit_cast(wbT, wbh, cb * 64, 64, h_cast, h_cast)
        for cb in range(2):
            h_cast = emit_cast(aaT, aah, cb * 64, 64, h_cast, h_cast)
        h_cast = emit_cast(wrT, wrh, 0, 8, h_cast, h_cast)
        for half in range(2):
            h1 = prog.stage("q", (lambda half_: lambda q: [
                q.dma_start(out=cast_in[:], in_=baT[:, half_ * 2048:(half_ + 1) * 2048])])(half),
                deps=[h_cast], ndma=1)
            h2 = prog.stage("s", lambda s: s.copy(out=cast_out[:], in_=cast_in[:]), deps=[h1])
            h_cast = prog.stage("g", (lambda half_: lambda g: [
                g.dma_start(out=bah[:, half_ * 2048:(half_ + 1) * 2048], in_=cast_out[:])])(half),
                deps=[h2], ndma=1)
        h_castall = h_cast

        TA = 2048
        NA = TTOK // TA
        xa = aslc(4 * TA)
        sqa = aslc(4 * TA)
        xpd_st = aslc(TA)
        xpbc_st = aslc(TA, 32)
        srow_x = aslc(TA, 1)
        srow_q = aslc(TA, 1)

        def cs_emit(p):
            for k in range(4):
                p.matmul(pbank[6][:, 0:1], xpw_sb[:, k * 160:k * 160 + 128], ones_col[:],
                         start=(k == 0), stop=(k == 3))
            for k in range(4):
                p.matmul(pbank[7][0:32, 0:1], xpw_sb[:, k * 160 + 128:(k + 1) * 160], ones_col[:],
                         start=(k == 0), stop=(k == 3))
        h_cs = prog.stage("p", cs_emit, deps=[h_w0, h_ones])
        h_cse = prog.stage("v", lambda v: (v.tensor_copy(out=csd_sb[:], in_=pbank[6][:, 0:1]),
                                           v.tensor_copy(out=csb_sb[:], in_=pbank[7][0:32, 0:1])),
                           deps=[h_cs])
        h_z = prog.stage("g", lambda g: [
            g.dma_start(out=ar_in[162:163, k * 2048:(k + 1) * 2048], in_=zrow[:])
            for k in range(8)], deps=[h_ones], ndma=8)
        h_csw = prog.stage("g", lambda g: [
            g.dma_start(out=ar_in[162:163, 0:128].rearrange("one (p c) -> (one p) c", c=1), in_=csd_sb[:]),
            g.dma_start(out=ar_in[162:163, 128:160].rearrange("one (p c) -> (one p) c", c=1), in_=csb_sb[0:32, :]),
        ], deps=[h_cse, h_z], ndma=2)

        h_xa_prev = None
        h_arw = h_csw
        for t in range(NA):
            tsl = slice(t * TA, (t + 1) * TA)
            h_xa = prog.stage("q", (lambda tsl_: lambda q: [
                q.dma_start(out=xa.rearrange("p (k c) -> p k c", c=TA),
                            in_=xc[:, tsl_].rearrange("(k p) c -> p k c", p=128)),
            ])(tsl), deps=[h_xa_prev], ndma=1)
            h_sq = prog.stage("s", (lambda: lambda s: s.activation(out=sqa, in_=xa, func=AF.Square))(),
                              deps=[h_xa])
            h_pa = None
            h_ve = None
            h_ve2 = None
            for u in range(4):
                def pa_u(p, u_=u):
                    b0, b1, b2, b3 = (0, 1, 2, 3) if u_ % 2 == 0 else (4, 5, 6, 7)
                    for k in range(4):
                        kc = slice(k * TA + u_ * 512, k * TA + (u_ + 1) * 512)
                        p.matmul(pbank[b0][0:1, :], ones_col[:], xa[:, kc],
                                 start=(k == 0), stop=(k == 3))
                    for k in range(4):
                        kc = slice(k * TA + u_ * 512, k * TA + (u_ + 1) * 512)
                        p.matmul(pbank[b1][0:1, :], ones_col[:], sqa[:, kc],
                                 start=(k == 0), stop=(k == 3))
                    for k in range(4):
                        kc = slice(k * TA + u_ * 512, k * TA + (u_ + 1) * 512)
                        p.matmul(pbank[b2][:, :], xpw_sb[:, k * 160:k * 160 + 128], xa[:, kc],
                                 start=(k == 0), stop=(k == 3))
                    for k in range(4):
                        kc = slice(k * TA + u_ * 512, k * TA + (u_ + 1) * 512)
                        p.matmul(pbank[b3][0:32, :], xpw_sb[:, k * 160 + 128:(k + 1) * 160], xa[:, kc],
                                 start=(k == 0), stop=(k == 3))
                prev2 = h_ve2
                h_pa = prog.stage("p", pa_u, deps=[h_sq, h_xa, prev2, h_cse])

                def ve_u(v, u_=u):
                    b0, b1, b2, b3 = (0, 1, 2, 3) if u_ % 2 == 0 else (4, 5, 6, 7)
                    cs = slice(u_ * 512, (u_ + 1) * 512)
                    v.tensor_copy(out=srow_x[:, cs], in_=pbank[b0][0:1, :])
                    v.tensor_copy(out=srow_q[:, cs], in_=pbank[b1][0:1, :])
                    v.tensor_copy(out=xpd_st[:, cs], in_=pbank[b2][:, :])
                    v.tensor_copy(out=xpbc_st[:, cs], in_=pbank[b3][0:32, :])
                h_ve2 = h_ve
                h_ve = prog.stage("v", (ve_u), deps=[h_pa, h_arw])
            h_arw = prog.stage("g", (lambda tsl_: lambda g: [
                g.dma_start(out=ar_in[0:128, tsl_], in_=xpd_st),
                g.dma_start(out=ar_in[128:160, tsl_], in_=xpbc_st),
                g.dma_start(out=ar_in[160:161, tsl_], in_=srow_x),
                g.dma_start(out=ar_in[161:162, tsl_], in_=srow_q),
            ])(tsl), deps=[h_ve], ndma=4)
            h_xa_prev = h_ve

        csem = ctx.enter_context(nc.semaphore("sem_c"))

        def ar_emit(g):
            g.collective_compute("AllReduce", ALU.add, replica_groups=RG,
                                 ins=[ar_in], outs=[ar_out]).then_inc(csem, 1)
        h_arq = prog.stage("g", ar_emit, deps=[h_arw, h_csw], raw=True)
        ncc["c"] += 1
        AR_DONE = ("c", ncc["c"])

        h_strd = prog.stage("q", lambda q: [
            q.dma_start(out=stt_a[:], in_=ar_out[160:161, :].rearrange("one (p c) -> (one p) c", c=128)),
            q.dma_start(out=stt_b[:], in_=ar_out[161:162, :].rearrange("one (p c) -> (one p) c", c=128)),
            q.dma_start(out=csd_sb[:], in_=ar_out[162:163, 0:128].rearrange("one (p c) -> (one p) c", c=1)),
            q.dma_start(out=csb_sb[0:32, :], in_=ar_out[162:163, 128:160].rearrange("one (p c) -> (one p) c", c=1)),
        ], deps=[AR_DONE, h_cse], ndma=4)

        def stats_emit(v):
            v.tensor_scalar_mul(stt_a[:], stt_a[:], 1.0 / D)
            v.tensor_scalar(stt_b[:], stt_b[:], 1.0 / D, EPS, op0=ALU.mult, op1=ALU.add)
            v.drain()
            v.tensor_tensor(out=stt_c[:], in0=stt_a[:], in1=stt_a[:], op=ALU.mult)
            v.drain()
            v.tensor_tensor(out=stt_c[:], in0=stt_b[:], in1=stt_c[:], op=ALU.subtract)
            v.drain()
            v.reciprocal(out=stt_d[:], in_=stt_c[:])
        h_st1 = prog.stage("v", stats_emit, deps=[h_strd])
        h_st2 = prog.stage("s", lambda s: s.activation(out=stt_b[:], in_=stt_d[:], func=AF.Sqrt),
                           deps=[h_st1])
        h_st3 = prog.stage("v", lambda v: v.tensor_tensor(out=stt_c[:], in0=stt_a[:], in1=stt_b[:],
                                                          op=ALU.mult), deps=[h_st2])
        h_mw = prog.stage("g", lambda g: [
            g.dma_start(out=mrows[0:1, :].rearrange("one (p c) -> (one p) c", c=128), in_=stt_a[:]),
            g.dma_start(out=mrows[1:2, :].rearrange("one (p c) -> (one p) c", c=128), in_=stt_b[:]),
            g.dma_start(out=mrows[2:3, :].rearrange("one (p c) -> (one p) c", c=128), in_=stt_c[:]),
        ], deps=[h_st3], ndma=3)

        apos["o"] = 0
        TB = 512
        NB = TTOK // TB
        NCH = 4
        xbp = [[aslc(TB) for _ in range(NCH)] for _ in range(2)]
        dtp = [[aslc(TB) for _ in range(NCH)] for _ in range(2)]
        dtxp = [[aslc(TB, 128, BF16) for _ in range(NCH)] for _ in range(2)]
        xlnb = [aslc(TB) for _ in range(NCH)]
        seedb = [aslc(TB) for _ in range(NCH)]
        dAn = [[aslc(TB) for _ in range(NCH)] for _ in range(2)]
        dBxb = [aslc(TB, 128, BF16) for _ in range(NCH)]
        hbf = [aslc(TB, 128, BF16) for _ in range(NCH)]
        bcb_sb = [[aslc(TB, 128, BF16) for _ in range(2)] for _ in range(2)]
        qbf = [[aslc(TB, 128, BF16) for _ in range(NCH)] for _ in range(2)]
        xpdtp = [aslc(TB) for _ in range(2)]
        xpdcp = [aslc(TB, 128, BF16) for _ in range(2)]
        xpbtp = [aslc(TB, 32) for _ in range(2)]
        xpbcp = [aslc(TB, 32, BF16) for _ in range(2)]
        t1s = aslc(TB)
        t2s_ = aslc(TB)
        mrowp = [aslc(TB, 1) for _ in range(2)]
        rrowp = [aslc(TB, 1) for _ in range(2)]
        mrrowp = [aslc(TB, 1) for _ in range(2)]

        h_scan_hist = []
        h_yacc_hist = []
        h_scan_done = {}
        h_prep_done = {}
        h_dt_done = {}
        h_gy = {-1: h_mw}
        h_yev = {-1: None}
        h_corr = None
        h_sp = None
        ag_handles = []

        def sh(idx):
            return h_scan_hist[idx] if 0 <= idx < len(h_scan_hist) else None

        def yh(idx):
            return h_yacc_hist[idx] if 0 <= idx < len(h_yacc_hist) else None

        for t in range(NB):
            par = t % 2
            tsl = slice(t * TB, (t + 1) * TB)
            h_ld = prog.stage("q", (lambda tsl_, par_: lambda q: [
                q.dma_start(out=xbp[par_][i], in_=xc[128 * i:128 * (i + 1), tsl_])
                for i in range(NCH)] + [
                q.dma_start(out=xpdtp[par_], in_=ar_out[0:128, tsl_]),
                q.dma_start(out=xpbtp[par_], in_=ar_out[128:160, tsl_]),
                q.dma_start(out=mrowp[par_][:], in_=mrows[0:1, tsl_]),
                q.dma_start(out=rrowp[par_][:], in_=mrows[1:2, tsl_]),
                q.dma_start(out=mrrowp[par_][:], in_=mrows[2:3, tsl_]),
            ])(tsl, par), deps=[h_prep_done.get(t - 2), h_scan_done.get(t - 2),
                                h_dt_done.get(t - 2), AR_DONE, h_mw], ndma=NCH + 5)

            def bcrm_emit(p, par_=par):
                p.matmul(pbank[6][:, :], ones_row[:], rrowp[par_][:], start=True, stop=True)
                p.matmul(pbank[7][:, :], ones_row[:], mrrowp[par_][:], start=True, stop=True)
            h_bcrm = prog.stage("p", bcrm_emit, deps=[h_ld, h_scan_done.get(t - 1), h_ones])

            def corr_emit(v, par_=par):
                v.tensor_tensor(out=t1s, in0=xpdtp[par_], in1=pbank[6][:, :], op=ALU.mult)
                v.tensor_scalar_mul(t2s_, pbank[7][:, :], csd_sb[:, 0:1])
                v.drain()
                v.tensor_tensor(out=xpdcp[par_], in0=t1s, in1=t2s_, op=ALU.subtract)
                v.drain()
                v.tensor_tensor(out=t1s[0:32, :], in0=xpbtp[par_], in1=pbank[6][0:32, :], op=ALU.mult)
                v.tensor_scalar_mul(t2s_[0:32, :], pbank[7][0:32, :], csb_sb[:, 0:1])
                v.drain()
                v.tensor_tensor(out=xpbcp[par_], in0=t1s[0:32, :], in1=t2s_[0:32, :], op=ALU.subtract)
            h_corr = prog.stage("v", corr_emit, deps=[h_bcrm, h_cse])

            h_pdp = None
            h_sp = None
            for i in range(NCH):
                def pd3(p, i_=i, par_=par):
                    bk = pbank[6] if i_ % 2 == 0 else pbank[7]
                    p.matmul(bk[:, :], dtwb_sb[:, i_ * 128:(i_ + 1) * 128], xpdcp[par_],
                             start=True, stop=True)
                h_pdp2 = prog.stage("p", pd3, deps=[h_corr, h_sp, h_dtwb])

                def sp3(s, i_=i, par_=par):
                    bk = pbank[6] if i_ % 2 == 0 else pbank[7]
                    s.activation(out=dtp[par_][i_], in_=bk[:, :], func=AF.Exp,
                                 bias=dtb_sb[:, i_:i_ + 1])
                    s.drain()
                    s.activation(out=dtp[par_][i_], in_=dtp[par_][i_], func=AF.Ln, bias=1.0)
                h_sp = prog.stage("s", sp3, deps=[h_pdp, h_pdp2])
                h_pdp = h_pdp2
            h_dt_done[t] = h_sp

            def bcmr_emit(p, par_=par):
                p.matmul(pbank[4][:, :], ones_row[:], mrowp[par_][:], start=True, stop=True)
                p.matmul(pbank[5][:, :], ones_row[:], rrowp[par_][:], start=True, stop=True)
            h_bcmr = prog.stage("p", bcmr_emit, deps=[h_ld, h_scan_done.get(t - 1), h_ones])

            def xln_emit(v, par_=par):
                for i in range(NCH):
                    v.tensor_tensor(out=dAn[0][i], in0=xbp[par_][i], in1=pbank[4][:, :], op=ALU.subtract)
                v.drain()
                for i in range(NCH):
                    v.tensor_tensor(out=xlnb[i], in0=dAn[0][i], in1=pbank[5][:, :], op=ALU.mult)
            h_xln = prog.stage("v", xln_emit, deps=[h_bcmr, h_scan_done.get(t - 1),
                                                    h_yev.get(t - 1)])

            def prep_emit(v, par_=par):
                for i in range(NCH):
                    v.tensor_tensor(out=dtxp[par_][i], in0=dtp[par_][i], in1=xlnb[i], op=ALU.mult)
            h_prep = prog.stage("v", prep_emit, deps=[h_xln, h_sp])
            h_prep_done[t] = h_prep

            def seed_emit(s, par_=par):
                for i in range(NCH):
                    s.activation(out=seedb[i], in_=xlnb[i], func=AF.Copy, scale=d_sb[:, i:i + 1])
            h_seed = prog.stage("s", seed_emit, deps=[h_xln, h_sp, h_gy.get(t - 1), h_w0])

            h_yacc = None
            base = len(h_scan_hist)
            for n in range(16):
                np2 = n % 2
                bB, bC = (4, 5) if np2 == 0 else (6, 7)

                def bcb_emit(p, n_=n, bB_=bB, bC_=bC, par_=par):
                    p.matmul(pbank[bB_][:, :], sel_sb[:, n_ * 128:(n_ + 1) * 128],
                             xpbcp[par_], start=True, stop=True)
                    p.matmul(pbank[bC_][:, :], sel_sb[:, (16 + n_) * 128:(17 + n_) * 128],
                             xpbcp[par_], start=True, stop=True)
                bdeps = [h_corr, sh(base + n - 2), h_w0]
                if np2 == 0:
                    bdeps.append(h_xln if n == 0 else None)
                else:
                    bdeps.append(h_dt_done[t])
                h_pbB = prog.stage("p", bcb_emit, deps=bdeps)

                def sev_emit(s, np2_=np2, bB_=bB, bC_=bC):
                    s.activation(out=bcb_sb[np2_][0], in_=pbank[bB_][:, :], func=AF.Copy)
                    s.activation(out=bcb_sb[np2_][1], in_=pbank[bC_][:, :], func=AF.Copy)
                h_sev = prog.stage("s", sev_emit, deps=[h_pbB, sh(base + n - 2)])

                def dA_emit(s, n_=n, par_=par, np2_=np2):
                    for i in range(NCH):
                        s.activation(out=dAn[np2_][i], in_=dtp[par_][i], func=AF.Exp,
                                     scale=a_sb[:, i * 16 + n_:i * 16 + n_ + 1])
                h_sn = prog.stage("s", dA_emit, deps=[h_sp, h_a, sh(base + n - 2), h_seed])

                def scan_emit(v, n_=n, t_=t, par_=par, np2_=np2, bB_=bB, bC_=bC):
                    for i in range(NCH):
                        v.tensor_tensor(out=dBxb[i], in0=dtxp[par_][i], in1=bcb_sb[np2_][0],
                                        op=ALU.mult)
                    if t_ == NB // 2:
                        for i in range(NCH):
                            v.memset(carry[:, i * 16 + n_:i * 16 + n_ + 1], 0.0)
                    v.drain()
                    for i in range(NCH):
                        col = slice(i * 16 + n_, i * 16 + n_ + 1)
                        v.tensor_tensor_scan(out=hbf[i], data0=dAn[np2_][i], data1=dBxb[i],
                                             initial=carry[:, col], op0=ALU.mult, op1=ALU.add)
                    v.drain()
                    for i in range(NCH):
                        col = slice(i * 16 + n_, i * 16 + n_ + 1)
                        v.tensor_copy(out=carry[:, col], in_=hbf[i][:, TB - 1:TB])
                    for i in range(NCH):
                        v.tensor_tensor(out=qbf[np2_][i], in0=hbf[i], in1=bcb_sb[np2_][1],
                                        op=ALU.mult)
                sdeps = [h_pbB, h_sev, h_sn, h_prep, yh(base + n - 2)]
                if n == 0:
                    sdeps.append(h_seed)
                h_vq = prog.stage("v", scan_emit, deps=sdeps)
                h_scan_hist.append(h_vq)

                def yacc_emit(p, n_=n - 1, np2_=(n - 1) % 2):
                    for i in range(NCH):
                        p.matmul(pbank[i][:, :], idb_sb[:], qbf[np2_][i],
                                 start=(n_ == 0), stop=(n_ == 15))
                if n >= 1:
                    h_yacc = prog.stage("p", yacc_emit,
                                        deps=[sh(base + n - 1), h_yev.get(t - 1), h_w0])
                    h_yacc_hist.append(h_yacc)

            def yacc15_emit(p):
                for i in range(NCH):
                    p.matmul(pbank[i][:, :], idb_sb[:], qbf[15 % 2][i],
                             start=False, stop=True)
            h_yacc = prog.stage("p", yacc15_emit,
                                deps=[sh(base + 15), h_yev.get(t - 1), h_w0])
            h_yacc_hist.append(h_yacc)
            h_scan_done[t] = h_vq

            def yev_emit(v):
                for i in range(NCH):
                    v.tensor_tensor(out=seedb[i], in0=pbank[i][:, :], in1=seedb[i], op=ALU.add)
            h_yev[t] = prog.stage("v", yev_emit, deps=[h_yacc, h_seed])

            j = t // 4
            co = (t % 4) * TB
            h_gy[t] = prog.stage("g", (lambda j_, co_: lambda g: [
                g.dma_start(out=ylocal[CSL * j_ + 128 * i:CSL * j_ + 128 * (i + 1), co_:co_ + TB],
                            in_=seedb[i]) for i in range(NCH)])(j, co),
                deps=[h_yev[t]], ndma=NCH)

            if t % 4 == 3:
                agdep = ag_handles[-1] if ag_handles else None

                def ag_emit(g, j_=j):
                    g.collective_compute("AllGather", ALU.bypass, replica_groups=RG,
                                         ins=[ylocal[CSL * j_:CSL * (j_ + 1), :]],
                                         outs=[ygath[D * j_:D * (j_ + 1), :]]).then_inc(csem, 1)
                prog.stage("g", ag_emit, deps=[h_gy[t], agdep], raw=True)
                ncc["c"] += 1
                ag_handles.append(("c", ncc["c"]))

        AG_ALL = ag_handles[-1]
        B_TAIL = [h_scan_done[NB - 1], h_prep_done[NB - 1], h_dt_done[NB - 1],
                  h_yev[NB - 1], h_gy[NB - 1]]

        apos["o"] = 0
        TC3 = 512
        NC3 = TSL // TC3
        NTI = TC3 // 128
        x3 = aslc(32 * TC3)
        xln3 = aslc(32 * TC3, 128, BF16)
        sqpp = [aslc(1024) for _ in range(2)]
        lg3 = sqpp[1][0:8, 0:512]
        wbc3 = sqpp[0][:, 0:512]
        wb3p = [aslc(32 * 128, 128, BF16) for _ in range(2)]
        aa3 = aslc(32 * 128, 128, BF16)
        ba3 = aslc(D, 128, BF16)
        evb = [aslc(TC3) for _ in range(2)]
        lgT3t = aslc(8 * NTI)
        w23t = aslc(8 * NTI)
        t2s3 = aslc(16 * NTI)
        sc3 = aslc(4 * NTI)
        w83 = aslc(TC3, 8)
        g3 = aslc(TC3, 128, BF16)

        ps_sx, ps_sq, ps_mb, ps_rb, ps_aux, ps_h, ps_o0, ps_o1 = pbank

        def w2_emit(q):
            return [
                q.dma_start(out=aa3.rearrange("p (k m) -> p k m", m=128), in_=_r3(aah, 128)),
                q.dma_start(out=ba3, in_=bah),
                q.dma_start(out=wr_sb[:].rearrange("p (k m) -> p k m", m=8), in_=_r3(wrh, 8)),
            ]
        h_w2 = prog.stage("q", w2_emit, deps=[AG_ALL, h_castall] + B_TAIL, ndma=3)

        ygv = ygath.rearrange("(b p) c -> p b c", p=128)
        h_out = {-1: None}
        h_ffn_done = None
        h_acc = None
        h_wb_used = [None, None]
        h_ev_hist = {}
        h_go_hist = {}
        for t in range(NC3):
            tsl = slice(t * TC3, (t + 1) * TC3)

            def ld3(q, tsl_=tsl):
                return [q.dma_start(out=_s3(x3, TC3), in_=_r3(xs[:, tsl_], TC3))]
            h_ld = prog.stage("q", ld3, deps=[h_ffn_done, h_out[t - 1], h_w2], ndma=1)

            def acc3(g, tsl_=tsl):
                if "cid_g" not in cidbox:
                    cidbox["cid_g"] = g.partition_id()
                cid = cidbox["cid_g"]
                return [g.dma_start(out=y3v, in_=ygv[:, bass.ts(cid, 32), tsl_],
                                    accum_op=ALU.add)
                        for y3v in [x3.rearrange("p (k c) -> p k c", c=TC3)]]
            h_acc = prog.stage("g", acc3, deps=[h_ld, AG_ALL, h_out[t - 1]], ndma=1)

            h_sqs = None
            h_sums = None
            h_sums2 = None
            for jp in range(16):
                def sq_emit(s, jp_=jp):
                    c = slice(jp_ * 2 * TC3, (jp_ * 2 + 2) * TC3)
                    s.activation(out=sqpp[jp_ % 2][:, 0:1024], in_=x3[:, c], func=AF.Square)
                h_sqs = prog.stage("s", sq_emit, deps=[h_acc, h_sums2, h_out[t - 1]])

                def sum_emit(p, jp_=jp):
                    for kk in range(2):
                        k = jp_ * 2 + kk
                        c = slice(k * TC3, (k + 1) * TC3)
                        p.matmul(ps_sx[0:1, :], ones_col[:], x3[:, c],
                                 start=(k == 0), stop=(k == 31))
                        p.matmul(ps_sq[0:1, :], ones_col[:],
                                 sqpp[jp_ % 2][:, kk * TC3:(kk + 1) * TC3],
                                 start=(k == 0), stop=(k == 31))
                h_sums2 = h_sums
                h_sums = prog.stage("p", sum_emit,
                                    deps=[h_sqs, h_ones, h_sums2] + h_wb_used)
            h_sum3 = h_sums

            def st31(v):
                v.tensor_scalar_mul(st[:, 0:512], ps_sx[0:1, :], 1.0 / D)
                v.tensor_scalar(st[:, 512:1024], ps_sq[0:1, :], 1.0 / D, EPS,
                                op0=ALU.mult, op1=ALU.add)
                v.drain()
                v.tensor_tensor(out=st[:, 1024:1536], in0=st[:, 0:512],
                                in1=st[:, 0:512], op=ALU.mult)
                v.drain()
                v.tensor_tensor(out=st[:, 1024:1536], in0=st[:, 512:1024],
                                in1=st[:, 1024:1536], op=ALU.subtract)
                v.drain()
                v.reciprocal(out=st[:, 512:1024], in_=st[:, 1024:1536])
            h_st31 = prog.stage("v", st31, deps=[h_sum3])
            h_st32 = prog.stage("s", lambda s: s.activation(out=st[:, 1536:2048],
                                                            in_=st[:, 512:1024], func=AF.Sqrt),
                                deps=[h_st31])

            def bc3(p):
                p.matmul(ps_mb[:, :], ones_row[:], st[0:1, 0:512], start=True, stop=True)
                p.matmul(ps_rb[:, :], ones_row[:], st[0:1, 1536:2048], start=True, stop=True)
            h_bc3 = prog.stage("p", bc3, deps=[h_st32, h_ones])

            def xln3_emit(v):
                for k in range(32):
                    c = slice(k * TC3, (k + 1) * TC3)
                    v.tensor_tensor(out=xln3[:, c], in0=x3[:, c], in1=ps_mb[:, :], op=ALU.subtract)
                v.drain()
                for k in range(32):
                    c = slice(k * TC3, (k + 1) * TC3)
                    v.tensor_tensor(out=xln3[:, c], in0=xln3[:, c], in1=ps_rb[:, :], op=ALU.mult)
            h_xln3 = prog.stage("v", xln3_emit, deps=[h_bc3])

            def lg_emit(p):
                for k in range(32):
                    c = slice(k * TC3, (k + 1) * TC3)
                    p.matmul(ps_aux[0:8, :], wr_sb[:, k * 8:(k + 1) * 8], xln3[:, c],
                             start=(k == 0), stop=(k == 31))
            h_lg = prog.stage("p", lg_emit, deps=[h_xln3, h_w2])
            h_lge = prog.stage("v", lambda v: v.tensor_copy(out=lg3, in_=ps_aux[0:8, :]),
                               deps=[h_lg])

            trb = (ps_mb, ps_rb, ps_o0, ps_o1)

            def tr_emit(p):
                for i in range(NTI):
                    p.transpose(trb[i][:, 0:8], lg3[:, i * 128:(i + 1) * 128], id8_sb[:])
            h_tr = prog.stage("p", tr_emit, deps=[h_lge, h_xln3] + h_wb_used)

            def tre_emit(v):
                for i in range(NTI):
                    v.tensor_copy(out=lgT3t[:, i * 8:(i + 1) * 8], in_=trb[i][:, 0:8])
            h_tre = prog.stage("v", tre_emit, deps=[h_tr])

            lgT3v = lgT3t.rearrange("p (i e) -> p i e", e=8)
            w23v = w23t.rearrange("p (i e) -> p i e", e=8)
            eq23 = t2s3[:, 0:8 * NTI].rearrange("p (i e) -> p i e", e=8)
            eq2b = t2s3[:, 8 * NTI:16 * NTI].rearrange("p (i e) -> p i e", e=8)
            m1 = sc3[:, 0 * NTI:1 * NTI]
            m2 = sc3[:, 1 * NTI:2 * NTI]
            dd = sc3[:, 2 * NTI:3 * NTI]
            e1 = sc3[:, 3 * NTI:4 * NTI]

            def top2a_emit(v):
                v.reduce_max(m1, lgT3v, axis=mybir.AxisListType.X)
                v.drain()
                a_, b_ = bass.broadcast_tensor_aps(lgT3v, m1[:, :, None])
                v.tensor_tensor(out=w23v, in0=a_, in1=b_, op=ALU.is_equal)
                v.drain()
                v.scalar_tensor_tensor(out=lgT3v, in0=w23v, scalar=-1e30, in1=lgT3v,
                                       op0=ALU.mult, op1=ALU.add)
                v.drain()
                v.reduce_max(m2, lgT3v, axis=mybir.AxisListType.X)
                v.drain()
                v.tensor_tensor(out=dd, in0=m1, in1=m2, op=ALU.subtract)
            h_t2a = prog.stage("v", top2a_emit, deps=[h_tre])

            h_sg = prog.stage("s", lambda s: s.activation(out=e1, in_=dd, func=AF.Sigmoid),
                              deps=[h_t2a])

            def top2b_emit(v):
                a_, b_ = bass.broadcast_tensor_aps(lgT3v, m2[:, :, None])
                v.tensor_tensor(out=eq23, in0=a_, in1=b_, op=ALU.is_equal)
                a_, b_ = bass.broadcast_tensor_aps(w23v, e1[:, :, None])
                v.tensor_tensor(out=w23v, in0=a_, in1=b_, op=ALU.mult)
                v.drain()
                a_, b_ = bass.broadcast_tensor_aps(eq23, e1[:, :, None])
                v.tensor_tensor(out=eq2b, in0=a_, in1=b_, op=ALU.mult)
                v.drain()
                v.tensor_tensor(out=eq23, in0=eq23, in1=eq2b, op=ALU.subtract)
                v.drain()
                v.tensor_tensor(out=w23t, in0=w23t, in1=t2s3[:, 0:8 * NTI], op=ALU.add)
            h_t2b = prog.stage("v", top2b_emit, deps=[h_sg])

            def trw_emit(p):
                for i in range(NTI):
                    p.transpose(trb[i][0:8, 0:128], w23t[:, i * 8:(i + 1) * 8], idf_sb[:])
            h_trw = prog.stage("p", trw_emit, deps=[h_t2b, h_tre])

            def w8_emit(v):
                for i in range(NTI):
                    v.tensor_copy(out=w83[:, i * 128:(i + 1) * 128], in_=trb[i][0:8, 0:128])
            h_w8 = prog.stage("v", w8_emit, deps=[h_trw])

            def hall_emit(p):
                for k in range(32):
                    c = slice(k * TC3, (k + 1) * TC3)
                    p.matmul(ps_h[:, :], aa3[:, k * 128:(k + 1) * 128], xln3[:, c],
                             start=(k == 0), stop=(k == 31))
                p.matmul(ps_aux[:, :], e8_sb[:], w83, start=True, stop=True)
            h_hall = prog.stage("p", hall_emit, deps=[h_xln3, h_w8, h_w2, h_lge])

            def g_emit(v):
                v.tensor_copy(out=wbc3, in_=ps_aux[:, :])
                v.drain()
                v.scalar_tensor_tensor(out=g3, in0=ps_h[:, :], scalar=2.0, in1=wbc3,
                                       op0=ALU.mult, op1=ALU.mult)
            h_g = prog.stage("v", g_emit, deps=[h_hall])

            h_ev = None
            h_mm = None
            for ofb in range(32):
                wpar = ofb % 2
                h_wb = prog.stage("q", (lambda ofb_, wpar_: lambda q: [
                    q.dma_start(out=wb3p[wpar_].rearrange("p (k c) -> p k c", c=128),
                                in_=_r3(wbh[:, ofb_ * 128:(ofb_ + 1) * 128], 128)),
                ])(ofb, wpar), deps=[h_xln3, h_wb_used[wpar], h_castall], ndma=1)

                def mm_emit(p, ofb_=ofb, wpar_=wpar):
                    pso = ps_o0 if ofb_ % 2 == 0 else ps_o1
                    for k in range(32):
                        c = slice(k * TC3, (k + 1) * TC3)
                        p.matmul(pso[:, :], wb3p[wpar_][:, k * 128:(k + 1) * 128],
                                 xln3[:, c], start=(k == 0), stop=False)
                    p.matmul(pso[:, :], ba3[:, ofb_ * 128:(ofb_ + 1) * 128], g3,
                             start=False, stop=True)
                h_mm = prog.stage("p", mm_emit, deps=[h_wb, h_g, h_ev_hist.get(ofb - 2),
                                                      h_trw])
                h_wb_used[wpar] = h_mm

                def ev_emit(v, ofb_=ofb):
                    pso = ps_o0 if ofb_ % 2 == 0 else ps_o1
                    c = slice(ofb_ * TC3, (ofb_ + 1) * TC3)
                    v.scalar_tensor_tensor(out=evb[ofb_ % 2], in0=pso[:, :],
                                           scalar=bb_sb[:, ofb_:ofb_ + 1],
                                           in1=x3[:, c], op0=ALU.add, op1=ALU.add)
                h_ev = prog.stage("v", ev_emit, deps=[h_mm, h_go_hist.get(ofb - 2)])
                h_ev_hist[ofb] = h_ev

                h_go = prog.stage("g", (lambda ofb_, tsl_: lambda g: [
                    g.dma_start(out=outs[ofb_ * 128:(ofb_ + 1) * 128, tsl_],
                                in_=evb[ofb_ % 2]),
                ])(ofb, tsl), deps=[h_ev], ndma=1)
                h_go_hist[ofb] = h_go
            h_ffn_done = h_ev
            h_out[t] = h_go
            h_ev_hist = {}
            h_go_hist = {}

        _run(nc, prog, ctx, [h_out[NC3 - 1]], extra_sems={"c": csem})
    return nc



_CACHE = {}


def _get(name, builder):
    if name not in _CACHE:
        _CACHE[name] = builder()
    return _CACHE[name]


def kernel(L0_attn_x, L0_attn_A_log, L0_attn_D, L0_attn_x_proj_w, L0_attn_dt_proj_w,
           L0_attn_dt_proj_b, L0_ffn_W_base, L0_ffn_b_base, L0_ffn_W_router,
           L0_ffn_A, L0_ffn_B):
    f32 = np.float32
    xT = np.ascontiguousarray(np.asarray(L0_attn_x, f32).reshape(TTOK, D).T)
    xpwT = np.ascontiguousarray(np.asarray(L0_attn_x_proj_w, f32).T)
    dtwT = np.ascontiguousarray(np.asarray(L0_attn_dt_proj_w, f32).T)
    dtb = np.ascontiguousarray(np.asarray(L0_attn_dt_proj_b, f32).reshape(32, 128).T)
    alog = np.ascontiguousarray(np.asarray(L0_attn_A_log, f32))
    dvec = np.asarray(L0_attn_D, f32)
    wbT = np.ascontiguousarray(np.asarray(L0_ffn_W_base, f32).T)
    bb = np.ascontiguousarray(np.asarray(L0_ffn_b_base, f32).reshape(32, 128).T)
    wrT = np.ascontiguousarray(np.asarray(L0_ffn_W_router, f32).T)
    aaT = np.ascontiguousarray(np.asarray(L0_ffn_A, f32).reshape(128, D).T)
    baT = np.ascontiguousarray(np.asarray(L0_ffn_B, f32).transpose(0, 2, 1).reshape(128, D))
    import ml_dtypes
    identb = np.eye(128, dtype=ml_dtypes.bfloat16)
    ident8 = np.eye(8, dtype=f32)
    identf = np.eye(128, dtype=f32)
    e8m = np.repeat(np.eye(8, dtype=f32), 16, axis=1)
    sel32 = np.zeros((32, 32 * 128), dtype=ml_dtypes.bfloat16)
    for n in range(32):
        sel32[n, n * 128:(n + 1) * 128] = 1.0

    cores = list(range(NCORE))
    nc = _get("fused", build_fused)
    in_maps = []
    for c in cores:
        ch = slice(c * CSL, (c + 1) * CSL)
        in_maps.append(dict(
            xs=np.ascontiguousarray(xT[:, c * TSL:(c + 1) * TSL]),
            xc=np.ascontiguousarray(xT[ch]),
            xpw_my=np.ascontiguousarray(
                xpwT[ch].reshape(4, 128, 160).transpose(1, 0, 2).reshape(128, 640)),
            dtw_my=np.ascontiguousarray(dtwT[:, ch]),
            dtb_my=np.ascontiguousarray(dtb[:, 4 * c:4 * (c + 1)]),
            alog_my=np.ascontiguousarray(
                alog[ch].reshape(4, 128, 16).transpose(1, 0, 2).reshape(128, 64)),
            dvec_my=np.ascontiguousarray(dvec[ch].reshape(4, 128).T),
            wbT=wbT, bb=bb, wrT=wrT, aaT=aaT, baT=baT,
            identb=identb, ident8=ident8, identf=identf, e8m=e8m, sel32=sel32,
        ))
    r = run_bass_kernel_spmd(nc, in_maps, core_ids=cores)
    out_full = np.concatenate([r.results[c]["outs"] for c in cores], axis=1)
    return np.ascontiguousarray(out_full.T).reshape(2, SEQ, D).astype(np.float32)
